# version 07
# baseline (speedup 1.0000x reference)
import contextlib
import numpy as np
import concourse.bass as bass
import concourse.mybir as mybir
from concourse.bass_utils import run_bass_kernel_spmd

F32 = mybir.dt.float32
BF16 = mybir.dt.bfloat16
AF = mybir.ActivationFunctionType
OP = mybir.AluOpType

NCORES = 8
D, Dc = 1024, 8
INNER, Ic = 2048, 16
HID, Hc = 4096, 32
NB, S = 4, 2048
NTOK = NB * S
TA, NKT = 512, 4
NTA = NB * NKT
TF, NTF = 1024, 8
EPS = 1e-5

RF = 15360
DBG = False


def build_nc():
    nc = bass.Bass("TRN2", target_bir_lowering=False)

    x_h = nc.dram_tensor("x", [NTOK, D], F32, kind="ExternalInput").ap()
    wn_h = nc.dram_tensor("w_norm", [8, 128], F32, kind="ExternalInput").ap()
    win_h = nc.dram_tensor("W_in", [2 * INNER, D], F32, kind="ExternalInput").ap()
    wdt_h = nc.dram_tensor("W_dt", [INNER, INNER], F32, kind="ExternalInput").ap()
    bdt_h = nc.dram_tensor("b_dt", [16, 128], F32, kind="ExternalInput").ap()
    wout_h = nc.dram_tensor("W_out", [D, INNER], F32, kind="ExternalInput").ap()
    gw_h = nc.dram_tensor("gate_w", [2, D], F32, kind="ExternalInput").ap()
    w1_h = nc.dram_tensor("w1", [2 * D, HID], F32, kind="ExternalInput").ap()
    b1_h = nc.dram_tensor("b1", [64, 128], F32, kind="ExternalInput").ap()
    w2_h = nc.dram_tensor("w2", [2 * HID, D], F32, kind="ExternalInput").ap()
    b2_h = nc.dram_tensor("b2", [16, 128], F32, kind="ExternalInput").ap()
    sw1_h = nc.dram_tensor("sw1", [D, HID], F32, kind="ExternalInput").ap()
    sb1_h = nc.dram_tensor("sb1", [32, 128], F32, kind="ExternalInput").ap()
    sw2_h = nc.dram_tensor("sw2", [HID, D], F32, kind="ExternalInput").ap()
    sb2_h = nc.dram_tensor("sb2", [8, 128], F32, kind="ExternalInput").ap()
    out_h = nc.dram_tensor("out", [NTOK, D], F32, kind="ExternalOutput").ap()

    nat_in = nc.dram_tensor("nat_in", [2 * INNER, D], BF16, kind="Internal").ap()
    nat_dt = nc.dram_tensor("nat_dt", [INNER, INNER], BF16, kind="Internal").ap()
    nat_out = nc.dram_tensor("nat_out", [D, INNER], BF16, kind="Internal").ap()
    blobA1 = nc.dram_tensor("blobA1", [8 * 128, 8 * 512], BF16, kind="Internal").ap()
    blobA2 = nc.dram_tensor("blobA2", [4 * 128, 16 * 512], BF16, kind="Internal").ap()
    blobA3 = nc.dram_tensor("blobA3", [2 * 128, 16 * 512], BF16, kind="Internal").ap()
    blobW1 = [nc.dram_tensor(f"blobW1_{e}", [8 * 128, 8 * 512], BF16, kind="Internal").ap() for e in range(2)]
    blobSW1 = nc.dram_tensor("blobSW1", [8 * 128, 8 * 512], BF16, kind="Internal").ap()
    blobW2 = [nc.dram_tensor(f"blobW2_{e}", [4 * 128, 16 * 512], BF16, kind="Internal").ap() for e in range(2)]
    blobSW2 = nc.dram_tensor("blobSW2", [4 * 128, 16 * 512], BF16, kind="Internal").ap()
    x1blob = nc.dram_tensor("x1blob", [Dc * 128, NTOK], F32,
                            kind="ExternalOutput" if DBG else "Internal").ap()
    if DBG:
        dbgA = nc.dram_tensor("dbgA", [128, 16384], BF16, kind="ExternalOutput").ap()
        dbgB = nc.dram_tensor("dbgB", [128, 16384], BF16, kind="ExternalOutput").ap()
        dbgC = nc.dram_tensor("dbgC", [128, 16384], BF16, kind="ExternalOutput").ap()
        dbgD = nc.dram_tensor("dbgD", [128, 8192], F32, kind="ExternalOutput").ap()
        dbgE = nc.dram_tensor("dbgE", [128, 8192], F32, kind="ExternalOutput").ap()
        dbgF = nc.dram_tensor("dbgF", [128, 8192], F32, kind="ExternalOutput").ap()
        dbgG = nc.dram_tensor("dbgG", [128, 8], F32, kind="ExternalOutput").ap()
        dbgH = nc.dram_tensor("dbgH", [128, 40], F32, kind="ExternalOutput").ap()

    with contextlib.ExitStack() as es:
        E = es.enter_context
        AR1 = E(nc.sbuf_tensor("AR1", [128, 8192], F32))
        AR2h = E(nc.sbuf_tensor("AR2", [128, 16384], BF16))
        AR3h = E(nc.sbuf_tensor("AR3", [128, 16384], BF16))
        AR4 = E(nc.sbuf_tensor("AR4", [128, 8192], F32))
        ring = E(nc.sbuf_tensor("ringt", [128, RF], BF16))
        pack = E(nc.sbuf_tensor("pack", [128, 9216], BF16))
        statq = E(nc.sbuf_tensor("statq", [128, 8], F32))
        ident = E(nc.sbuf_tensor("ident", [128, 128], F32))
        ones_row = E(nc.sbuf_tensor("ones_row", [1, 128], F32))
        ones_colf = E(nc.sbuf_tensor("ones_colf", [128, 1], F32))
        ones_colb = E(nc.sbuf_tensor("ones_colb", [128, 1], BF16))
        identb = E(nc.sbuf_tensor("identb", [128, 128], BF16))
        rows = E(nc.sbuf_tensor("rows", [1, 6144], F32))
        cst = E(nc.sbuf_tensor("cst", [64, 1024], F32))
        wnc = E(nc.sbuf_tensor("wnc", [128, 8], F32))
        bdtc = E(nc.sbuf_tensor("bdtc", [128, 16], F32))
        b1c = E(nc.sbuf_tensor("b1c", [128, 64], F32))
        sb1c = E(nc.sbuf_tensor("sb1c", [128, 32], F32))
        b2c = E(nc.sbuf_tensor("b2c", [128, 16], F32))
        sb2c = E(nc.sbuf_tensor("sb2c", [128, 8], F32))
        bfc = E(nc.sbuf_tensor("bfc", [128, 8], F32))
        gwd = E(nc.sbuf_tensor("gwd", [128, 8], F32))
        swd = E(nc.sbuf_tensor("swd", [1, 1], F32))
        stst = E(nc.sbuf_tensor("stst", [128, 16], F32))
        ceps = E(nc.sbuf_tensor("ceps", [128, 2], F32))
        bfc2 = E(nc.sbuf_tensor("bfc2", [128, 8], F32)) if DBG else None
        pb = [E(nc.psum_tensor(f"pb{i}", [128, 512], F32)) for i in range(8)]

        sem = {k: E(nc.semaphore(f"sem{k}")) for k in "TVSGWUP"}

        AR4b = AR4.bitcast(BF16)

        OPS = {k: [] for k in "tvsgy"}
        N = {k: 0 for k in "TVSGWUP"}
        WAITED = {e: {} for e in "tvsgy"}

        def emit(e, fn, inc=None):
            v = None
            if inc is not None:
                key, amt = inc
                N[key] += amt
                v = N[key]
                h = sem[key]
                def thunk(eng, fn=fn, h=h, amt=amt):
                    fn(eng).then_inc(h, amt)
            else:
                def thunk(eng, fn=fn):
                    fn(eng)
            OPS[e].append(thunk)
            return v

        def wait(e, key, val):
            if val is None or val <= 0:
                return
            if val <= WAITED[e].get(key, 0):
                return
            WAITED[e][key] = val
            h = sem[key]
            OPS[e].append(lambda eng, h=h, val=val: eng.wait_ge(h, val))

        def wait_tok(e, tok):
            if tok is not None:
                wait(e, tok[0], tok[1])

        bank_tok = [None] * 8
        bank_rr = [0]

        def take_banks(n):
            ids = []
            for _ in range(n):
                i = bank_rr[0] % 8
                bank_rr[0] += 1
                wait_tok('t', bank_tok[i])
                bank_tok[i] = None
                ids.append(i)
            return ids

        def release_bank(i, tok):
            bank_tok[i] = tok

        ring_state = {"head": 0, "spans": []}
        span_autoid = [0]
        span_free = {}

        def ring_load(src_ap, ncols, ready_tok=None):
            st = ring_state
            head = st["head"]
            if head % 4096:
                head += 4096 - head % 4096
            if head + ncols > RF:
                head = 0
            keep = []
            for (s0, e0, sid) in st["spans"]:
                if s0 < head + ncols and e0 > head:
                    tok = span_free.get(sid)
                    assert tok is not None, "ring overwrite before consumer registered"
                    wait_tok('y', tok)
                else:
                    keep.append((s0, e0, sid))
            st["spans"] = keep
            sid = span_autoid[0]
            span_autoid[0] += 1
            st["spans"].append((head, head + ncols, sid))
            st["head"] = head + ncols
            if ready_tok is not None:
                wait_tok('y', ready_tok)
            dst = ring[:, head:head + ncols]
            wv = emit('y', lambda eng, dst=dst, src=src_ap: eng.dma_start(dst, src), inc=('W', 16))
            return head, wv, sid

        def ring_free(sid, tok):
            span_free[sid] = tok

        def xr(pa, c, lo=0, n=512):
            return AR1[:, pa * 4096 + c * 512 + lo: pa * 4096 + c * 512 + lo + n]

        def x1f(c, lo=0, n=1024):
            return AR1[:, c * 1024 + lo: c * 1024 + lo + n]

        def out_tm(slot):
            return AR1[:, slot * 1024:(slot + 1) * 1024]

        def ut(j, lo=0, n=512):
            return AR2h[:, j * 512 + lo: j * 512 + lo + n]

        def zt(j, lo=0, n=512):
            return AR2h[:, 8192 + j * 512 + lo: 8192 + j * 512 + lo + n]

        def g2(c, lo=0, n=1024):
            return AR2h[:, c * 1024 + lo: c * 1024 + lo + n]

        def m0(c, lo=0, n=1024):
            return AR2h[:, 8192 + c * 1024 + lo: 8192 + c * 1024 + lo + n]

        def pt(j, lo=0, n=512):
            return AR3h[:, j * 512 + lo: j * 512 + lo + n]

        def gt(c, lo=0, n=512):
            return AR3h[:, 8192 + c * 512 + lo: 8192 + c * 512 + lo + n]

        AR3f = AR3h.bitcast(F32)

        def lam_g(q, n=512):
            return AR3f[:, 6144 + q * 512: 6144 + q * 512 + n]

        def d1_g(q, n=512):
            return AR3h[:, 8192 + q * 512: 8192 + q * 512 + n]

        def ht(j, lo=0, n=1024):
            return AR3h[:, j * 1024 + lo: j * 1024 + lo + n]

        def s_g(q, lo=0, n=512):
            return AR4[:, q * 512 + lo: q * 512 + lo + n]

        def x2scr(slot, n=512):
            return AR4[:, slot * 512: slot * 512 + n]

        def x1_st(slot):
            return AR4[:, slot * 512:(slot + 1) * 512]

        def bcB(i, n=512):
            return AR4[:, 2048 + i * 512: 2048 + i * 512 + n]

        def stage_x(sub, lo=0, n=1024):
            return AR4[:, 4096 + sub * 1024 + lo: 4096 + sub * 1024 + lo + n]

        def out_acc(c, lo=0, n=1024):
            return AR4[:, c * 1024 + lo: c * 1024 + lo + n]

        def rstd2B(lo=0, n=1024):
            return AR4[:, 1024 + lo: 1024 + lo + n]

        def m2r2B(lo=0, n=1024):
            return AR4[:, 2048 + lo: 2048 + lo + n]

        def mask0B(lo=0, n=1024):
            return AR4b[:, 6144 + lo: 6144 + lo + n]

        ROWA = {"sx": 0, "sx2": 512, "m": 1024, "rstd": 1536, "sg": 2048, "p4": 2560, "p2": 3072}
        ROWF = {"sx": 0, "sx2": 1024, "m2": 2048, "rstd2": 3072, "m2r2": 4096, "gd": 5120}

        def rowA(nm, n=512):
            return rows[:, ROWA[nm]:ROWA[nm] + n]

        def rowF(nm, n=1024):
            return rows[:, ROWF[nm]:ROWF[nm] + n]

        nc.const_aps.register(F32, EPS, ceps[:, 0:1])
        nc.const_aps.register(F32, EPS * EPS, ceps[:, 1:2])
        emit('g', lambda eng: eng.iota(ident[:], [[1, 128]], base=0, channel_multiplier=-1,
                                       allow_small_or_imprecise_dtypes=True))
        emit('g', lambda eng: eng.memset(ceps[:, 0:1], EPS))
        emit('g', lambda eng: eng.memset(ceps[:, 1:2], EPS * EPS))
        emit('g', lambda eng: eng.memset(ones_row[:], 1.0))
        emit('g', lambda eng: eng.tensor_scalar(ident[:], ident[:], 0.0, None, OP.is_equal))
        emit('g', lambda eng: eng.memset(ones_colb[:], 1.0))
        emit('g', lambda eng: eng.tensor_copy(identb[:], ident[:]))
        pP0 = emit('g', lambda eng: eng.memset(ones_colf[:], 1.0), inc=('P', 1))

        xload_tok = {}
        xload_sub = {}

        def emit_xload(i):
            b, kt = divmod(i, NKT)
            tok0 = b * S + kt * TA
            subs = []
            for sub in range(4):
                src = x_h[tok0 + sub * 128: tok0 + (sub + 1) * 128, :]
                v = emit('g', lambda eng, d=stage_x(sub), s=src: eng.dma_start(d, s), inc=('G', 16))
                subs.append(v)
            xload_tok[i] = ('G', v)
            xload_sub[i] = subs

        emit_xload(0)

        cst_prev_T = [0]

        def load_const(src_ap, nrows, ncols, dst_cols_ap):
            wait('g', 'T', cst_prev_T[0])
            gv = emit('g', lambda eng, d=cst[0:nrows, 0:128], s=src_ap: eng.dma_start(d, s), inc=('G', 16))
            wait('t', 'G', gv)
            wait('t', 'P', pP0)
            bk = take_banks(1)[0]
            tv = emit('t', lambda eng, o=pb[bk][0:128, 0:nrows], i_=cst[0:nrows, 0:128],
                      idn=ident[0:nrows, 0:nrows]:
                      eng.transpose(o, i_, idn), inc=('T', 1))
            cst_prev_T[0] = tv
            wait('s', 'T', tv)
            emit('s', lambda eng, d=dst_cols_ap, s=pb[bk][0:128, 0:nrows]: eng.copy(d, s))
            sv = emit('s', lambda eng: eng.drain(), inc=('S', 1))
            release_bank(bk, ('S', sv))
            return sv

        sv_wnc = load_const(wn_h, 8, 128, wnc[:, 0:8])
        load_const(bdt_h, 16, 128, bdtc[:, 0:16])
        load_const(b1_h, 64, 128, b1c[:, 0:64])
        load_const(sb1_h, 32, 128, sb1c[:, 0:32])
        load_const(b2_h, 16, 128, b2c[:, 0:16])
        sv_sb2 = load_const(sb2_h, 8, 128, sb2c[:, 0:8])
        wait('v', 'S', sv_sb2)
        emit('v', lambda eng: eng.tensor_add(bfc[:], b2c[:, 0:8], b2c[:, 8:16]))
        emit('v', lambda eng: eng.drain())
        emit('v', lambda eng: eng.tensor_add(bfc[:], bfc[:], sb2c[:, 0:8]))
        vv_bfc = emit('v', lambda eng: eng.drain(), inc=('V', 1))

        wait('g', 'T', cst_prev_T[0])
        gv = emit('g', lambda eng: eng.dma_start(cst[0:2, 0:1024], gw_h), inc=('G', 16))
        wait('t', 'G', gv)
        for c in range(Dc):
            bk = take_banks(1)[0]
            tv = emit('t', lambda eng, o=pb[bk][0:128, 0:2], i_=cst[0:2, c * 128:(c + 1) * 128]:
                      eng.transpose(o, i_, ident[0:2, 0:2]), inc=('T', 1))
            wait('s', 'T', tv)
            emit('s', lambda eng, d=gwd[:, c:c + 1], s=pb[bk][0:128, 0:1]: eng.copy(d, s))
            sv = emit('s', lambda eng: eng.drain(), inc=('S', 1))
            wait('v', 'S', sv)
            emit('v', lambda eng, d=gwd[:, c:c + 1], b=pb[bk][0:128, 1:2]:
                 eng.tensor_sub(d, d, b))
            vv = emit('v', lambda eng: eng.drain(), inc=('V', 1))
            release_bank(bk, ('V', vv))
        bk = take_banks(1)[0]
        wait('t', 'V', vv)
        for c in range(Dc):
            tv = emit('t', lambda eng, o=pb[bk][0:1, 0:1], l=ones_colf[:, 0:1], r=gwd[:, c:c + 1],
                      st=(c == 0), sp=(c == Dc - 1): eng.matmul(o, l, r, start=st, stop=sp),
                      inc=('T', 1) if c == Dc - 1 else None)
        wait('s', 'T', tv)
        sv = emit('s', lambda eng, s=pb[bk][0:1, 0:1]: eng.copy(swd[:], s), inc=('S', 1))
        release_bank(bk, ('S', sv))

        if DBG:
            emit('v', lambda eng: eng.tensor_add(bfc2[:], b2c[:, 0:8], b2c[:, 8:16]))
            emit('v', lambda eng: eng.tensor_add(bfc2[:], bfc2[:], sb2c[:, 0:8]), inc=('V', 1))
            wait('g', 'V', N['V'])
            wait('g', 'S', N['S'])
            emit('g', lambda eng: eng.dma_start(dbgH[:, 0:8], bfc[:, 0:8]), inc=('G', 16))
            emit('g', lambda eng: eng.dma_start(dbgH[:, 32:40], bfc2[:, 0:8]), inc=('G', 16))
            emit('g', lambda eng: eng.dma_start(dbgH[:, 8:24], b2c[:, 0:16]), inc=('G', 16))
            emit('g', lambda eng: eng.dma_start(dbgH[:, 24:32], sb2c[:, 0:8]), inc=('G', 16))

        nat_tok = {}

        nat_last = [0]

        def nat_cast(dst_blob, src_ap, nrows, rowlen, key):
            wait('g', 'G', nat_last[0])
            toks = []
            for r in range(nrows // 128):
                slot = r % 2
                stg = pack[:, 1024 + slot * 2048: 1024 + slot * 2048 + rowlen]
                if len(toks) >= 2:
                    wait('g', 'G', toks[-2])
                gv = emit('g', lambda eng, d=stg, s=src_ap[r * 128:(r + 1) * 128, :]:
                          eng.dma_start(d, s), inc=('G', 16))
                wait('g', 'G', gv)
                gv2 = emit('g', lambda eng, d=dst_blob[r * 128:(r + 1) * 128, :], s=stg:
                           eng.dma_start(d, s), inc=('G', 16))
                toks.append(gv2)
            nat_tok[key] = toks
            nat_last[0] = toks[-1]

        blob_tok = {}
        pk_state = {"u": 0, "g": 0, "v": 0, "slot": 0, "hist": []}

        def pack_piece(dst_ap, src_ap, mode, fold_col=None):
            slot = pk_state["slot"]
            pk_state["slot"] ^= 1
            stg = pack[:, slot * 512:(slot + 1) * 512]
            hist = pk_state["hist"]
            prev = None
            for (sl, tok) in reversed(hist):
                if sl == slot:
                    prev = tok
                    break
            if mode == 'T':
                if prev is not None:
                    wait('y', 'G', prev)
                uv = emit('y', lambda eng, d=stg, s=src_ap: eng.dma_start(d, s, transpose=True),
                          inc=('W', 16))
                fill_tok = ('W', uv)
            else:
                if prev is not None:
                    wait('g', 'G', prev)
                gv = emit('g', lambda eng, d=stg, s=src_ap: eng.dma_start(d, s), inc=('G', 16))
                fill_tok = ('G', gv)
            if fold_col is not None:
                wait('v', 'S', sv_wnc)
                wait_tok('v', fill_tok)
                emit('v', lambda eng, d=stg, col=fold_col:
                     eng.tensor_scalar(d, d, col, None, OP.mult))
                vv = emit('v', lambda eng: eng.drain(), inc=('V', 1))
                fill_tok = ('V', vv)
            wait_tok('g', fill_tok)
            gv2 = emit('g', lambda eng, d=dst_ap, s=stg: eng.dma_start(d, s), inc=('G', 16))
            hist.append((slot, gv2))
            if len(hist) > 8:
                del hist[:4]
            return gv2

        def pack_attn(blob, nat, name, G, K, fold=False):
            outw = G * 512
            for g in range(G):
                last = None
                for k in range(K):
                    sl0 = (g * 512) // 128
                    for sl in range(sl0, sl0 + 4):
                        wait('y', 'G', nat_tok[name][sl])
                    src = nat[g * 512:(g + 1) * 512, k * 128:(k + 1) * 128]
                    dst = blob[g * 128:(g + 1) * 128, k * 512:(k + 1) * 512]
                    last = pack_piece(dst, src, 'T', fold_col=wnc[:, k:k + 1] if fold else None)
                blob_tok[(name, g)] = ('G', last)

        pbB = [p.bitcast(BF16) for p in pb]
        asm = pack[:, 5120:9216]
        a1_slot_T = [0, 0]
        asm_store = [0]
        for g in range(8):
            ev_toks = []
            for sl in range(4):
                r = g * 4 + sl
                slot = r % 2
                stg = pack[:, 1024 + slot * 2048: 1024 + slot * 2048 + 1024]
                wait('g', 'T', a1_slot_T[slot])
                gv = emit('g', lambda eng, d=stg, s=win_h[r * 128:(r + 1) * 128, :]:
                          eng.dma_start(d, s), inc=('G', 16))
                wait('t', 'G', gv)
                wait('t', 'P', pP0)
                for kh in range(2):
                    bk = take_banks(1)[0]
                    for kq in range(4):
                        k = kh * 4 + kq
                        tv = emit('t', lambda eng, o=pbB[bk][:, kq * 128:(kq + 1) * 128],
                                  i_=stg[:, k * 128:(k + 1) * 128]:
                                  eng.transpose(o, i_, identb[:]),
                                  inc=('T', 1) if kq == 3 else None)
                    a1_slot_T[slot] = tv
                    e = 's' if (r + kh) % 2 else 'v'
                    wait(e, 'T', tv)
                    wait(e, 'S', sv_wnc)
                    wait(e, 'G', asm_store[0])
                    for kq in range(4):
                        k = kh * 4 + kq
                        dst = asm[:, k * 512 + sl * 128: k * 512 + sl * 128 + 128]
                        srcp = pbB[bk][:, kq * 128:(kq + 1) * 128]
                        if e == 's':
                            emit('s', lambda eng, d=dst, s=srcp, sc=wnc[:, k:k + 1]:
                                 eng.activation(d, s, AF.Copy, bias=0.0, scale=sc))
                        else:
                            emit('v', lambda eng, d=dst, s=srcp, sc=wnc[:, k:k + 1]:
                                 eng.tensor_scalar(d, s, sc, None, OP.mult))
                    ev = emit(e, lambda eng: eng.drain(), inc=('S' if e == 's' else 'V', 1))
                    release_bank(bk, ('S' if e == 's' else 'V', ev))
                    ev_toks.append(('S' if e == 's' else 'V', ev))
            for tok in ev_toks:
                wait_tok('g', tok)
            gsv = emit('g', lambda eng, d=blobA1[g * 128:(g + 1) * 128, :], s=asm:
                       eng.dma_start(d, s), inc=('G', 16))
            asm_store[0] = gsv
            blob_tok[("in", g)] = ('G', gsv)

        def APACK2():
            nat_cast(nat_dt, wdt_h, INNER, INNER, "dt")
            pack_attn(blobA2, nat_dt, "dt", 4, 16)

        def APACK3():
            nat_cast(nat_out, wout_h, D, INNER, "out")
            pack_attn(blobA3, nat_out, "out", 2, 16)

        FPACK = []

        def defer_fpack_w1like(blob, src_f32, name):
            for g in range(8):
                def mk(g=g, blob=blob, src=src_f32, name=name):
                    last = None
                    for k in range(8):
                        s = src[k * 128:(k + 1) * 128, g * 512:(g + 1) * 512]
                        d = blob[g * 128:(g + 1) * 128, k * 512:(k + 1) * 512]
                        last = pack_piece(d, s, 'C')
                    blob_tok[(name, g)] = ('G', last)
                FPACK.append(mk)

        def defer_fpack_w2like(blob, src_f32, name):
            for hg in range(2):
                for cg in range(2):
                    gi = hg * 2 + cg
                    def mk(hg=hg, cg=cg, gi=gi, blob=blob, src=src_f32, name=name):
                        last = None
                        for kl in range(16):
                            k = hg * 16 + kl
                            s = src[k * 128:(k + 1) * 128, cg * 512:(cg + 1) * 512]
                            d = blob[gi * 128:(gi + 1) * 128, kl * 512:(kl + 1) * 512]
                            last = pack_piece(d, s, 'C')
                        blob_tok[(name, gi)] = ('G', last)
                    FPACK.append(mk)

        defer_fpack_w1like(blobW1[0], w1_h[0:D, :], "w1_0")
        defer_fpack_w1like(blobW1[1], w1_h[D:2 * D, :], "w1_1")
        defer_fpack_w1like(blobSW1, sw1_h, "sw1")
        defer_fpack_w2like(blobW2[0], w2_h[0:HID, :], "w2_0")
        defer_fpack_w2like(blobW2[1], w2_h[HID:2 * HID, :], "w2_1")
        defer_fpack_w2like(blobSW2, sw2_h, "sw2")

        tok_tpose_done = {}
        tok_x1evac = {}
        tok_x1store = {}
        tok_ln1ops = {}
        tok_wdt_mm = {}
        tok_d1 = {}
        tok_pmul = {}
        tok_win_mm = {}
        tok_wout_mm = {}
        tok_uevac = {}
        tok_statT = {}

        for i in range(NTA):
            b, kt = divmod(i, NKT)
            tok0 = b * S + kt * TA
            pa = i % 2

            wait_tok('t', xload_tok[i])
            if i >= 2:
                wait_tok('t', tok_x1evac[i - 2])
            cp_last = None
            for sub in range(4):
                bks = take_banks(2)
                for c in range(Dc):
                    bk = bks[c // 4]
                    q = c % 4
                    tv = emit('t', lambda eng, o=pb[bk][:, q * 128:(q + 1) * 128],
                              i_=stage_x(sub, c * 128, 128): eng.transpose(o, i_, ident[:]),
                              inc=('T', 1) if c % 4 == 3 else None)
                    if c == 3:
                        tv_a = tv
                wait('s', 'T', tv_a)
                if i >= 2:
                    wait_tok('s', tok_x1evac[i - 2])
                for c in range(4):
                    emit('s', lambda eng, d=xr(pa, c, sub * 128, 128),
                         s=pb[bks[0]][:, (c % 4) * 128:(c % 4 + 1) * 128]: eng.copy(d, s))
                wait('s', 'T', tv)
                for c in range(4, 8):
                    emit('s', lambda eng, d=xr(pa, c, sub * 128, 128),
                         s=pb[bks[1]][:, (c % 4) * 128:(c % 4 + 1) * 128]: eng.copy(d, s))
                sv = emit('s', lambda eng: eng.drain(), inc=('S', 1))
                cp_last = sv
                release_bank(bks[0], ('S', sv))
                release_bank(bks[1], ('S', sv))
            tok_tpose_done[i] = ('T', tv)


            if i >= 1:
                wait_tok('s', tok_x1store[i - 1])
                wait_tok('s', tok_pmul[i - 1])
                wait_tok('s', tok_statT[i - 1])
            dump = AR4[:, 0:1024]
            for sub in range(4):
                wait('s', 'G', xload_sub[i][sub])
                emit('s', lambda eng, s=stage_x(sub), a=statq[:, 4 + sub:5 + sub]:
                     eng.activation(dump, s, AF.Square, accum_out=a))
                emit('s', lambda eng, s=stage_x(sub), a=statq[:, sub:sub + 1]:
                     eng.activation(dump, s, AF.Copy, bias=0.0, accum_out=a))
            stat_sv = emit('s', lambda eng: eng.drain(), inc=('S', 1))
            if i + 1 < NTA:
                wait_tok('g', tok_tpose_done[i])
                wait('g', 'S', stat_sv)
                emit_xload(i + 1)
            bsx = take_banks(1)[0]
            bsx2 = take_banks(1)[0]
            wait('t', 'S', stat_sv)
            for sub in range(4):
                emit('t', lambda eng, o=pb[bsx][0:1, sub * 128:(sub + 1) * 128],
                     i_=statq[:, sub:sub + 1]: eng.transpose(o, i_, ident[:]))
                tv2 = emit('t', lambda eng, o=pb[bsx2][0:1, sub * 128:(sub + 1) * 128],
                           i_=statq[:, 4 + sub:5 + sub]: eng.transpose(o, i_, ident[:]),
                           inc=('T', 1) if sub == 3 else None)
            tok_statT[i] = ('T', tv2)
            wait('s', 'T', tv2)
            emit('s', lambda eng, s=pb[bsx][0:1, :]: eng.mul(rowA("m"), s, 1.0 / D))
            emit('s', lambda eng, s=pb[bsx2][0:1, :]: eng.mul(rowA("sx2"), s, 1.0 / D))
            sv = emit('s', lambda eng: eng.drain(), inc=('S', 1))
            release_bank(bsx, ('S', sv))
            release_bank(bsx2, ('S', sv))
            wait('v', 'S', sv)
            emit('v', lambda eng: eng.tensor_mul(rowA("sx"), rowA("m"), rowA("m")))
            emit('v', lambda eng: eng.drain())
            emit('v', lambda eng: eng.tensor_sub(rowA("sx2"), rowA("sx2"), rowA("sx")))
            vv = emit('v', lambda eng: eng.drain(), inc=('V', 1))
            wait('s', 'V', vv)
            emit('s', lambda eng: eng.activation(rowA("rstd"), rowA("sx2"), AF.Ln, bias=EPS, scale=1.0))
            emit('s', lambda eng: eng.activation(rowA("sg"), rowA("sx2"), AF.Ln, bias=EPS * EPS,
                                                 scale=1.0 + EPS))
            emit('s', lambda eng: eng.activation(rowA("rstd"), rowA("rstd"), AF.Exp, bias=0.0, scale=-0.5))
            emit('s', lambda eng: eng.activation(rowA("sg"), rowA("sg"), AF.Exp, bias=0.0, scale=-0.5))
            sv = emit('s', lambda eng: eng.drain(), inc=('S', 1))
            wait('v', 'S', sv)
            emit('v', lambda eng: eng.tensor_mul(rowA("p4"), rowA("m"), rowA("sg")))
            emit('v', lambda eng: eng.tensor_mul(rowA("p2"), rowA("m"), rowA("rstd")))
            emit('v', lambda eng: eng.tensor_scalar(rowA("rstd"), rowA("rstd"), 1.0, None, OP.add))
            vv = emit('v', lambda eng: eng.drain(), inc=('V', 1))
            wait('t', 'V', vv)
            bks = take_banks(4)
            srcs = [rowA("rstd"), rowA("p2"), rowA("sg"), rowA("p4")]
            for bi, (bk, src) in enumerate(zip(bks, srcs)):
                tv = emit('t', lambda eng, o=pb[bk][:, 0:512], l=ones_row[0:1, :], r=src:
                          eng.matmul(o, l, r, start=True, stop=True),
                          inc=('T', 1) if bi == 3 else None)
            wait('s', 'T', tv)
            if i >= 1:
                wait_tok('s', tok_ln1ops[i - 1])
                wait_tok('s', tok_x1store[i - 1])
            for bi, bk in enumerate(bks):
                emit('s', lambda eng, d=bcB(bi), s=pb[bk][:, 0:512]: eng.copy(d, s))
            sv = emit('s', lambda eng: eng.drain(), inc=('S', 1))
            for bk in bks:
                release_bank(bk, ('S', sv))
            wait('v', 'S', sv)
            for c0 in range(0, Dc, 2):
                for c in (c0, c0 + 1):
                    emit('v', lambda eng, a=xr(pa, c), s=x2scr(c % 2): eng.tensor_mul(s, a, bcB(2)))
                for c in (c0, c0 + 1):
                    emit('v', lambda eng, c=c, s=x2scr(c % 2): eng.tensor_sub(gt(c), s, bcB(3)))
                for c in (c0, c0 + 1):
                    emit('v', lambda eng, a=xr(pa, c): eng.tensor_mul(a, a, bcB(0)))
                for c in (c0, c0 + 1):
                    emit('v', lambda eng, a=xr(pa, c): eng.tensor_sub(a, a, bcB(1)))
            vv = emit('v', lambda eng: eng.drain(), inc=('V', 1))
            tok_ln1ops[i] = ('V', vv)

            g_ready = ('V', vv)
            for g in range(8):
                col0, wv, sid = ring_load(blobA1[g * 128:(g + 1) * 128, :], 4096,
                                          ready_tok=blob_tok.get(("in", g)))
                bks = take_banks(4)
                wait('t', 'W', wv)
                wait_tok('t', g_ready)
                if i >= 1 and g == 0:
                    wait_tok('t', tok_uevac[i - 1])
                for c4 in range(4):
                    for k in range(8):
                        tv = emit('t', lambda eng, o=pb[bks[c4]][:, 0:512],
                                  l=ring[:, col0 + k * 512 + c4 * 128: col0 + k * 512 + c4 * 128 + 128],
                                  r=gt(k), st=(k == 0), sp=(k == 7):
                                  eng.matmul(o, l, r, start=st, stop=sp),
                                  inc=('T', 1) if (c4 == 3 and k == 7) else None)
                ring_free(sid, ('T', tv))
                wait('s', 'T', tv)
                if i >= 1 and g == 0:
                    wait_tok('s', tok_wdt_mm[i - 1])
                    wait_tok('s', tok_d1[i - 1])
                    wait_tok('s', tok_pmul[i - 1])
                for c4 in range(4):
                    oc = g * 4 + c4
                    dst = ut(oc) if oc < Ic else zt(oc - Ic)
                    emit('s', lambda eng, d=dst, s=pb[bks[c4]][:, 0:512]:
                         eng.activation(d, s, AF.Silu))
                sv = emit('s', lambda eng: eng.drain(), inc=('S', 1))
                for bk in bks:
                    release_bank(bk, ('S', sv))
                if g == 3:
                    u_done = ('S', sv)
            tok_uevac[i] = u_done
            z_done = ('S', sv)

            if i == 0:
                APACK2()

            last_scan_v = None
            for q in range(4):
                col0, wv, sid = ring_load(blobA2[q * 128:(q + 1) * 128, :], 8192,
                                          ready_tok=blob_tok.get(("dt", q)))
                bks = take_banks(4)
                wait('t', 'W', wv)
                wait_tok('t', u_done)
                for c4 in range(4):
                    for k in range(Ic):
                        tv = emit('t', lambda eng, o=pb[bks[c4]][:, 0:512],
                                  l=ring[:, col0 + k * 512 + c4 * 128: col0 + k * 512 + c4 * 128 + 128],
                                  r=ut(k), st=(k == 0), sp=(k == Ic - 1):
                                  eng.matmul(o, l, r, start=st, stop=sp),
                                  inc=('T', 1) if (c4 == 3 and k == Ic - 1) else None)
                ring_free(sid, ('T', tv))
                tok_wdt_mm[i] = ('T', tv)
                wait('s', 'T', tv)
                for c4 in range(4):
                    j = q * 4 + c4
                    emit('s', lambda eng, d=lam_g(c4), s=pb[bks[c4]][:, 0:512], bc=bdtc[:, j:j + 1]:
                         eng.activation(d, s, AF.Sigmoid, bias=bc, scale=1.0))
                sv = emit('s', lambda eng: eng.drain(), inc=('S', 1))
                for bk in bks:
                    release_bank(bk, ('S', sv))
                wait('v', 'S', sv)
                wait_tok('v', z_done)
                if i >= 1 and q == 0:
                    wait_tok('v', tok_x1store[i - 1])
                for c4 in range(4):
                    j = q * 4 + c4
                    emit('v', lambda eng, c4=c4, j=j:
                         eng.scalar_tensor_tensor(d1_g(c4), lam_g(c4), 1.0, ut(j), OP.subtract, OP.mult))
                for c4 in range(4):
                    j = q * 4 + c4
                    if kt == 0:
                        emit('v', lambda eng, c4=c4:
                             eng.tensor_tensor_scan(s_g(c4), lam_g(c4), d1_g(c4), 0.0, OP.mult, OP.subtract))
                    else:
                        emit('v', lambda eng, c4=c4, j=j:
                             eng.tensor_tensor_scan(s_g(c4), lam_g(c4), d1_g(c4), stst[:, j:j + 1],
                                                    OP.mult, OP.subtract))
                for c4 in range(4):
                    j = q * 4 + c4
                    emit('v', lambda eng, c4=c4, j=j:
                         eng.tensor_copy(stst[:, j:j + 1], s_g(c4, 511, 1)))
                for c4 in range(4):
                    j = q * 4 + c4
                    emit('v', lambda eng, c4=c4, j=j:
                         eng.tensor_mul(pt(j), s_g(c4), zt(j)))
                last_scan_v = emit('v', lambda eng: eng.drain(), inc=('V', 1))
            tok_d1[i] = ('V', last_scan_v)
            tok_pmul[i] = ('V', last_scan_v)
            p_done = ('V', last_scan_v)

            if i == 0:
                APACK3()

            st_tok = {}
            for g in range(2):
                col0, wv, sid = ring_load(blobA3[g * 128:(g + 1) * 128, :], 8192,
                                          ready_tok=blob_tok.get(("out", g)))
                bks = take_banks(4)
                wait('t', 'W', wv)
                wait_tok('t', p_done)
                for c4 in range(4):
                    for k in range(Ic):
                        tv = emit('t', lambda eng, o=pb[bks[c4]][:, 0:512],
                                  l=ring[:, col0 + k * 512 + c4 * 128: col0 + k * 512 + c4 * 128 + 128],
                                  r=pt(k), st=(k == 0), sp=(k == Ic - 1):
                                  eng.matmul(o, l, r, start=st, stop=sp),
                                  inc=('T', 1) if (c4 == 3 and k == Ic - 1) else None)
                ring_free(sid, ('T', tv))
                tok_wout_mm[i] = ('T', tv)
                wait('v', 'T', tv)
                for c4 in range(4):
                    c = g * 4 + c4
                    slot = c % 2
                    if c >= 2:
                        wait('v', 'G', st_tok[c - 2])
                    emit('v', lambda eng, slot=slot, s=pb[bks[c4]][:, 0:512], r=xr(pa, c):
                         eng.tensor_add(x1_st(slot), s, r))
                    vv = emit('v', lambda eng: eng.drain(), inc=('V', 1))
                    wait('g', 'V', vv)
                    gv = emit('g', lambda eng, slot=slot, c=c,
                              d=x1blob[c * 128:(c + 1) * 128, tok0:tok0 + TA]:
                              eng.dma_start(d, x1_st(slot)), inc=('G', 16))
                    st_tok[c] = gv
                sv_rel = ('V', vv)
                for bk in bks:
                    release_bank(bk, sv_rel)
            tok_x1evac[i] = ('V', vv)
            tok_x1store[i] = ('G', st_tok[7])

            if DBG and i == 0:
                wait('g', 'V', N['V'])
                wait('g', 'S', N['S'])
                emit('g', lambda eng: eng.dma_start(dbgA, AR2h[:, :]), inc=('G', 16))
                dbg_gv = emit('g', lambda eng: eng.dma_start(dbgB, AR3h[:, :]), inc=('G', 16))
                for e in 'tvs':
                    wait(e, 'G', dbg_gv)

            nslice = (len(FPACK) + NTA - 1 - i) // (NTA - i) if i < NTA - 1 else len(FPACK)
            for _ in range(min(nslice, len(FPACK))):
                FPACK.pop(0)()

        A_last_x1store = tok_x1store[NTA - 1]
        A_last_x1evac = tok_x1evac[NTA - 1]
        A_ln1 = tok_ln1ops[NTA - 1]
        A_pmul = tok_pmul[NTA - 1]
        A_wout = tok_wout_mm[NTA - 1]
        A_wdt = tok_wdt_mm[NTA - 1]
        A_d1 = tok_d1[NTA - 1]
        A_uevac = ('S', N['S'])
        A_tpose = tok_tpose_done[NTA - 1]

        tok_f_stt = {}
        tok_f_outst = {}
        tok_f_outcp = {}
        tok_f_m0rd = {}
        tok_f_g2rd = {}
        tok_f_hrd = {}
        tok_f_acc = {}

        for i in range(NTF):
            tok0 = i * TF

            if i == 0:
                wait_tok('g', A_last_x1evac)
                wait_tok('g', A_last_x1store)
            else:
                wait_tok('g', tok_f_stt[i - 1])
                wait_tok('g', tok_f_outst[i - 1])
            xg = {}
            for c in range(Dc):
                gv = emit('g', lambda eng, d=x1f(c), s=x1blob[c * 128:(c + 1) * 128, tok0:tok0 + TF]:
                          eng.dma_start(d, s), inc=('G', 16))
                xg[c] = gv
            x1_tok = ('G', gv)

            bsx = take_banks(2)
            bsx2 = take_banks(2)
            bgd = take_banks(2)
            wait('t', 'G', gv)
            if i == 0:
                wait_tok('s', A_last_x1store)
                wait_tok('s', A_pmul)
            else:
                wait_tok('s', tok_f_outcp[i - 1])
            sq_tok = {}
            bslot = lambda sl: AR4b[:, sl * 512:(sl + 1) * 512]
            for half in range(2):
                for c in range(Dc):
                    slot = (half * Dc + c) % 2
                    key = (half, c)
                    if (half * Dc + c) >= 2:
                        pk = (half, c - 2) if c >= 2 else (0, Dc - 2 + c)
                        wait('s', 'T', sq_tok[pk])
                    if i >= 1:
                        wait_tok('s', tok_f_acc[i - 1])
                    emit('s', lambda eng, d=bslot(slot * 2 + 1), s=x1f(c, half * 512, 512):
                         eng.activation(d, s, AF.Square))
                    ssv = emit('s', lambda eng: eng.drain(), inc=('S', 1))
                    tv = emit('t', lambda eng, o=pb[bsx[half]][0:1, :], l=ones_colf[:, 0:1],
                              r=x1f(c, half * 512, 512), st=(c == 0), sp=(c == Dc - 1):
                              eng.matmul(o, l, r, start=st, stop=sp),
                              inc=('T', 1) if c == Dc - 1 else None)
                    wait('t', 'S', ssv)
                    tv2 = emit('t', lambda eng, o=pb[bsx2[half]][0:1, :], l=ones_colb[:, 0:1],
                               r=bslot(slot * 2 + 1), st=(c == 0), sp=(c == Dc - 1):
                               eng.matmul(o, l, r, start=st, stop=sp), inc=('T', 1))
                    sq_tok[key] = tv2
                for c in range(Dc):
                    tg = emit('t', lambda eng, o=pb[bgd[half]][0:1, :], l=gwd[:, c:c + 1],
                              r=x1f(c, half * 512, 512), st=(c == 0), sp=(c == Dc - 1):
                              eng.matmul(o, l, r, start=st, stop=sp),
                              inc=('T', 1) if c == Dc - 1 else None)
            wait('s', 'T', tg)
            for half in range(2):
                emit('s', lambda eng, h=half, s=pb[bsx[half]][0:1, :]:
                     eng.mul(rowF("m2")[:, h * 512:(h + 1) * 512], s, 1.0 / D))
                emit('s', lambda eng, h=half, s=pb[bsx2[half]][0:1, :]:
                     eng.mul(rowF("sx2")[:, h * 512:(h + 1) * 512], s, 1.0 / D))
                emit('s', lambda eng, h=half, s=pb[bgd[half]][0:1, :]:
                     eng.copy(rowF("gd")[:, h * 512:(h + 1) * 512], s))
            sv = emit('s', lambda eng: eng.drain(), inc=('S', 1))
            for bk in bsx + bsx2 + bgd:
                release_bank(bk, ('S', sv))
            wait('v', 'S', sv)
            emit('v', lambda eng: eng.tensor_mul(rowF("sx"), rowF("m2"), rowF("m2")))
            emit('v', lambda eng: eng.drain())
            emit('v', lambda eng: eng.tensor_sub(rowF("sx2"), rowF("sx2"), rowF("sx")))
            vv = emit('v', lambda eng: eng.drain(), inc=('V', 1))
            wait('s', 'V', vv)
            emit('s', lambda eng: eng.activation(rowF("rstd2"), rowF("sx2"), AF.Ln, bias=EPS, scale=1.0))
            emit('s', lambda eng: eng.drain())
            emit('s', lambda eng: eng.activation(rowF("rstd2"), rowF("rstd2"), AF.Exp, bias=0.0,
                                                 scale=-0.5))
            sv = emit('s', lambda eng: eng.drain(), inc=('S', 1))
            wait('v', 'S', sv)
            emit('v', lambda eng: eng.tensor_scalar(rowF("sx"), rowF("m2"), swd[0:1, 0:1], None, OP.mult))
            emit('v', lambda eng: eng.tensor_mul(rowF("m2r2"), rowF("m2"), rowF("rstd2")))
            emit('v', lambda eng: eng.tensor_sub(rowF("gd"), rowF("gd"), rowF("sx")))
            emit('v', lambda eng: eng.drain())
            emit('v', lambda eng: eng.tensor_scalar(rowF("gd"), rowF("gd"), 0.0, None, OP.is_ge))
            vv = emit('v', lambda eng: eng.drain(), inc=('V', 1))
            wait('t', 'V', vv)
            bks = take_banks(6)
            for half in range(2):
                for bi, src in enumerate([rowF("rstd2"), rowF("m2r2"), rowF("gd")]):
                    bk = bks[bi * 2 + half]
                    tv = emit('t', lambda eng, o=pb[bk][:, 0:512], l=ones_row[0:1, :],
                              r=src[:, half * 512:(half + 1) * 512]:
                              eng.matmul(o, l, r, start=True, stop=True),
                              inc=('T', 1) if (half == 1 and bi == 2) else None)
            wait('s', 'T', tv)
            if i >= 1:
                wait_tok('s', tok_f_acc[i - 1])
            for half in range(2):
                for bi, dstf in enumerate([rstd2B, m2r2B, mask0B]):
                    bk = bks[bi * 2 + half]
                    emit('s', lambda eng, d=dstf(half * 512, 512), s=pb[bk][:, 0:512]:
                         eng.copy(d, s))
            sv = emit('s', lambda eng: eng.drain(), inc=('S', 1))
            for bk in bks:
                release_bank(bk, ('S', sv))
            wait('v', 'S', sv)
            if i == 0:
                wait_tok('v', A_wdt)
                wait_tok('v', A_d1)
                wait_tok('v', A_uevac)
                wait_tok('v', A_pmul)
            else:
                wait_tok('v', tok_f_g2rd[i - 1])
                wait_tok('v', tok_f_m0rd[i - 1])
            scrL = [AR4[:, 3584:4608], AR4[:, 4608:5632]]
            for c0 in range(0, Dc, 2):
                for c in (c0, c0 + 1):
                    emit('v', lambda eng, c=c, s=scrL[c % 2]: eng.tensor_mul(s, x1f(c), rstd2B()))
                for c in (c0, c0 + 1):
                    emit('v', lambda eng, c=c, s=scrL[c % 2]: eng.tensor_sub(g2(c), s, m2r2B()))
                for c in (c0, c0 + 1):
                    emit('v', lambda eng, c=c: eng.tensor_mul(m0(c), g2(c), mask0B()))
            vv = emit('v', lambda eng: eng.drain(), inc=('V', 1))
            m_ready = ('V', vv)

            if DBG and i == 0:
                wait('g', 'V', N['V'])
                dbg_gv = emit('g', lambda eng: eng.dma_start(dbgC, AR2h[:, :]), inc=('G', 16))
                for e in 'tvs':
                    wait(e, 'G', dbg_gv)

            def h_pass(blob_name, blob, rhs_fn, hg, pass_idx):
                relu_last = None
                if blob_name == "w1_0":
                    bias_col = lambda j: b1c[:, j:j + 1]
                elif blob_name == "w1_1":
                    bias_col = lambda j: b1c[:, 32 + j:33 + j]
                else:
                    bias_col = lambda j: sb1c[:, j:j + 1]
                for gl in range(4):
                    g = hg * 4 + gl
                    col0, wv, sid = ring_load(blob[g * 128:(g + 1) * 128, :], 4096,
                                              ready_tok=blob_tok.get((blob_name, g)))
                    for sub in range(2):
                        bks = take_banks(4)
                        wait('t', 'W', wv)
                        wait_tok('t', m_ready)
                        if pass_idx == 0 and hg == 0 and gl == 0 and sub == 0:
                            if i == 0:
                                wait_tok('t', A_wout)
                                wait_tok('s', A_d1)
                            else:
                                wait_tok('t', tok_f_hrd[i - 1])
                        for c2 in range(2):
                            for half in range(2):
                                bk = bks[c2 * 2 + half]
                                for k in range(Dc):
                                    tv = emit('t', lambda eng, o=pb[bk][:, 0:512],
                                              l=ring[:, col0 + k * 512 + (sub * 2 + c2) * 128:
                                                     col0 + k * 512 + (sub * 2 + c2) * 128 + 128],
                                              r=rhs_fn(k, half), st=(k == 0), sp=(k == Dc - 1):
                                              eng.matmul(o, l, r, start=st, stop=sp),
                                              inc=('T', 1) if (c2 == 1 and half == 1 and k == Dc - 1) else None)
                        if sub == 1:
                            ring_free(sid, ('T', tv))
                        wait('s', 'T', tv)
                        for c2 in range(2):
                            j = g * 4 + sub * 2 + c2
                            jj = j - hg * 16
                            for half in range(2):
                                bk = bks[c2 * 2 + half]
                                emit('s', lambda eng, d=ht(jj, half * 512, 512), s=pb[bk][:, 0:512],
                                     bc=bias_col(j):
                                     eng.activation(d, s, AF.Relu, bias=bc, scale=1.0))
                        sv = emit('s', lambda eng: eng.drain(), inc=('S', 1))
                        for bk in bks:
                            release_bank(bk, ('S', sv))
                        relu_last = sv
                return ('S', relu_last)

            def w2_pass(blob_name, blob, hg, first):
                h_ready = None
                for cg in range(2):
                    gi = hg * 2 + cg
                    col0, wv, sid = ring_load(blob[gi * 128:(gi + 1) * 128, :], 8192,
                                              ready_tok=blob_tok.get((blob_name, gi)))
                    for sub in range(2):
                        bks = take_banks(4)
                        wait('t', 'W', wv)
                        for c2 in range(2):
                            for half in range(2):
                                bk = bks[c2 * 2 + half]
                                for kl in range(16):
                                    tv = emit('t', lambda eng, o=pb[bk][:, 0:512],
                                              l=ring[:, col0 + kl * 512 + (sub * 2 + c2) * 128:
                                                     col0 + kl * 512 + (sub * 2 + c2) * 128 + 128],
                                              r=ht(kl, half * 512, 512), st=(kl == 0), sp=(kl == 15):
                                              eng.matmul(o, l, r, start=st, stop=sp),
                                              inc=('T', 1) if (c2 == 1 and half == 1 and kl == 15) else None)
                        if sub == 1:
                            ring_free(sid, ('T', tv))
                        wait('v', 'T', tv)
                        for c2 in range(2):
                            c = cg * 4 + sub * 2 + c2
                            for half in range(2):
                                bk = bks[c2 * 2 + half]
                                if first:
                                    vv = emit('v', lambda eng, c=c, half=half, bk=bk:
                                              eng.tensor_copy(out_acc(c, half * 512, 512),
                                                              pb[bk][:, 0:512]),
                                              inc=('V', 1) if (c2 == 1 and half == 1) else None)
                                else:
                                    vv = emit('v', lambda eng, c=c, half=half, bk=bk:
                                              eng.tensor_add(out_acc(c, half * 512, 512),
                                                             out_acc(c, half * 512, 512),
                                                             pb[bk][:, 0:512]),
                                              inc=('V', 1) if (c2 == 1 and half == 1) else None)
                        for bk in bks:
                            release_bank(bk, ('V', vv))
                return ('V', vv)

            first_acc = True
            acc_tok = None
            for (nm1, bl1, rhs1, nm2, bl2) in [("w1_0", blobW1[0], lambda k, h: m0(k, h * 512, 512), "w2_0", blobW2[0]),
                                               ("w1_1", blobW1[1], lambda k, h: m0(k, h * 512, 512), "w2_1", blobW2[1]),
                                               ("sw1", blobSW1, lambda k, h: g2(k, h * 512, 512), "sw2", blobSW2)]:
                if nm1 == "w1_1":
                    wait_tok('v', tok_f_m0rd[i])
                    for c in range(Dc):
                        emit('v', lambda eng, c=c: eng.tensor_sub(m0(c), g2(c), m0(c)))
                    mv = emit('v', lambda eng: eng.drain(), inc=('V', 1))
                    m_ready = ('V', mv)
                for hg in range(2):
                    hr = h_pass(nm1, bl1, rhs1, hg, 0 if first_acc else 1)
                    wait_tok('t', hr)
                    acc_tok = w2_pass(nm2, bl2, hg, first_acc and hg == 0)
                    if hg == 0 and first_acc:
                        first_acc = False
                    tok_f_hrd[i] = ('T', N['T'])
                if nm1 == "w1_0":
                    tok_f_m0rd[i] = ('T', N['T'])
                if nm1 == "w1_1":
                    tok_f_m0rd[i] = ('T', N['T'])
                if nm1 == "sw1":
                    tok_f_g2rd[i] = ('T', N['T'])

            if DBG and i == 0:
                wait_tok('g', acc_tok)
                emit('g', lambda eng: eng.dma_start(dbgD, AR4[:, :]), inc=('G', 16))
                emit('g', lambda eng: eng.dma_start(dbgF, AR1[:, :]), inc=('G', 16))
                dbg_gv = emit('g', lambda eng: eng.dma_start(dbgG, bfc[:, 0:8]), inc=('G', 16))
                for e in 'tvs':
                    wait(e, 'G', dbg_gv)

            wait_tok('v', acc_tok)
            for c in range(Dc):
                emit('v', lambda eng, c=c:
                     eng.scalar_tensor_tensor(out_acc(c), out_acc(c), bfc[:, c:c + 1], x1f(c),
                                              OP.add, OP.add))
            fv = emit('v', lambda eng: eng.drain(), inc=('V', 1))
            tok_f_stt[i] = ('V', fv)

            if DBG and i == 0:
                wait('g', 'V', N['V'])
                dbg_gv = emit('g', lambda eng: eng.dma_start(dbgE, AR4[:, :]), inc=('G', 16))
                for e in 'tvs':
                    wait(e, 'G', dbg_gv)

            out_last_g = None
            for sub in range(8):
                slot = sub % 2
                bks = take_banks(2)
                wait('t', 'V', fv)
                for c in range(Dc):
                    bk = bks[c // 4]
                    qq = c % 4
                    tv = emit('t', lambda eng, o=pb[bk][:, qq * 128:(qq + 1) * 128],
                              i_=out_acc(c, sub * 128, 128): eng.transpose(o, i_, ident[:]),
                              inc=('T', 1) if qq == 3 else None)
                    if c == 3:
                        tv_a = tv
                wait('s', 'T', tv)
                if i >= 1 and sub < 2:
                    wait_tok('s', tok_f_outst[i - 1])
                if sub >= 2:
                    wait('s', 'G', outst_tok[sub - 2])
                elif i == 0 and sub < 2:
                    wait_tok('s', A_last_x1evac)
                if sub == 0:
                    outst_tok = {}
                for c in range(Dc):
                    bk = bks[c // 4]
                    emit('s', lambda eng, d=out_tm(slot)[:, c * 128:(c + 1) * 128],
                         s=pb[bk][:, (c % 4) * 128:(c % 4 + 1) * 128]: eng.copy(d, s))
                sv = emit('s', lambda eng: eng.drain(), inc=('S', 1))
                for bk in bks:
                    release_bank(bk, ('S', sv))
                tok_f_outcp[i] = ('S', sv)
                wait('g', 'S', sv)
                gv = emit('g', lambda eng, d=out_h[tok0 + sub * 128: tok0 + (sub + 1) * 128, :],
                          s=out_tm(slot): eng.dma_start(d, s), inc=('G', 16))
                outst_tok[sub] = gv
                out_last_g = gv
            tok_f_outst[i] = ('G', out_last_g)
            tok_f_acc[i] = ('T', N['T'])

        while FPACK:
            FPACK.pop(0)()

        wait('g', 'G', N['G'])

        block = E(nc.Block())

        @block.tensor
        def _(eng):
            for f in OPS['t']:
                f(eng)

        @block.vector
        def _(eng):
            for f in OPS['v']:
                f(eng)

        @block.scalar
        def _(eng):
            for f in OPS['s']:
                f(eng)

        @block.gpsimd
        def _(eng):
            for f in OPS['g']:
                f(eng)

        @block.sync
        def _(eng):
            for f in OPS['y']:
                f(eng)

    return nc


_NC_CACHE = [None]


def kernel(**inputs):
    xs = np.ascontiguousarray(inputs["L0_attn_x"], dtype=np.float32)
    ins_common = {
        "w_norm": np.ascontiguousarray(inputs["L0_attn_w_norm"], np.float32).reshape(8, 128),
        "W_in": np.ascontiguousarray(inputs["L0_attn_W_in"], np.float32),
        "W_dt": np.ascontiguousarray(inputs["L0_attn_W_dt"], np.float32),
        "b_dt": np.ascontiguousarray(inputs["L0_attn_b_dt"], np.float32).reshape(16, 128),
        "W_out": np.ascontiguousarray(inputs["L0_attn_W_out"], np.float32),
        "gate_w": np.ascontiguousarray(inputs["L0_ffn_gate_w"], np.float32),
        "w1": np.ascontiguousarray(inputs["L0_ffn_w1"], np.float32).reshape(2 * D, HID),
        "b1": np.ascontiguousarray(inputs["L0_ffn_b1"], np.float32).reshape(64, 128),
        "w2": np.ascontiguousarray(inputs["L0_ffn_w2"], np.float32).reshape(2 * HID, D),
        "b2": np.ascontiguousarray(inputs["L0_ffn_b2"], np.float32).reshape(16, 128),
        "sw1": np.ascontiguousarray(inputs["L0_ffn_sw1"], np.float32),
        "sb1": np.ascontiguousarray(inputs["L0_ffn_sb1"], np.float32).reshape(32, 128),
        "sw2": np.ascontiguousarray(inputs["L0_ffn_sw2"], np.float32),
        "sb2": np.ascontiguousarray(inputs["L0_ffn_sb2"], np.float32).reshape(8, 128),
    }
    in_maps = []
    for core in range(NCORES):
        m = dict(ins_common)
        m["x"] = np.ascontiguousarray(xs[core * NB:(core + 1) * NB].reshape(NTOK, D))
        in_maps.append(m)
    if _NC_CACHE[0] is None:
        _NC_CACHE[0] = build_nc()
    nc = _NC_CACHE[0]
    res = run_bass_kernel_spmd(nc, in_maps, core_ids=list(range(NCORES)))
    out = np.concatenate([r["out"].reshape(NB, S, D) for r in res.results], axis=0)
    return out


# version 08
# speedup vs baseline: 1.1833x; 1.1833x over previous
import contextlib
import numpy as np
import concourse.bass as bass
import concourse.mybir as mybir
from concourse.bass_utils import run_bass_kernel_spmd

F32 = mybir.dt.float32
BF16 = mybir.dt.bfloat16
AF = mybir.ActivationFunctionType
OP = mybir.AluOpType

NCORES = 8
D, Dc = 1024, 8
INNER, Ic = 2048, 16
HID, Hc = 4096, 32
NB, S = 4, 2048
NTOK = NB * S
TA, NKT = 512, 4
NTA = NB * NKT
TF, NTF = 1024, 8
EPS = 1e-5

RF = 16384
DBG = False


def build_nc():
    nc = bass.Bass("TRN2", target_bir_lowering=False)

    x_h = nc.dram_tensor("x", [NTOK, D], F32, kind="ExternalInput").ap()
    wn_h = nc.dram_tensor("w_norm", [8, 128], F32, kind="ExternalInput").ap()
    win_h = nc.dram_tensor("W_in", [2 * INNER, D], F32, kind="ExternalInput").ap()
    wdt_h = nc.dram_tensor("W_dt", [INNER, INNER], F32, kind="ExternalInput").ap()
    bdt_h = nc.dram_tensor("b_dt", [16, 128], F32, kind="ExternalInput").ap()
    wout_h = nc.dram_tensor("W_out", [D, INNER], F32, kind="ExternalInput").ap()
    gw_h = nc.dram_tensor("gate_w", [2, D], F32, kind="ExternalInput").ap()
    w1_h = nc.dram_tensor("w1", [2 * D, HID], F32, kind="ExternalInput").ap()
    b1_h = nc.dram_tensor("b1", [64, 128], F32, kind="ExternalInput").ap()
    w2_h = nc.dram_tensor("w2", [2 * HID, D], F32, kind="ExternalInput").ap()
    b2_h = nc.dram_tensor("b2", [16, 128], F32, kind="ExternalInput").ap()
    sw1_h = nc.dram_tensor("sw1", [D, HID], F32, kind="ExternalInput").ap()
    sb1_h = nc.dram_tensor("sb1", [32, 128], F32, kind="ExternalInput").ap()
    sw2_h = nc.dram_tensor("sw2", [HID, D], F32, kind="ExternalInput").ap()
    sb2_h = nc.dram_tensor("sb2", [8, 128], F32, kind="ExternalInput").ap()
    out_h = nc.dram_tensor("out", [NTOK, D], F32, kind="ExternalOutput").ap()

    nat_in = nc.dram_tensor("nat_in", [2 * INNER, D], BF16, kind="Internal").ap()
    nat_dt = nc.dram_tensor("nat_dt", [INNER, INNER], BF16, kind="Internal").ap()
    nat_out = nc.dram_tensor("nat_out", [D, INNER], BF16, kind="Internal").ap()
    blobA1 = nc.dram_tensor("blobA1", [8 * 128, 8 * 512], BF16, kind="Internal").ap()
    blobA2 = nc.dram_tensor("blobA2", [4 * 128, 16 * 512], BF16, kind="Internal").ap()
    blobA3 = nc.dram_tensor("blobA3", [2 * 128, 16 * 512], BF16, kind="Internal").ap()
    blobW1 = [nc.dram_tensor(f"blobW1_{e}", [8 * 128, 8 * 512], BF16, kind="Internal").ap() for e in range(2)]
    blobSW1 = nc.dram_tensor("blobSW1", [8 * 128, 8 * 512], BF16, kind="Internal").ap()
    blobW2 = [nc.dram_tensor(f"blobW2_{e}", [4 * 128, 16 * 512], BF16, kind="Internal").ap() for e in range(2)]
    blobSW2 = nc.dram_tensor("blobSW2", [4 * 128, 16 * 512], BF16, kind="Internal").ap()
    x1blob = nc.dram_tensor("x1blob", [Dc * 128, NTOK], F32,
                            kind="ExternalOutput" if DBG else "Internal").ap()
    if DBG:
        dbgA = nc.dram_tensor("dbgA", [128, 16384], BF16, kind="ExternalOutput").ap()
        dbgB = nc.dram_tensor("dbgB", [128, 16384], BF16, kind="ExternalOutput").ap()
        dbgC = nc.dram_tensor("dbgC", [128, 16384], BF16, kind="ExternalOutput").ap()
        dbgD = nc.dram_tensor("dbgD", [128, 8192], F32, kind="ExternalOutput").ap()
        dbgE = nc.dram_tensor("dbgE", [128, 8192], F32, kind="ExternalOutput").ap()
        dbgF = nc.dram_tensor("dbgF", [128, 8192], F32, kind="ExternalOutput").ap()
        dbgG = nc.dram_tensor("dbgG", [128, 8], F32, kind="ExternalOutput").ap()
        dbgH = nc.dram_tensor("dbgH", [128, 40], F32, kind="ExternalOutput").ap()

    with contextlib.ExitStack() as es:
        E = es.enter_context
        AR1 = E(nc.sbuf_tensor("AR1", [128, 8192], F32))
        AR2h = E(nc.sbuf_tensor("AR2", [128, 16384], BF16))
        AR3h = E(nc.sbuf_tensor("AR3", [128, 16384], BF16))
        AR4 = E(nc.sbuf_tensor("AR4", [128, 8192], F32))
        ring = E(nc.sbuf_tensor("ringt", [128, RF], BF16))
        pack = E(nc.sbuf_tensor("pack", [128, 7168], BF16))
        statq = E(nc.sbuf_tensor("statq", [128, 8], F32))
        ident = E(nc.sbuf_tensor("ident", [128, 128], F32))
        ones_row = E(nc.sbuf_tensor("ones_row", [1, 128], F32))
        ones_colf = E(nc.sbuf_tensor("ones_colf", [128, 1], F32))
        ones_colb = E(nc.sbuf_tensor("ones_colb", [128, 1], BF16))
        identb = E(nc.sbuf_tensor("identb", [128, 128], BF16))
        rows = E(nc.sbuf_tensor("rows", [1, 6144], F32))
        cst = E(nc.sbuf_tensor("cst", [64, 1024], F32))
        wnc = E(nc.sbuf_tensor("wnc", [128, 8], F32))
        bdtc = E(nc.sbuf_tensor("bdtc", [128, 16], F32))
        b1c = E(nc.sbuf_tensor("b1c", [128, 64], F32))
        sb1c = E(nc.sbuf_tensor("sb1c", [128, 32], F32))
        b2c = E(nc.sbuf_tensor("b2c", [128, 16], F32))
        sb2c = E(nc.sbuf_tensor("sb2c", [128, 8], F32))
        bfc = E(nc.sbuf_tensor("bfc", [128, 8], F32))
        gwd = E(nc.sbuf_tensor("gwd", [128, 8], F32))
        swd = E(nc.sbuf_tensor("swd", [1, 1], F32))
        stst = E(nc.sbuf_tensor("stst", [128, 16], F32))
        ceps = E(nc.sbuf_tensor("ceps", [128, 2], F32))
        bfc2 = E(nc.sbuf_tensor("bfc2", [128, 8], F32)) if DBG else None
        pb = [E(nc.psum_tensor(f"pb{i}", [128, 512], F32)) for i in range(8)]

        sem = {k: E(nc.semaphore(f"sem{k}")) for k in "TVSGWUP"}

        AR4b = AR4.bitcast(BF16)

        OPS = {k: [] for k in "tvsgy"}
        N = {k: 0 for k in "TVSGWUP"}
        WAITED = {e: {} for e in "tvsgy"}

        def emit(e, fn, inc=None):
            v = None
            if inc is not None:
                key, amt = inc
                N[key] += amt
                v = N[key]
                h = sem[key]
                def thunk(eng, fn=fn, h=h, amt=amt):
                    fn(eng).then_inc(h, amt)
            else:
                def thunk(eng, fn=fn):
                    fn(eng)
            OPS[e].append(thunk)
            return v

        def wait(e, key, val):
            if val is None or val <= 0:
                return
            if val <= WAITED[e].get(key, 0):
                return
            WAITED[e][key] = val
            h = sem[key]
            OPS[e].append(lambda eng, h=h, val=val: eng.wait_ge(h, val))

        def wait_tok(e, tok):
            if tok is not None:
                wait(e, tok[0], tok[1])

        bank_tok = [None] * 8
        bank_rr = [0]

        def take_banks(n):
            ids = []
            for _ in range(n):
                i = bank_rr[0] % 8
                bank_rr[0] += 1
                wait_tok('t', bank_tok[i])
                bank_tok[i] = None
                ids.append(i)
            return ids

        def release_bank(i, tok):
            bank_tok[i] = tok

        ring_state = {"head": 0, "spans": []}
        span_autoid = [0]
        span_free = {}

        def ring_load(src_ap, ncols, ready_tok=None):
            st = ring_state
            head = st["head"]
            if head % 4096:
                head += 4096 - head % 4096
            if head + ncols > RF:
                head = 0
            keep = []
            for (s0, e0, sid) in st["spans"]:
                if s0 < head + ncols and e0 > head:
                    tok = span_free.get(sid)
                    assert tok is not None, "ring overwrite before consumer registered"
                    wait_tok('y', tok)
                else:
                    keep.append((s0, e0, sid))
            st["spans"] = keep
            sid = span_autoid[0]
            span_autoid[0] += 1
            st["spans"].append((head, head + ncols, sid))
            st["head"] = head + ncols
            if ready_tok is not None:
                wait_tok('y', ready_tok)
            dst = ring[:, head:head + ncols]
            wv = emit('y', lambda eng, dst=dst, src=src_ap: eng.dma_start(dst, src), inc=('W', 16))
            return head, wv, sid

        def ring_free(sid, tok):
            span_free[sid] = tok

        def ring_load_multi(src_ap, total_cols, ready_tok=None):
            chunks = []
            wv = None
            for off in range(0, total_cols, 4096):
                c0, wv, sid = ring_load(src_ap[:, off:off + 4096], 4096, ready_tok=ready_tok)
                chunks.append((c0, sid))
            def colof(k, chunks=tuple(c for c, _ in chunks)):
                return chunks[(k * 512) // 4096] + (k * 512) % 4096
            return colof, wv, [s for _, s in chunks]

        def xr(pa, c, lo=0, n=512):
            return AR1[:, pa * 4096 + c * 512 + lo: pa * 4096 + c * 512 + lo + n]

        def x1f(c, lo=0, n=1024):
            return AR1[:, c * 1024 + lo: c * 1024 + lo + n]

        def out_tm(slot):
            return AR1[:, slot * 1024:(slot + 1) * 1024]

        def ut(j, lo=0, n=512):
            return AR2h[:, j * 512 + lo: j * 512 + lo + n]

        def zt(j, lo=0, n=512):
            return AR2h[:, 8192 + j * 512 + lo: 8192 + j * 512 + lo + n]

        def g2(c, lo=0, n=1024):
            return AR2h[:, c * 1024 + lo: c * 1024 + lo + n]

        def m0(c, lo=0, n=1024):
            return AR2h[:, 8192 + c * 1024 + lo: 8192 + c * 1024 + lo + n]

        def pt(j, lo=0, n=512):
            return AR3h[:, j * 512 + lo: j * 512 + lo + n]

        def gt(c, lo=0, n=512):
            return AR3h[:, 8192 + c * 512 + lo: 8192 + c * 512 + lo + n]

        AR3f = AR3h.bitcast(F32)

        def lam_g(q, n=512):
            return AR3f[:, 6144 + q * 512: 6144 + q * 512 + n]

        def d1_g(q, n=512):
            return AR3h[:, 8192 + q * 512: 8192 + q * 512 + n]

        def ht(j, lo=0, n=1024):
            return AR3h[:, j * 1024 + lo: j * 1024 + lo + n]

        def s_g(q, lo=0, n=512):
            return AR4[:, q * 512 + lo: q * 512 + lo + n]

        def x2scr(slot, n=512):
            return AR4[:, slot * 512: slot * 512 + n]

        def x1_st(slot):
            return AR4[:, slot * 512:(slot + 1) * 512]

        def bcB(i, n=512):
            return AR4[:, 2048 + i * 512: 2048 + i * 512 + n]

        def stage_x(sub, lo=0, n=1024):
            return AR4[:, 4096 + sub * 1024 + lo: 4096 + sub * 1024 + lo + n]

        def out_acc(c, lo=0, n=1024):
            return AR4[:, c * 1024 + lo: c * 1024 + lo + n]

        def rstd2B(lo=0, n=1024):
            return AR4[:, 1024 + lo: 1024 + lo + n]

        def m2r2B(lo=0, n=1024):
            return AR4[:, 2048 + lo: 2048 + lo + n]

        def mask0B(lo=0, n=1024):
            return AR4b[:, 6144 + lo: 6144 + lo + n]

        ROWA = {"sx": 0, "sx2": 512, "m": 1024, "rstd": 1536, "sg": 2048, "p4": 2560, "p2": 3072}
        ROWF = {"sx": 0, "sx2": 1024, "m2": 2048, "rstd2": 3072, "m2r2": 4096, "gd": 5120}

        def rowA(nm, n=512):
            return rows[:, ROWA[nm]:ROWA[nm] + n]

        def rowF(nm, n=1024):
            return rows[:, ROWF[nm]:ROWF[nm] + n]

        nc.const_aps.register(F32, EPS, ceps[:, 0:1])
        nc.const_aps.register(F32, EPS * EPS, ceps[:, 1:2])
        emit('g', lambda eng: eng.iota(ident[:], [[1, 128]], base=0, channel_multiplier=-1,
                                       allow_small_or_imprecise_dtypes=True))
        emit('g', lambda eng: eng.memset(ceps[:, 0:1], EPS))
        emit('g', lambda eng: eng.memset(ceps[:, 1:2], EPS * EPS))
        emit('g', lambda eng: eng.memset(ones_row[:], 1.0))
        emit('g', lambda eng: eng.tensor_scalar(ident[:], ident[:], 0.0, None, OP.is_equal))
        emit('g', lambda eng: eng.memset(ones_colb[:], 1.0))
        emit('g', lambda eng: eng.tensor_copy(identb[:], ident[:]))
        pP0 = emit('g', lambda eng: eng.memset(ones_colf[:], 1.0), inc=('P', 1))

        xload_tok = {}
        xload_sub = {}

        def emit_xload(i):
            b, kt = divmod(i, NKT)
            tok0 = b * S + kt * TA
            subs = []
            for sub in range(4):
                src = x_h[tok0 + sub * 128: tok0 + (sub + 1) * 128, :]
                v = emit('g', lambda eng, d=stage_x(sub), s=src: eng.dma_start(d, s), inc=('G', 16))
                subs.append(v)
            xload_tok[i] = ('G', v)
            xload_sub[i] = subs

        emit_xload(0)

        cst_prev_T = [0]

        def load_const(src_ap, nrows, ncols, dst_cols_ap):
            wait('g', 'T', cst_prev_T[0])
            gv = emit('g', lambda eng, d=cst[0:nrows, 0:128], s=src_ap: eng.dma_start(d, s), inc=('G', 16))
            wait('t', 'G', gv)
            wait('t', 'P', pP0)
            bk = take_banks(1)[0]
            tv = emit('t', lambda eng, o=pb[bk][0:128, 0:nrows], i_=cst[0:nrows, 0:128],
                      idn=ident[0:nrows, 0:nrows]:
                      eng.transpose(o, i_, idn), inc=('T', 1))
            cst_prev_T[0] = tv
            wait('s', 'T', tv)
            emit('s', lambda eng, d=dst_cols_ap, s=pb[bk][0:128, 0:nrows]: eng.copy(d, s))
            sv = emit('s', lambda eng: eng.drain(), inc=('S', 1))
            release_bank(bk, ('S', sv))
            return sv

        sv_wnc = load_const(wn_h, 8, 128, wnc[:, 0:8])
        load_const(bdt_h, 16, 128, bdtc[:, 0:16])
        load_const(b1_h, 64, 128, b1c[:, 0:64])
        load_const(sb1_h, 32, 128, sb1c[:, 0:32])
        load_const(b2_h, 16, 128, b2c[:, 0:16])
        sv_sb2 = load_const(sb2_h, 8, 128, sb2c[:, 0:8])
        wait('v', 'S', sv_sb2)
        emit('v', lambda eng: eng.tensor_add(bfc[:], b2c[:, 0:8], b2c[:, 8:16]))
        emit('v', lambda eng: eng.drain())
        emit('v', lambda eng: eng.tensor_add(bfc[:], bfc[:], sb2c[:, 0:8]))
        vv_bfc = emit('v', lambda eng: eng.drain(), inc=('V', 1))

        wait('g', 'T', cst_prev_T[0])
        gv = emit('g', lambda eng: eng.dma_start(cst[0:2, 0:1024], gw_h), inc=('G', 16))
        wait('t', 'G', gv)
        for c in range(Dc):
            bk = take_banks(1)[0]
            tv = emit('t', lambda eng, o=pb[bk][0:128, 0:2], i_=cst[0:2, c * 128:(c + 1) * 128]:
                      eng.transpose(o, i_, ident[0:2, 0:2]), inc=('T', 1))
            wait('s', 'T', tv)
            emit('s', lambda eng, d=gwd[:, c:c + 1], s=pb[bk][0:128, 0:1]: eng.copy(d, s))
            sv = emit('s', lambda eng: eng.drain(), inc=('S', 1))
            wait('v', 'S', sv)
            emit('v', lambda eng, d=gwd[:, c:c + 1], b=pb[bk][0:128, 1:2]:
                 eng.tensor_sub(d, d, b))
            vv = emit('v', lambda eng: eng.drain(), inc=('V', 1))
            release_bank(bk, ('V', vv))
        bk = take_banks(1)[0]
        wait('t', 'V', vv)
        for c in range(Dc):
            tv = emit('t', lambda eng, o=pb[bk][0:1, 0:1], l=ones_colf[:, 0:1], r=gwd[:, c:c + 1],
                      st=(c == 0), sp=(c == Dc - 1): eng.matmul(o, l, r, start=st, stop=sp),
                      inc=('T', 1) if c == Dc - 1 else None)
        wait('s', 'T', tv)
        sv = emit('s', lambda eng, s=pb[bk][0:1, 0:1]: eng.copy(swd[:], s), inc=('S', 1))
        release_bank(bk, ('S', sv))

        if DBG:
            emit('v', lambda eng: eng.tensor_add(bfc2[:], b2c[:, 0:8], b2c[:, 8:16]))
            emit('v', lambda eng: eng.tensor_add(bfc2[:], bfc2[:], sb2c[:, 0:8]), inc=('V', 1))
            wait('g', 'V', N['V'])
            wait('g', 'S', N['S'])
            emit('g', lambda eng: eng.dma_start(dbgH[:, 0:8], bfc[:, 0:8]), inc=('G', 16))
            emit('g', lambda eng: eng.dma_start(dbgH[:, 32:40], bfc2[:, 0:8]), inc=('G', 16))
            emit('g', lambda eng: eng.dma_start(dbgH[:, 8:24], b2c[:, 0:16]), inc=('G', 16))
            emit('g', lambda eng: eng.dma_start(dbgH[:, 24:32], sb2c[:, 0:8]), inc=('G', 16))

        nat_tok = {}

        nat_last = [0]

        def nat_cast(dst_blob, src_ap, nrows, rowlen, key):
            wait('g', 'G', nat_last[0])
            toks = []
            for r in range(nrows // 128):
                slot = r % 2
                stg = pack[:, 1024 + slot * 2048: 1024 + slot * 2048 + rowlen]
                if len(toks) >= 2:
                    wait('g', 'G', toks[-2])
                gv = emit('g', lambda eng, d=stg, s=src_ap[r * 128:(r + 1) * 128, :]:
                          eng.dma_start(d, s), inc=('G', 16))
                wait('g', 'G', gv)
                gv2 = emit('g', lambda eng, d=dst_blob[r * 128:(r + 1) * 128, :], s=stg:
                           eng.dma_start(d, s), inc=('G', 16))
                toks.append(gv2)
            nat_tok[key] = toks
            nat_last[0] = toks[-1]

        blob_tok = {}
        pk_state = {"u": 0, "g": 0, "v": 0, "slot": 0, "hist": []}

        def pack_piece(dst_ap, src_ap, mode, fold_col=None):
            slot = pk_state["slot"]
            pk_state["slot"] ^= 1
            stg = pack[:, slot * 512:(slot + 1) * 512]
            hist = pk_state["hist"]
            prev = None
            for (sl, tok) in reversed(hist):
                if sl == slot:
                    prev = tok
                    break
            if mode == 'T':
                if prev is not None:
                    wait('y', 'G', prev)
                uv = emit('y', lambda eng, d=stg, s=src_ap: eng.dma_start(d, s, transpose=True),
                          inc=('W', 16))
                fill_tok = ('W', uv)
            else:
                if prev is not None:
                    wait('g', 'G', prev)
                gv = emit('g', lambda eng, d=stg, s=src_ap: eng.dma_start(d, s), inc=('G', 16))
                fill_tok = ('G', gv)
            if fold_col is not None:
                wait('v', 'S', sv_wnc)
                wait_tok('v', fill_tok)
                emit('v', lambda eng, d=stg, col=fold_col:
                     eng.tensor_scalar(d, d, col, None, OP.mult))
                vv = emit('v', lambda eng: eng.drain(), inc=('V', 1))
                fill_tok = ('V', vv)
            wait_tok('g', fill_tok)
            gv2 = emit('g', lambda eng, d=dst_ap, s=stg: eng.dma_start(d, s), inc=('G', 16))
            hist.append((slot, gv2))
            if len(hist) > 8:
                del hist[:4]
            return gv2

        def pack_attn(blob, nat, name, G, K, fold=False):
            outw = G * 512
            for g in range(G):
                last = None
                for k in range(K):
                    sl0 = (g * 512) // 128
                    for sl in range(sl0, sl0 + 4):
                        wait('y', 'G', nat_tok[name][sl])
                    src = nat[g * 512:(g + 1) * 512, k * 128:(k + 1) * 128]
                    dst = blob[g * 128:(g + 1) * 128, k * 512:(k + 1) * 512]
                    last = pack_piece(dst, src, 'T', fold_col=wnc[:, k:k + 1] if fold else None)
                blob_tok[(name, g)] = ('G', last)

        pbB = [p.bitcast(BF16) for p in pb]
        asm = pack[:, 3072:7168]
        a1_slot_T = [0, 0]
        asm_store = [0]
        for g in range(8):
            ev_toks = []
            for sl in range(4):
                r = g * 4 + sl
                slot = r % 2
                stg = pack[:, 1024 + slot * 1024: 1024 + slot * 1024 + 1024]
                wait('g', 'T', a1_slot_T[slot])
                gv = emit('g', lambda eng, d=stg, s=win_h[r * 128:(r + 1) * 128, :]:
                          eng.dma_start(d, s), inc=('G', 16))
                wait('t', 'G', gv)
                wait('t', 'P', pP0)
                for kh in range(2):
                    bk = take_banks(1)[0]
                    for kq in range(4):
                        k = kh * 4 + kq
                        tv = emit('t', lambda eng, o=pbB[bk][:, kq * 128:(kq + 1) * 128],
                                  i_=stg[:, k * 128:(k + 1) * 128]:
                                  eng.transpose(o, i_, identb[:]),
                                  inc=('T', 1) if kq == 3 else None)
                    a1_slot_T[slot] = tv
                    e = 's' if (r + kh) % 2 else 'v'
                    wait(e, 'T', tv)
                    wait(e, 'S', sv_wnc)
                    wait(e, 'G', asm_store[0])
                    for kq in range(4):
                        k = kh * 4 + kq
                        dst = asm[:, k * 512 + sl * 128: k * 512 + sl * 128 + 128]
                        srcp = pbB[bk][:, kq * 128:(kq + 1) * 128]
                        if e == 's':
                            emit('s', lambda eng, d=dst, s=srcp, sc=wnc[:, k:k + 1]:
                                 eng.activation(d, s, AF.Copy, bias=0.0, scale=sc))
                        else:
                            emit('v', lambda eng, d=dst, s=srcp, sc=wnc[:, k:k + 1]:
                                 eng.tensor_scalar(d, s, sc, None, OP.mult))
                    ev = emit(e, lambda eng: eng.drain(), inc=('S' if e == 's' else 'V', 1))
                    release_bank(bk, ('S' if e == 's' else 'V', ev))
                    ev_toks.append(('S' if e == 's' else 'V', ev))
            for tok in ev_toks:
                wait_tok('g', tok)
            gsv = emit('g', lambda eng, d=blobA1[g * 128:(g + 1) * 128, :], s=asm:
                       eng.dma_start(d, s), inc=('G', 16))
            asm_store[0] = gsv
            blob_tok[("in", g)] = ('G', gsv)
            nat_last[0] = gsv

        def APACK2():
            nat_cast(nat_dt, wdt_h, INNER, INNER, "dt")
            pack_attn(blobA2, nat_dt, "dt", 4, 16)

        def APACK3():
            nat_cast(nat_out, wout_h, D, INNER, "out")
            pack_attn(blobA3, nat_out, "out", 2, 16)

        FPACK = []

        def defer_fpack_w1like(blob, src_f32, name):
            for g in range(8):
                def mk(g=g, blob=blob, src=src_f32, name=name):
                    last = None
                    for k in range(8):
                        s = src[k * 128:(k + 1) * 128, g * 512:(g + 1) * 512]
                        d = blob[g * 128:(g + 1) * 128, k * 512:(k + 1) * 512]
                        last = pack_piece(d, s, 'C')
                    blob_tok[(name, g)] = ('G', last)
                FPACK.append(mk)

        def defer_fpack_w2like(blob, src_f32, name):
            for hg in range(2):
                for cg in range(2):
                    gi = hg * 2 + cg
                    def mk(hg=hg, cg=cg, gi=gi, blob=blob, src=src_f32, name=name):
                        last = None
                        for kl in range(16):
                            k = hg * 16 + kl
                            s = src[k * 128:(k + 1) * 128, cg * 512:(cg + 1) * 512]
                            d = blob[gi * 128:(gi + 1) * 128, kl * 512:(kl + 1) * 512]
                            last = pack_piece(d, s, 'C')
                        blob_tok[(name, gi)] = ('G', last)
                    FPACK.append(mk)

        defer_fpack_w1like(blobW1[0], w1_h[0:D, :], "w1_0")
        defer_fpack_w1like(blobW1[1], w1_h[D:2 * D, :], "w1_1")
        defer_fpack_w1like(blobSW1, sw1_h, "sw1")
        defer_fpack_w2like(blobW2[0], w2_h[0:HID, :], "w2_0")
        defer_fpack_w2like(blobW2[1], w2_h[HID:2 * HID, :], "w2_1")
        defer_fpack_w2like(blobSW2, sw2_h, "sw2")

        tok_tpose_done = {}
        tok_x1evac = {}
        tok_x1store = {}
        tok_ln1ops = {}
        tok_wdt_mm = {}
        tok_d1 = {}
        tok_pmul = {}
        tok_win_mm = {}
        tok_wout_mm = {}
        tok_uevac = {}
        tok_statT = {}

        for i in range(NTA):
            b, kt = divmod(i, NKT)
            tok0 = b * S + kt * TA
            pa = i % 2

            wait_tok('t', xload_tok[i])
            if i >= 2:
                wait_tok('t', tok_x1evac[i - 2])
            cp_last = None
            for sub in range(4):
                bks = take_banks(2)
                for c in range(Dc):
                    bk = bks[c // 4]
                    q = c % 4
                    tv = emit('t', lambda eng, o=pb[bk][:, q * 128:(q + 1) * 128],
                              i_=stage_x(sub, c * 128, 128): eng.transpose(o, i_, ident[:]),
                              inc=('T', 1) if c % 4 == 3 else None)
                    if c == 3:
                        tv_a = tv
                wait('s', 'T', tv_a)
                if i >= 2:
                    wait_tok('s', tok_x1evac[i - 2])
                for c in range(4):
                    emit('s', lambda eng, d=xr(pa, c, sub * 128, 128),
                         s=pb[bks[0]][:, (c % 4) * 128:(c % 4 + 1) * 128]: eng.copy(d, s))
                wait('s', 'T', tv)
                for c in range(4, 8):
                    emit('s', lambda eng, d=xr(pa, c, sub * 128, 128),
                         s=pb[bks[1]][:, (c % 4) * 128:(c % 4 + 1) * 128]: eng.copy(d, s))
                sv = emit('s', lambda eng: eng.drain(), inc=('S', 1))
                cp_last = sv
                release_bank(bks[0], ('S', sv))
                release_bank(bks[1], ('S', sv))
            tok_tpose_done[i] = ('T', tv)


            if i >= 1:
                wait_tok('s', tok_x1store[i - 1])
                wait_tok('s', tok_pmul[i - 1])
                wait_tok('s', tok_statT[i - 1])
            dump = AR4[:, 0:1024]
            for sub in range(4):
                wait('s', 'G', xload_sub[i][sub])
                emit('s', lambda eng, s=stage_x(sub), a=statq[:, 4 + sub:5 + sub]:
                     eng.activation(dump, s, AF.Square, accum_out=a))
                emit('s', lambda eng, s=stage_x(sub), a=statq[:, sub:sub + 1]:
                     eng.activation(dump, s, AF.Copy, bias=0.0, accum_out=a))
            stat_sv = emit('s', lambda eng: eng.drain(), inc=('S', 1))
            if i + 1 < NTA:
                wait_tok('g', tok_tpose_done[i])
                wait('g', 'S', stat_sv)
                emit_xload(i + 1)
            bsx = take_banks(1)[0]
            bsx2 = take_banks(1)[0]
            wait('t', 'S', stat_sv)
            for sub in range(4):
                emit('t', lambda eng, o=pb[bsx][0:1, sub * 128:(sub + 1) * 128],
                     i_=statq[:, sub:sub + 1]: eng.transpose(o, i_, ident[:]))
                tv2 = emit('t', lambda eng, o=pb[bsx2][0:1, sub * 128:(sub + 1) * 128],
                           i_=statq[:, 4 + sub:5 + sub]: eng.transpose(o, i_, ident[:]),
                           inc=('T', 1) if sub == 3 else None)
            tok_statT[i] = ('T', tv2)
            wait('s', 'T', tv2)
            emit('s', lambda eng, s=pb[bsx][0:1, :]: eng.mul(rowA("m"), s, 1.0 / D))
            emit('s', lambda eng, s=pb[bsx2][0:1, :]: eng.mul(rowA("sx2"), s, 1.0 / D))
            sv = emit('s', lambda eng: eng.drain(), inc=('S', 1))
            release_bank(bsx, ('S', sv))
            release_bank(bsx2, ('S', sv))
            wait('v', 'S', sv)
            emit('v', lambda eng: eng.tensor_mul(rowA("sx"), rowA("m"), rowA("m")))
            emit('v', lambda eng: eng.drain())
            emit('v', lambda eng: eng.tensor_sub(rowA("sx2"), rowA("sx2"), rowA("sx")))
            vv = emit('v', lambda eng: eng.drain(), inc=('V', 1))
            wait('s', 'V', vv)
            emit('s', lambda eng: eng.activation(rowA("rstd"), rowA("sx2"), AF.Ln, bias=EPS, scale=1.0))
            emit('s', lambda eng: eng.activation(rowA("sg"), rowA("sx2"), AF.Ln, bias=EPS * EPS,
                                                 scale=1.0 + EPS))
            emit('s', lambda eng: eng.activation(rowA("rstd"), rowA("rstd"), AF.Exp, bias=0.0, scale=-0.5))
            emit('s', lambda eng: eng.activation(rowA("sg"), rowA("sg"), AF.Exp, bias=0.0, scale=-0.5))
            sv = emit('s', lambda eng: eng.drain(), inc=('S', 1))
            wait('v', 'S', sv)
            emit('v', lambda eng: eng.tensor_mul(rowA("p4"), rowA("m"), rowA("sg")))
            emit('v', lambda eng: eng.tensor_mul(rowA("p2"), rowA("m"), rowA("rstd")))
            emit('v', lambda eng: eng.tensor_scalar(rowA("rstd"), rowA("rstd"), 1.0, None, OP.add))
            vv = emit('v', lambda eng: eng.drain(), inc=('V', 1))
            wait('t', 'V', vv)
            bks = take_banks(4)
            srcs = [rowA("rstd"), rowA("p2"), rowA("sg"), rowA("p4")]
            for bi, (bk, src) in enumerate(zip(bks, srcs)):
                tv = emit('t', lambda eng, o=pb[bk][:, 0:512], l=ones_row[0:1, :], r=src:
                          eng.matmul(o, l, r, start=True, stop=True),
                          inc=('T', 1) if bi == 3 else None)
            wait('s', 'T', tv)
            if i >= 1:
                wait_tok('s', tok_ln1ops[i - 1])
                wait_tok('s', tok_x1store[i - 1])
            for bi, bk in enumerate(bks):
                emit('s', lambda eng, d=bcB(bi), s=pb[bk][:, 0:512]: eng.copy(d, s))
            sv = emit('s', lambda eng: eng.drain(), inc=('S', 1))
            for bk in bks:
                release_bank(bk, ('S', sv))
            wait('v', 'S', sv)
            for c0 in range(0, Dc, 2):
                for c in (c0, c0 + 1):
                    emit('v', lambda eng, a=xr(pa, c), s=x2scr(c % 2): eng.tensor_mul(s, a, bcB(2)))
                for c in (c0, c0 + 1):
                    emit('v', lambda eng, c=c, s=x2scr(c % 2): eng.tensor_sub(gt(c), s, bcB(3)))
                for c in (c0, c0 + 1):
                    emit('v', lambda eng, a=xr(pa, c): eng.tensor_mul(a, a, bcB(0)))
                for c in (c0, c0 + 1):
                    emit('v', lambda eng, a=xr(pa, c): eng.tensor_sub(a, a, bcB(1)))
            vv = emit('v', lambda eng: eng.drain(), inc=('V', 1))
            tok_ln1ops[i] = ('V', vv)

            g_ready = ('V', vv)
            for g in range(8):
                col0, wv, sid = ring_load(blobA1[g * 128:(g + 1) * 128, :], 4096,
                                          ready_tok=blob_tok.get(("in", g)))
                bks = take_banks(4)
                wait('t', 'W', wv)
                wait_tok('t', g_ready)
                if i >= 1 and g == 0:
                    wait_tok('t', tok_uevac[i - 1])
                for c4 in range(4):
                    for k in range(8):
                        tv = emit('t', lambda eng, o=pb[bks[c4]][:, 0:512],
                                  l=ring[:, col0 + k * 512 + c4 * 128: col0 + k * 512 + c4 * 128 + 128],
                                  r=gt(k), st=(k == 0), sp=(k == 7):
                                  eng.matmul(o, l, r, start=st, stop=sp),
                                  inc=('T', 1) if (c4 == 3 and k == 7) else None)
                ring_free(sid, ('T', tv))
                wait('s', 'T', tv)
                if i >= 1 and g == 0:
                    wait_tok('s', tok_wdt_mm[i - 1])
                    wait_tok('s', tok_d1[i - 1])
                    wait_tok('s', tok_pmul[i - 1])
                for c4 in range(4):
                    oc = g * 4 + c4
                    dst = ut(oc) if oc < Ic else zt(oc - Ic)
                    emit('s', lambda eng, d=dst, s=pb[bks[c4]][:, 0:512]:
                         eng.activation(d, s, AF.Silu))
                sv = emit('s', lambda eng: eng.drain(), inc=('S', 1))
                for bk in bks:
                    release_bank(bk, ('S', sv))
                if g == 3:
                    u_done = ('S', sv)
            tok_uevac[i] = u_done
            z_done = ('S', sv)

            if i == 0:
                APACK2()

            last_scan_v = None
            for q in range(4):
                colof, wv, sids = ring_load_multi(blobA2[q * 128:(q + 1) * 128, :], 8192,
                                                  ready_tok=blob_tok.get(("dt", q)))
                bks = take_banks(4)
                wait('t', 'W', wv)
                wait_tok('t', u_done)
                for c4 in range(4):
                    for k in range(Ic):
                        tv = emit('t', lambda eng, o=pb[bks[c4]][:, 0:512],
                                  l=ring[:, colof(k) + c4 * 128: colof(k) + c4 * 128 + 128],
                                  r=ut(k), st=(k == 0), sp=(k == Ic - 1):
                                  eng.matmul(o, l, r, start=st, stop=sp),
                                  inc=('T', 1) if (c4 == 3 and k == Ic - 1) else None)
                for sid in sids:
                    ring_free(sid, ('T', tv))
                tok_wdt_mm[i] = ('T', tv)
                wait('s', 'T', tv)
                for c4 in range(4):
                    j = q * 4 + c4
                    emit('s', lambda eng, d=lam_g(c4), s=pb[bks[c4]][:, 0:512], bc=bdtc[:, j:j + 1]:
                         eng.activation(d, s, AF.Sigmoid, bias=bc, scale=1.0))
                sv = emit('s', lambda eng: eng.drain(), inc=('S', 1))
                for bk in bks:
                    release_bank(bk, ('S', sv))
                wait('v', 'S', sv)
                wait_tok('v', z_done)
                if i >= 1 and q == 0:
                    wait_tok('v', tok_x1store[i - 1])
                for c4 in range(4):
                    j = q * 4 + c4
                    emit('v', lambda eng, c4=c4, j=j:
                         eng.scalar_tensor_tensor(d1_g(c4), lam_g(c4), 1.0, ut(j), OP.subtract, OP.mult))
                for c4 in range(4):
                    j = q * 4 + c4
                    if kt == 0:
                        emit('v', lambda eng, c4=c4:
                             eng.tensor_tensor_scan(s_g(c4), lam_g(c4), d1_g(c4), 0.0, OP.mult, OP.subtract))
                    else:
                        emit('v', lambda eng, c4=c4, j=j:
                             eng.tensor_tensor_scan(s_g(c4), lam_g(c4), d1_g(c4), stst[:, j:j + 1],
                                                    OP.mult, OP.subtract))
                for c4 in range(4):
                    j = q * 4 + c4
                    emit('v', lambda eng, c4=c4, j=j:
                         eng.tensor_copy(stst[:, j:j + 1], s_g(c4, 511, 1)))
                for c4 in range(4):
                    j = q * 4 + c4
                    emit('v', lambda eng, c4=c4, j=j:
                         eng.tensor_mul(pt(j), s_g(c4), zt(j)))
                last_scan_v = emit('v', lambda eng: eng.drain(), inc=('V', 1))
            tok_d1[i] = ('V', last_scan_v)
            tok_pmul[i] = ('V', last_scan_v)
            p_done = ('V', last_scan_v)

            if i == 0:
                APACK3()

            st_tok = {}
            for g in range(2):
                colof, wv, sids = ring_load_multi(blobA3[g * 128:(g + 1) * 128, :], 8192,
                                                  ready_tok=blob_tok.get(("out", g)))
                bks = take_banks(4)
                wait('t', 'W', wv)
                wait_tok('t', p_done)
                for c4 in range(4):
                    for k in range(Ic):
                        tv = emit('t', lambda eng, o=pb[bks[c4]][:, 0:512],
                                  l=ring[:, colof(k) + c4 * 128: colof(k) + c4 * 128 + 128],
                                  r=pt(k), st=(k == 0), sp=(k == Ic - 1):
                                  eng.matmul(o, l, r, start=st, stop=sp),
                                  inc=('T', 1) if (c4 == 3 and k == Ic - 1) else None)
                for sid in sids:
                    ring_free(sid, ('T', tv))
                tok_wout_mm[i] = ('T', tv)
                wait('v', 'T', tv)
                for c4 in range(4):
                    c = g * 4 + c4
                    slot = c % 2
                    if c >= 2:
                        wait('v', 'G', st_tok[c - 2])
                    emit('v', lambda eng, slot=slot, s=pb[bks[c4]][:, 0:512], r=xr(pa, c):
                         eng.tensor_add(x1_st(slot), s, r))
                    vv = emit('v', lambda eng: eng.drain(), inc=('V', 1))
                    wait('g', 'V', vv)
                    gv = emit('g', lambda eng, slot=slot, c=c,
                              d=x1blob[c * 128:(c + 1) * 128, tok0:tok0 + TA]:
                              eng.dma_start(d, x1_st(slot)), inc=('G', 16))
                    st_tok[c] = gv
                sv_rel = ('V', vv)
                for bk in bks:
                    release_bank(bk, sv_rel)
            tok_x1evac[i] = ('V', vv)
            tok_x1store[i] = ('G', st_tok[7])

            if DBG and i == 0:
                wait('g', 'V', N['V'])
                wait('g', 'S', N['S'])
                emit('g', lambda eng: eng.dma_start(dbgA, AR2h[:, :]), inc=('G', 16))
                dbg_gv = emit('g', lambda eng: eng.dma_start(dbgB, AR3h[:, :]), inc=('G', 16))
                for e in 'tvs':
                    wait(e, 'G', dbg_gv)

            nslice = (len(FPACK) + NTA - 1 - i) // (NTA - i) if i < NTA - 1 else len(FPACK)
            for _ in range(min(nslice, len(FPACK))):
                FPACK.pop(0)()

        A_last_x1store = tok_x1store[NTA - 1]
        A_last_x1evac = tok_x1evac[NTA - 1]
        A_ln1 = tok_ln1ops[NTA - 1]
        A_pmul = tok_pmul[NTA - 1]
        A_wout = tok_wout_mm[NTA - 1]
        A_wdt = tok_wdt_mm[NTA - 1]
        A_d1 = tok_d1[NTA - 1]
        A_uevac = ('S', N['S'])
        A_tpose = tok_tpose_done[NTA - 1]

        tok_f_stt = {}
        tok_f_outst = {}
        tok_f_outcp = {}
        tok_f_m0rd = {}
        tok_f_g2rd = {}
        tok_f_hrd = {}
        tok_f_acc = {}

        for i in range(NTF):
            tok0 = i * TF

            if i == 0:
                wait_tok('g', A_last_x1evac)
                wait_tok('g', A_last_x1store)
            else:
                wait_tok('g', tok_f_stt[i - 1])
                wait_tok('g', tok_f_outst[i - 1])
            xg = {}
            for c in range(Dc):
                gv = emit('g', lambda eng, d=x1f(c), s=x1blob[c * 128:(c + 1) * 128, tok0:tok0 + TF]:
                          eng.dma_start(d, s), inc=('G', 16))
                xg[c] = gv
            x1_tok = ('G', gv)

            bsx = take_banks(2)
            bsx2 = take_banks(2)
            bgd = take_banks(2)
            wait('t', 'G', gv)
            if i == 0:
                wait_tok('s', A_last_x1store)
                wait_tok('s', A_pmul)
            else:
                wait_tok('s', tok_f_outcp[i - 1])
            sq_tok = {}
            bslot = lambda sl: AR4b[:, sl * 512:(sl + 1) * 512]
            for half in range(2):
                for c in range(Dc):
                    slot = (half * Dc + c) % 2
                    key = (half, c)
                    if (half * Dc + c) >= 2:
                        pk = (half, c - 2) if c >= 2 else (0, Dc - 2 + c)
                        wait('s', 'T', sq_tok[pk])
                    if i >= 1:
                        wait_tok('s', tok_f_acc[i - 1])
                    emit('s', lambda eng, d=bslot(slot * 2 + 1), s=x1f(c, half * 512, 512):
                         eng.activation(d, s, AF.Square))
                    ssv = emit('s', lambda eng: eng.drain(), inc=('S', 1))
                    tv = emit('t', lambda eng, o=pb[bsx[half]][0:1, :], l=ones_colf[:, 0:1],
                              r=x1f(c, half * 512, 512), st=(c == 0), sp=(c == Dc - 1):
                              eng.matmul(o, l, r, start=st, stop=sp),
                              inc=('T', 1) if c == Dc - 1 else None)
                    wait('t', 'S', ssv)
                    tv2 = emit('t', lambda eng, o=pb[bsx2[half]][0:1, :], l=ones_colb[:, 0:1],
                               r=bslot(slot * 2 + 1), st=(c == 0), sp=(c == Dc - 1):
                               eng.matmul(o, l, r, start=st, stop=sp), inc=('T', 1))
                    sq_tok[key] = tv2
                for c in range(Dc):
                    tg = emit('t', lambda eng, o=pb[bgd[half]][0:1, :], l=gwd[:, c:c + 1],
                              r=x1f(c, half * 512, 512), st=(c == 0), sp=(c == Dc - 1):
                              eng.matmul(o, l, r, start=st, stop=sp),
                              inc=('T', 1) if c == Dc - 1 else None)
            wait('s', 'T', tg)
            for half in range(2):
                emit('s', lambda eng, h=half, s=pb[bsx[half]][0:1, :]:
                     eng.mul(rowF("m2")[:, h * 512:(h + 1) * 512], s, 1.0 / D))
                emit('s', lambda eng, h=half, s=pb[bsx2[half]][0:1, :]:
                     eng.mul(rowF("sx2")[:, h * 512:(h + 1) * 512], s, 1.0 / D))
                emit('s', lambda eng, h=half, s=pb[bgd[half]][0:1, :]:
                     eng.copy(rowF("gd")[:, h * 512:(h + 1) * 512], s))
            sv = emit('s', lambda eng: eng.drain(), inc=('S', 1))
            for bk in bsx + bsx2 + bgd:
                release_bank(bk, ('S', sv))
            wait('v', 'S', sv)
            emit('v', lambda eng: eng.tensor_mul(rowF("sx"), rowF("m2"), rowF("m2")))
            emit('v', lambda eng: eng.drain())
            emit('v', lambda eng: eng.tensor_sub(rowF("sx2"), rowF("sx2"), rowF("sx")))
            vv = emit('v', lambda eng: eng.drain(), inc=('V', 1))
            wait('s', 'V', vv)
            emit('s', lambda eng: eng.activation(rowF("rstd2"), rowF("sx2"), AF.Ln, bias=EPS, scale=1.0))
            emit('s', lambda eng: eng.drain())
            emit('s', lambda eng: eng.activation(rowF("rstd2"), rowF("rstd2"), AF.Exp, bias=0.0,
                                                 scale=-0.5))
            sv = emit('s', lambda eng: eng.drain(), inc=('S', 1))
            wait('v', 'S', sv)
            emit('v', lambda eng: eng.tensor_scalar(rowF("sx"), rowF("m2"), swd[0:1, 0:1], None, OP.mult))
            emit('v', lambda eng: eng.tensor_mul(rowF("m2r2"), rowF("m2"), rowF("rstd2")))
            emit('v', lambda eng: eng.tensor_sub(rowF("gd"), rowF("gd"), rowF("sx")))
            emit('v', lambda eng: eng.drain())
            emit('v', lambda eng: eng.tensor_scalar(rowF("gd"), rowF("gd"), 0.0, None, OP.is_ge))
            vv = emit('v', lambda eng: eng.drain(), inc=('V', 1))
            wait('t', 'V', vv)
            bks = take_banks(6)
            for half in range(2):
                for bi, src in enumerate([rowF("rstd2"), rowF("m2r2"), rowF("gd")]):
                    bk = bks[bi * 2 + half]
                    tv = emit('t', lambda eng, o=pb[bk][:, 0:512], l=ones_row[0:1, :],
                              r=src[:, half * 512:(half + 1) * 512]:
                              eng.matmul(o, l, r, start=True, stop=True),
                              inc=('T', 1) if (half == 1 and bi == 2) else None)
            wait('s', 'T', tv)
            if i >= 1:
                wait_tok('s', tok_f_acc[i - 1])
            for half in range(2):
                for bi, dstf in enumerate([rstd2B, m2r2B, mask0B]):
                    bk = bks[bi * 2 + half]
                    emit('s', lambda eng, d=dstf(half * 512, 512), s=pb[bk][:, 0:512]:
                         eng.copy(d, s))
            sv = emit('s', lambda eng: eng.drain(), inc=('S', 1))
            for bk in bks:
                release_bank(bk, ('S', sv))
            wait('v', 'S', sv)
            if i == 0:
                wait_tok('v', A_wdt)
                wait_tok('v', A_d1)
                wait_tok('v', A_uevac)
                wait_tok('v', A_pmul)
            else:
                wait_tok('v', tok_f_g2rd[i - 1])
                wait_tok('v', tok_f_m0rd[i - 1])
            scrL = [AR4[:, 3584:4608], AR4[:, 4608:5632]]
            for c0 in range(0, Dc, 2):
                for c in (c0, c0 + 1):
                    emit('v', lambda eng, c=c, s=scrL[c % 2]: eng.tensor_mul(s, x1f(c), rstd2B()))
                for c in (c0, c0 + 1):
                    emit('v', lambda eng, c=c, s=scrL[c % 2]: eng.tensor_sub(g2(c), s, m2r2B()))
                for c in (c0, c0 + 1):
                    emit('v', lambda eng, c=c: eng.tensor_mul(m0(c), g2(c), mask0B()))
            vv = emit('v', lambda eng: eng.drain(), inc=('V', 1))
            m_ready = ('V', vv)

            if DBG and i == 0:
                wait('g', 'V', N['V'])
                dbg_gv = emit('g', lambda eng: eng.dma_start(dbgC, AR2h[:, :]), inc=('G', 16))
                for e in 'tvs':
                    wait(e, 'G', dbg_gv)

            def h_pass(blob_name, blob, rhs_fn, hg, pass_idx):
                relu_last = None
                if blob_name == "w1_0":
                    bias_col = lambda j: b1c[:, j:j + 1]
                elif blob_name == "w1_1":
                    bias_col = lambda j: b1c[:, 32 + j:33 + j]
                else:
                    bias_col = lambda j: sb1c[:, j:j + 1]
                for gl in range(4):
                    g = hg * 4 + gl
                    col0, wv, sid = ring_load(blob[g * 128:(g + 1) * 128, :], 4096,
                                              ready_tok=blob_tok.get((blob_name, g)))
                    for sub in range(2):
                        bks = take_banks(4)
                        wait('t', 'W', wv)
                        wait_tok('t', m_ready)
                        if pass_idx == 0 and hg == 0 and gl == 0 and sub == 0:
                            if i == 0:
                                wait_tok('t', A_wout)
                                wait_tok('s', A_d1)
                            else:
                                wait_tok('t', tok_f_hrd[i - 1])
                        for c2 in range(2):
                            for half in range(2):
                                bk = bks[c2 * 2 + half]
                                for k in range(Dc):
                                    tv = emit('t', lambda eng, o=pb[bk][:, 0:512],
                                              l=ring[:, col0 + k * 512 + (sub * 2 + c2) * 128:
                                                     col0 + k * 512 + (sub * 2 + c2) * 128 + 128],
                                              r=rhs_fn(k, half), st=(k == 0), sp=(k == Dc - 1):
                                              eng.matmul(o, l, r, start=st, stop=sp),
                                              inc=('T', 1) if (c2 == 1 and half == 1 and k == Dc - 1) else None)
                        if sub == 1:
                            ring_free(sid, ('T', tv))
                        wait('s', 'T', tv)
                        for c2 in range(2):
                            j = g * 4 + sub * 2 + c2
                            jj = j - hg * 16
                            for half in range(2):
                                bk = bks[c2 * 2 + half]
                                emit('s', lambda eng, d=ht(jj, half * 512, 512), s=pb[bk][:, 0:512],
                                     bc=bias_col(j):
                                     eng.activation(d, s, AF.Relu, bias=bc, scale=1.0))
                        sv = emit('s', lambda eng: eng.drain(), inc=('S', 1))
                        for bk in bks:
                            release_bank(bk, ('S', sv))
                        relu_last = sv
                return ('S', relu_last)

            def w2_pass(blob_name, blob, hg, first):
                h_ready = None
                for cg in range(2):
                    gi = hg * 2 + cg
                    colof, wv, sids = ring_load_multi(blob[gi * 128:(gi + 1) * 128, :], 8192,
                                                      ready_tok=blob_tok.get((blob_name, gi)))
                    for sub in range(2):
                        bks = take_banks(4)
                        wait('t', 'W', wv)
                        for c2 in range(2):
                            for half in range(2):
                                bk = bks[c2 * 2 + half]
                                for kl in range(16):
                                    tv = emit('t', lambda eng, o=pb[bk][:, 0:512],
                                              l=ring[:, colof(kl) + (sub * 2 + c2) * 128:
                                                     colof(kl) + (sub * 2 + c2) * 128 + 128],
                                              r=ht(kl, half * 512, 512), st=(kl == 0), sp=(kl == 15):
                                              eng.matmul(o, l, r, start=st, stop=sp),
                                              inc=('T', 1) if (c2 == 1 and half == 1 and kl == 15) else None)
                        if sub == 1:
                            for sid in sids:
                                ring_free(sid, ('T', tv))
                        wait('v', 'T', tv)
                        for c2 in range(2):
                            c = cg * 4 + sub * 2 + c2
                            for half in range(2):
                                bk = bks[c2 * 2 + half]
                                if first:
                                    vv = emit('v', lambda eng, c=c, half=half, bk=bk:
                                              eng.tensor_copy(out_acc(c, half * 512, 512),
                                                              pb[bk][:, 0:512]),
                                              inc=('V', 1) if (c2 == 1 and half == 1) else None)
                                else:
                                    vv = emit('v', lambda eng, c=c, half=half, bk=bk:
                                              eng.tensor_add(out_acc(c, half * 512, 512),
                                                             out_acc(c, half * 512, 512),
                                                             pb[bk][:, 0:512]),
                                              inc=('V', 1) if (c2 == 1 and half == 1) else None)
                        for bk in bks:
                            release_bank(bk, ('V', vv))
                return ('V', vv)

            first_acc = True
            acc_tok = None
            for (nm1, bl1, rhs1, nm2, bl2) in [("w1_0", blobW1[0], lambda k, h: m0(k, h * 512, 512), "w2_0", blobW2[0]),
                                               ("w1_1", blobW1[1], lambda k, h: m0(k, h * 512, 512), "w2_1", blobW2[1]),
                                               ("sw1", blobSW1, lambda k, h: g2(k, h * 512, 512), "sw2", blobSW2)]:
                if nm1 == "w1_1":
                    wait_tok('v', tok_f_m0rd[i])
                    for c in range(Dc):
                        emit('v', lambda eng, c=c: eng.tensor_sub(m0(c), g2(c), m0(c)))
                    mv = emit('v', lambda eng: eng.drain(), inc=('V', 1))
                    m_ready = ('V', mv)
                for hg in range(2):
                    hr = h_pass(nm1, bl1, rhs1, hg, 0 if first_acc else 1)
                    wait_tok('t', hr)
                    acc_tok = w2_pass(nm2, bl2, hg, first_acc and hg == 0)
                    if hg == 0 and first_acc:
                        first_acc = False
                    tok_f_hrd[i] = ('T', N['T'])
                if nm1 == "w1_0":
                    tok_f_m0rd[i] = ('T', N['T'])
                if nm1 == "w1_1":
                    tok_f_m0rd[i] = ('T', N['T'])
                if nm1 == "sw1":
                    tok_f_g2rd[i] = ('T', N['T'])

            if DBG and i == 0:
                wait_tok('g', acc_tok)
                emit('g', lambda eng: eng.dma_start(dbgD, AR4[:, :]), inc=('G', 16))
                emit('g', lambda eng: eng.dma_start(dbgF, AR1[:, :]), inc=('G', 16))
                dbg_gv = emit('g', lambda eng: eng.dma_start(dbgG, bfc[:, 0:8]), inc=('G', 16))
                for e in 'tvs':
                    wait(e, 'G', dbg_gv)

            wait_tok('v', acc_tok)
            for c in range(Dc):
                emit('v', lambda eng, c=c:
                     eng.scalar_tensor_tensor(out_acc(c), out_acc(c), bfc[:, c:c + 1], x1f(c),
                                              OP.add, OP.add))
            fv = emit('v', lambda eng: eng.drain(), inc=('V', 1))
            tok_f_stt[i] = ('V', fv)

            if DBG and i == 0:
                wait('g', 'V', N['V'])
                dbg_gv = emit('g', lambda eng: eng.dma_start(dbgE, AR4[:, :]), inc=('G', 16))
                for e in 'tvs':
                    wait(e, 'G', dbg_gv)

            out_last_g = None
            for sub in range(8):
                slot = sub % 2
                bks = take_banks(2)
                wait('t', 'V', fv)
                for c in range(Dc):
                    bk = bks[c // 4]
                    qq = c % 4
                    tv = emit('t', lambda eng, o=pb[bk][:, qq * 128:(qq + 1) * 128],
                              i_=out_acc(c, sub * 128, 128): eng.transpose(o, i_, ident[:]),
                              inc=('T', 1) if qq == 3 else None)
                    if c == 3:
                        tv_a = tv
                wait('s', 'T', tv)
                if i >= 1 and sub < 2:
                    wait_tok('s', tok_f_outst[i - 1])
                if sub >= 2:
                    wait('s', 'G', outst_tok[sub - 2])
                elif i == 0 and sub < 2:
                    wait_tok('s', A_last_x1evac)
                if sub == 0:
                    outst_tok = {}
                for c in range(Dc):
                    bk = bks[c // 4]
                    emit('s', lambda eng, d=out_tm(slot)[:, c * 128:(c + 1) * 128],
                         s=pb[bk][:, (c % 4) * 128:(c % 4 + 1) * 128]: eng.copy(d, s))
                sv = emit('s', lambda eng: eng.drain(), inc=('S', 1))
                for bk in bks:
                    release_bank(bk, ('S', sv))
                tok_f_outcp[i] = ('S', sv)
                wait('g', 'S', sv)
                gv = emit('g', lambda eng, d=out_h[tok0 + sub * 128: tok0 + (sub + 1) * 128, :],
                          s=out_tm(slot): eng.dma_start(d, s), inc=('G', 16))
                outst_tok[sub] = gv
                out_last_g = gv
            tok_f_outst[i] = ('G', out_last_g)
            tok_f_acc[i] = ('T', N['T'])

        while FPACK:
            FPACK.pop(0)()

        wait('g', 'G', N['G'])

        block = E(nc.Block())

        @block.tensor
        def _(eng):
            for f in OPS['t']:
                f(eng)

        @block.vector
        def _(eng):
            for f in OPS['v']:
                f(eng)

        @block.scalar
        def _(eng):
            for f in OPS['s']:
                f(eng)

        @block.gpsimd
        def _(eng):
            for f in OPS['g']:
                f(eng)

        @block.sync
        def _(eng):
            for f in OPS['y']:
                f(eng)

    return nc


_NC_CACHE = [None]


def kernel(**inputs):
    xs = np.ascontiguousarray(inputs["L0_attn_x"], dtype=np.float32)
    ins_common = {
        "w_norm": np.ascontiguousarray(inputs["L0_attn_w_norm"], np.float32).reshape(8, 128),
        "W_in": np.ascontiguousarray(inputs["L0_attn_W_in"], np.float32),
        "W_dt": np.ascontiguousarray(inputs["L0_attn_W_dt"], np.float32),
        "b_dt": np.ascontiguousarray(inputs["L0_attn_b_dt"], np.float32).reshape(16, 128),
        "W_out": np.ascontiguousarray(inputs["L0_attn_W_out"], np.float32),
        "gate_w": np.ascontiguousarray(inputs["L0_ffn_gate_w"], np.float32),
        "w1": np.ascontiguousarray(inputs["L0_ffn_w1"], np.float32).reshape(2 * D, HID),
        "b1": np.ascontiguousarray(inputs["L0_ffn_b1"], np.float32).reshape(64, 128),
        "w2": np.ascontiguousarray(inputs["L0_ffn_w2"], np.float32).reshape(2 * HID, D),
        "b2": np.ascontiguousarray(inputs["L0_ffn_b2"], np.float32).reshape(16, 128),
        "sw1": np.ascontiguousarray(inputs["L0_ffn_sw1"], np.float32),
        "sb1": np.ascontiguousarray(inputs["L0_ffn_sb1"], np.float32).reshape(32, 128),
        "sw2": np.ascontiguousarray(inputs["L0_ffn_sw2"], np.float32),
        "sb2": np.ascontiguousarray(inputs["L0_ffn_sb2"], np.float32).reshape(8, 128),
    }
    in_maps = []
    for core in range(NCORES):
        m = dict(ins_common)
        m["x"] = np.ascontiguousarray(xs[core * NB:(core + 1) * NB].reshape(NTOK, D))
        in_maps.append(m)
    if _NC_CACHE[0] is None:
        _NC_CACHE[0] = build_nc()
    nc = _NC_CACHE[0]
    res = run_bass_kernel_spmd(nc, in_maps, core_ids=list(range(NCORES)))
    out = np.concatenate([r["out"].reshape(NB, S, D) for r in res.results], axis=0)
    return out


# version 09
# speedup vs baseline: 1.2179x; 1.0293x over previous
import contextlib
import numpy as np
import concourse.bass as bass
import concourse.mybir as mybir
from concourse.bass_utils import run_bass_kernel_spmd

F32 = mybir.dt.float32
BF16 = mybir.dt.bfloat16
AF = mybir.ActivationFunctionType
OP = mybir.AluOpType

NCORES = 8
D, Dc = 1024, 8
INNER, Ic = 2048, 16
HID, Hc = 4096, 32
NB, S = 4, 2048
NTOK = NB * S
TA, NKT = 512, 4
NTA = NB * NKT
TF, NTF = 1024, 8
EPS = 1e-5

RF = 16384
DBG = False


def build_nc():
    nc = bass.Bass("TRN2", target_bir_lowering=False)

    x_h = nc.dram_tensor("x", [NTOK, D], F32, kind="ExternalInput").ap()
    wn_h = nc.dram_tensor("w_norm", [8, 128], F32, kind="ExternalInput").ap()
    win_h = nc.dram_tensor("W_in", [2 * INNER, D], F32, kind="ExternalInput").ap()
    wdt_h = nc.dram_tensor("W_dt", [INNER, INNER], F32, kind="ExternalInput").ap()
    bdt_h = nc.dram_tensor("b_dt", [16, 128], F32, kind="ExternalInput").ap()
    wout_h = nc.dram_tensor("W_out", [D, INNER], F32, kind="ExternalInput").ap()
    gw_h = nc.dram_tensor("gate_w", [2, D], F32, kind="ExternalInput").ap()
    w1_h = nc.dram_tensor("w1", [2 * D, HID], F32, kind="ExternalInput").ap()
    b1_h = nc.dram_tensor("b1", [64, 128], F32, kind="ExternalInput").ap()
    w2_h = nc.dram_tensor("w2", [2 * HID, D], F32, kind="ExternalInput").ap()
    b2_h = nc.dram_tensor("b2", [16, 128], F32, kind="ExternalInput").ap()
    sw1_h = nc.dram_tensor("sw1", [D, HID], F32, kind="ExternalInput").ap()
    sb1_h = nc.dram_tensor("sb1", [32, 128], F32, kind="ExternalInput").ap()
    sw2_h = nc.dram_tensor("sw2", [HID, D], F32, kind="ExternalInput").ap()
    sb2_h = nc.dram_tensor("sb2", [8, 128], F32, kind="ExternalInput").ap()
    out_h = nc.dram_tensor("out", [NTOK, D], F32, kind="ExternalOutput").ap()

    nat_in = nc.dram_tensor("nat_in", [2 * INNER, D], BF16, kind="Internal").ap()
    nat_dt = nc.dram_tensor("nat_dt", [INNER, INNER], BF16, kind="Internal").ap()
    nat_out = nc.dram_tensor("nat_out", [D, INNER], BF16, kind="Internal").ap()
    blobA1 = nc.dram_tensor("blobA1", [8 * 128, 8 * 512], BF16, kind="Internal").ap()
    blobA2 = nc.dram_tensor("blobA2", [4 * 128, 16 * 512], BF16, kind="Internal").ap()
    blobA3 = nc.dram_tensor("blobA3", [2 * 128, 16 * 512], BF16, kind="Internal").ap()
    blobW1 = [nc.dram_tensor(f"blobW1_{e}", [8 * 128, 8 * 512], BF16, kind="Internal").ap() for e in range(2)]
    blobSW1 = nc.dram_tensor("blobSW1", [8 * 128, 8 * 512], BF16, kind="Internal").ap()
    blobW2 = [nc.dram_tensor(f"blobW2_{e}", [4 * 128, 16 * 512], BF16, kind="Internal").ap() for e in range(2)]
    blobSW2 = nc.dram_tensor("blobSW2", [4 * 128, 16 * 512], BF16, kind="Internal").ap()
    x1blob = nc.dram_tensor("x1blob", [Dc * 128, NTOK], F32,
                            kind="ExternalOutput" if DBG else "Internal").ap()
    if DBG:
        dbgA = nc.dram_tensor("dbgA", [128, 16384], BF16, kind="ExternalOutput").ap()
        dbgB = nc.dram_tensor("dbgB", [128, 16384], BF16, kind="ExternalOutput").ap()
        dbgC = nc.dram_tensor("dbgC", [128, 16384], BF16, kind="ExternalOutput").ap()
        dbgD = nc.dram_tensor("dbgD", [128, 8192], F32, kind="ExternalOutput").ap()
        dbgE = nc.dram_tensor("dbgE", [128, 8192], F32, kind="ExternalOutput").ap()
        dbgF = nc.dram_tensor("dbgF", [128, 8192], F32, kind="ExternalOutput").ap()
        dbgG = nc.dram_tensor("dbgG", [128, 8], F32, kind="ExternalOutput").ap()
        dbgH = nc.dram_tensor("dbgH", [128, 40], F32, kind="ExternalOutput").ap()

    with contextlib.ExitStack() as es:
        E = es.enter_context
        AR1 = E(nc.sbuf_tensor("AR1", [128, 8192], F32))
        AR2h = E(nc.sbuf_tensor("AR2", [128, 16384], BF16))
        AR3h = E(nc.sbuf_tensor("AR3", [128, 16384], BF16))
        AR4 = E(nc.sbuf_tensor("AR4", [128, 8192], F32))
        ring = E(nc.sbuf_tensor("ringt", [128, RF], BF16))
        pack = E(nc.sbuf_tensor("pack", [128, 7168], BF16))
        statq = E(nc.sbuf_tensor("statq", [128, 8], F32))
        ident = E(nc.sbuf_tensor("ident", [128, 128], F32))
        ones_row = E(nc.sbuf_tensor("ones_row", [1, 128], F32))
        ones_colf = E(nc.sbuf_tensor("ones_colf", [128, 1], F32))
        ones_colb = E(nc.sbuf_tensor("ones_colb", [128, 1], BF16))
        identb = E(nc.sbuf_tensor("identb", [128, 128], BF16))
        rows = E(nc.sbuf_tensor("rows", [1, 6144], F32))
        cst = E(nc.sbuf_tensor("cst", [64, 1024], F32))
        wnc = E(nc.sbuf_tensor("wnc", [128, 8], F32))
        bdtc = E(nc.sbuf_tensor("bdtc", [128, 16], F32))
        b1c = E(nc.sbuf_tensor("b1c", [128, 64], F32))
        sb1c = E(nc.sbuf_tensor("sb1c", [128, 32], F32))
        b2c = E(nc.sbuf_tensor("b2c", [128, 16], F32))
        sb2c = E(nc.sbuf_tensor("sb2c", [128, 8], F32))
        bfc = E(nc.sbuf_tensor("bfc", [128, 8], F32))
        gwd = E(nc.sbuf_tensor("gwd", [128, 8], F32))
        swd = E(nc.sbuf_tensor("swd", [1, 1], F32))
        stst = E(nc.sbuf_tensor("stst", [128, 16], F32))
        ceps = E(nc.sbuf_tensor("ceps", [128, 2], F32))
        bfc2 = E(nc.sbuf_tensor("bfc2", [128, 8], F32)) if DBG else None
        pb = [E(nc.psum_tensor(f"pb{i}", [128, 512], F32)) for i in range(8)]

        sem = {k: E(nc.semaphore(f"sem{k}")) for k in "TVSGWUP"}

        AR4b = AR4.bitcast(BF16)

        OPS = {k: [] for k in "tvsgy"}
        N = {k: 0 for k in "TVSGWUP"}
        WAITED = {e: {} for e in "tvsgy"}

        def emit(e, fn, inc=None):
            v = None
            if inc is not None:
                key, amt = inc
                N[key] += amt
                v = N[key]
                h = sem[key]
                def thunk(eng, fn=fn, h=h, amt=amt):
                    fn(eng).then_inc(h, amt)
            else:
                def thunk(eng, fn=fn):
                    fn(eng)
            OPS[e].append(thunk)
            return v

        def wait(e, key, val):
            if val is None or val <= 0:
                return
            if val <= WAITED[e].get(key, 0):
                return
            WAITED[e][key] = val
            h = sem[key]
            OPS[e].append(lambda eng, h=h, val=val: eng.wait_ge(h, val))

        def wait_tok(e, tok):
            if tok is not None:
                wait(e, tok[0], tok[1])

        bank_tok = [None] * 8
        bank_rr = [0]

        def take_banks(n):
            ids = []
            for _ in range(n):
                i = bank_rr[0] % 8
                bank_rr[0] += 1
                wait_tok('t', bank_tok[i])
                bank_tok[i] = None
                ids.append(i)
            return ids

        def release_bank(i, tok):
            bank_tok[i] = tok

        ring_state = {"head": 0, "spans": []}
        span_autoid = [0]
        span_free = {}

        def ring_load(src_ap, ncols, ready_tok=None):
            st = ring_state
            head = st["head"]
            if head % 4096:
                head += 4096 - head % 4096
            if head + ncols > RF:
                head = 0
            keep = []
            for (s0, e0, sid) in st["spans"]:
                if s0 < head + ncols and e0 > head:
                    tok = span_free.get(sid)
                    assert tok is not None, "ring overwrite before consumer registered"
                    wait_tok('y', tok)
                else:
                    keep.append((s0, e0, sid))
            st["spans"] = keep
            sid = span_autoid[0]
            span_autoid[0] += 1
            st["spans"].append((head, head + ncols, sid))
            st["head"] = head + ncols
            if ready_tok is not None:
                wait_tok('y', ready_tok)
            dst = ring[:, head:head + ncols]
            wv = emit('y', lambda eng, dst=dst, src=src_ap: eng.dma_start(dst, src), inc=('W', 16))
            return head, wv, sid

        def ring_free(sid, tok):
            span_free[sid] = tok

        def ring_load_multi(src_ap, total_cols, ready_tok=None):
            chunks = []
            wv = None
            for off in range(0, total_cols, 4096):
                c0, wv, sid = ring_load(src_ap[:, off:off + 4096], 4096, ready_tok=ready_tok)
                chunks.append((c0, sid))
            def colof(k, chunks=tuple(c for c, _ in chunks)):
                return chunks[(k * 512) // 4096] + (k * 512) % 4096
            return colof, wv, [s for _, s in chunks]

        def xr(pa, c, lo=0, n=512):
            return AR1[:, pa * 4096 + c * 512 + lo: pa * 4096 + c * 512 + lo + n]

        def x1f(c, lo=0, n=1024):
            return AR1[:, c * 1024 + lo: c * 1024 + lo + n]

        def out_tm(slot):
            return AR1[:, slot * 1024:(slot + 1) * 1024]

        def ut(j, lo=0, n=512):
            return AR2h[:, j * 512 + lo: j * 512 + lo + n]

        def zt(j, lo=0, n=512):
            return AR2h[:, 8192 + j * 512 + lo: 8192 + j * 512 + lo + n]

        def g2(c, lo=0, n=1024):
            return AR2h[:, c * 1024 + lo: c * 1024 + lo + n]

        def m0(c, lo=0, n=1024):
            return AR2h[:, 8192 + c * 1024 + lo: 8192 + c * 1024 + lo + n]

        def pt(j, lo=0, n=512):
            return AR3h[:, j * 512 + lo: j * 512 + lo + n]

        def gt(c, lo=0, n=512):
            return AR3h[:, 8192 + c * 512 + lo: 8192 + c * 512 + lo + n]

        def lam_g(q, n=512):
            return AR3h[:, 12288 + q * 512: 12288 + q * 512 + n]

        def d1_g(q, n=512):
            return AR3h[:, 14336 + q * 512: 14336 + q * 512 + n]

        def ht(j, lo=0, n=1024):
            return AR3h[:, j * 1024 + lo: j * 1024 + lo + n]

        def s_g(q, lo=0, n=512):
            return AR4[:, q * 512 + lo: q * 512 + lo + n]

        def x2scr(slot, n=512):
            return AR4[:, slot * 512: slot * 512 + n]

        def x1_st(slot):
            return AR4[:, slot * 512:(slot + 1) * 512]

        def bcB(i, n=512):
            return AR4[:, 2048 + i * 512: 2048 + i * 512 + n]

        def stage_x(sub, lo=0, n=1024):
            return AR4[:, 4096 + sub * 1024 + lo: 4096 + sub * 1024 + lo + n]

        def out_acc(c, lo=0, n=1024):
            return AR4[:, c * 1024 + lo: c * 1024 + lo + n]

        def rstd2B(lo=0, n=1024):
            return AR4[:, 1024 + lo: 1024 + lo + n]

        def m2r2B(lo=0, n=1024):
            return AR4[:, 2048 + lo: 2048 + lo + n]

        def mask0B(lo=0, n=1024):
            return AR4b[:, 6144 + lo: 6144 + lo + n]

        ROWA = {"sx": 0, "sx2": 512, "m": 1024, "rstd": 1536, "sg": 2048, "p4": 2560, "p2": 3072}
        ROWF = {"sx": 0, "sx2": 1024, "m2": 2048, "rstd2": 3072, "m2r2": 4096, "gd": 5120}

        def rowA(nm, n=512):
            return rows[:, ROWA[nm]:ROWA[nm] + n]

        def rowF(nm, n=1024):
            return rows[:, ROWF[nm]:ROWF[nm] + n]

        nc.const_aps.register(F32, EPS, ceps[:, 0:1])
        nc.const_aps.register(F32, EPS * EPS, ceps[:, 1:2])
        emit('g', lambda eng: eng.iota(ident[:], [[1, 128]], base=0, channel_multiplier=-1,
                                       allow_small_or_imprecise_dtypes=True))
        emit('g', lambda eng: eng.memset(ceps[:, 0:1], EPS))
        emit('g', lambda eng: eng.memset(ceps[:, 1:2], EPS * EPS))
        emit('g', lambda eng: eng.memset(ones_row[:], 1.0))
        emit('g', lambda eng: eng.tensor_scalar(ident[:], ident[:], 0.0, None, OP.is_equal))
        emit('g', lambda eng: eng.memset(ones_colb[:], 1.0))
        emit('g', lambda eng: eng.tensor_copy(identb[:], ident[:]))
        pP0 = emit('g', lambda eng: eng.memset(ones_colf[:], 1.0), inc=('P', 1))

        xload_tok = {}
        xload_sub = {}

        def emit_xload(i):
            b, kt = divmod(i, NKT)
            tok0 = b * S + kt * TA
            subs = []
            for sub in range(4):
                src = x_h[tok0 + sub * 128: tok0 + (sub + 1) * 128, :]
                v = emit('g', lambda eng, d=stage_x(sub), s=src: eng.dma_start(d, s), inc=('G', 16))
                subs.append(v)
            xload_tok[i] = ('G', v)
            xload_sub[i] = subs

        emit_xload(0)

        cst_prev_T = [0]

        def load_const(src_ap, nrows, ncols, dst_cols_ap):
            wait('g', 'T', cst_prev_T[0])
            gv = emit('g', lambda eng, d=cst[0:nrows, 0:128], s=src_ap: eng.dma_start(d, s), inc=('G', 16))
            wait('t', 'G', gv)
            wait('t', 'P', pP0)
            bk = take_banks(1)[0]
            tv = emit('t', lambda eng, o=pb[bk][0:128, 0:nrows], i_=cst[0:nrows, 0:128],
                      idn=ident[0:nrows, 0:nrows]:
                      eng.transpose(o, i_, idn), inc=('T', 1))
            cst_prev_T[0] = tv
            wait('s', 'T', tv)
            emit('s', lambda eng, d=dst_cols_ap, s=pb[bk][0:128, 0:nrows]: eng.copy(d, s))
            sv = emit('s', lambda eng: eng.drain(), inc=('S', 1))
            release_bank(bk, ('S', sv))
            return sv

        sv_wnc = load_const(wn_h, 8, 128, wnc[:, 0:8])
        load_const(bdt_h, 16, 128, bdtc[:, 0:16])
        load_const(b1_h, 64, 128, b1c[:, 0:64])
        load_const(sb1_h, 32, 128, sb1c[:, 0:32])
        load_const(b2_h, 16, 128, b2c[:, 0:16])
        sv_sb2 = load_const(sb2_h, 8, 128, sb2c[:, 0:8])
        wait('v', 'S', sv_sb2)
        emit('v', lambda eng: eng.tensor_add(bfc[:], b2c[:, 0:8], b2c[:, 8:16]))
        emit('v', lambda eng: eng.drain())
        emit('v', lambda eng: eng.tensor_add(bfc[:], bfc[:], sb2c[:, 0:8]))
        vv_bfc = emit('v', lambda eng: eng.drain(), inc=('V', 1))

        wait('g', 'T', cst_prev_T[0])
        gv = emit('g', lambda eng: eng.dma_start(cst[0:2, 0:1024], gw_h), inc=('G', 16))
        wait('t', 'G', gv)
        for c in range(Dc):
            bk = take_banks(1)[0]
            tv = emit('t', lambda eng, o=pb[bk][0:128, 0:2], i_=cst[0:2, c * 128:(c + 1) * 128]:
                      eng.transpose(o, i_, ident[0:2, 0:2]), inc=('T', 1))
            wait('s', 'T', tv)
            emit('s', lambda eng, d=gwd[:, c:c + 1], s=pb[bk][0:128, 0:1]: eng.copy(d, s))
            sv = emit('s', lambda eng: eng.drain(), inc=('S', 1))
            wait('v', 'S', sv)
            emit('v', lambda eng, d=gwd[:, c:c + 1], b=pb[bk][0:128, 1:2]:
                 eng.tensor_sub(d, d, b))
            vv = emit('v', lambda eng: eng.drain(), inc=('V', 1))
            release_bank(bk, ('V', vv))
        bk = take_banks(1)[0]
        wait('t', 'V', vv)
        for c in range(Dc):
            tv = emit('t', lambda eng, o=pb[bk][0:1, 0:1], l=ones_colf[:, 0:1], r=gwd[:, c:c + 1],
                      st=(c == 0), sp=(c == Dc - 1): eng.matmul(o, l, r, start=st, stop=sp),
                      inc=('T', 1) if c == Dc - 1 else None)
        wait('s', 'T', tv)
        sv = emit('s', lambda eng, s=pb[bk][0:1, 0:1]: eng.copy(swd[:], s), inc=('S', 1))
        release_bank(bk, ('S', sv))

        if DBG:
            emit('v', lambda eng: eng.tensor_add(bfc2[:], b2c[:, 0:8], b2c[:, 8:16]))
            emit('v', lambda eng: eng.tensor_add(bfc2[:], bfc2[:], sb2c[:, 0:8]), inc=('V', 1))
            wait('g', 'V', N['V'])
            wait('g', 'S', N['S'])
            emit('g', lambda eng: eng.dma_start(dbgH[:, 0:8], bfc[:, 0:8]), inc=('G', 16))
            emit('g', lambda eng: eng.dma_start(dbgH[:, 32:40], bfc2[:, 0:8]), inc=('G', 16))
            emit('g', lambda eng: eng.dma_start(dbgH[:, 8:24], b2c[:, 0:16]), inc=('G', 16))
            emit('g', lambda eng: eng.dma_start(dbgH[:, 24:32], sb2c[:, 0:8]), inc=('G', 16))

        nat_tok = {}

        nat_last = [0]

        def nat_cast(dst_blob, src_ap, nrows, rowlen, key):
            wait('g', 'G', nat_last[0])
            toks = []
            for r in range(nrows // 128):
                slot = r % 2
                stg = pack[:, 1024 + slot * 2048: 1024 + slot * 2048 + rowlen]
                if len(toks) >= 2:
                    wait('g', 'G', toks[-2])
                gv = emit('g', lambda eng, d=stg, s=src_ap[r * 128:(r + 1) * 128, :]:
                          eng.dma_start(d, s), inc=('G', 16))
                wait('g', 'G', gv)
                gv2 = emit('g', lambda eng, d=dst_blob[r * 128:(r + 1) * 128, :], s=stg:
                           eng.dma_start(d, s), inc=('G', 16))
                toks.append(gv2)
            nat_tok[key] = toks
            nat_last[0] = toks[-1]

        blob_tok = {}
        pk_state = {"u": 0, "g": 0, "v": 0, "slot": 0, "hist": []}

        def pack_piece(dst_ap, src_ap, mode, fold_col=None):
            slot = pk_state["slot"]
            pk_state["slot"] ^= 1
            stg = pack[:, slot * 512:(slot + 1) * 512]
            hist = pk_state["hist"]
            prev = None
            for (sl, tok) in reversed(hist):
                if sl == slot:
                    prev = tok
                    break
            if mode == 'T':
                if prev is not None:
                    wait('y', 'G', prev)
                uv = emit('y', lambda eng, d=stg, s=src_ap: eng.dma_start(d, s, transpose=True),
                          inc=('W', 16))
                fill_tok = ('W', uv)
            else:
                if prev is not None:
                    wait('g', 'G', prev)
                gv = emit('g', lambda eng, d=stg, s=src_ap: eng.dma_start(d, s), inc=('G', 16))
                fill_tok = ('G', gv)
            if fold_col is not None:
                wait('v', 'S', sv_wnc)
                wait_tok('v', fill_tok)
                emit('v', lambda eng, d=stg, col=fold_col:
                     eng.tensor_scalar(d, d, col, None, OP.mult))
                vv = emit('v', lambda eng: eng.drain(), inc=('V', 1))
                fill_tok = ('V', vv)
            wait_tok('g', fill_tok)
            gv2 = emit('g', lambda eng, d=dst_ap, s=stg: eng.dma_start(d, s), inc=('G', 16))
            hist.append((slot, gv2))
            if len(hist) > 8:
                del hist[:4]
            return gv2

        def pack_attn(blob, nat, name, G, K, fold=False):
            outw = G * 512
            for g in range(G):
                last = None
                for k in range(K):
                    sl0 = (g * 512) // 128
                    for sl in range(sl0, sl0 + 4):
                        wait('y', 'G', nat_tok[name][sl])
                    src = nat[g * 512:(g + 1) * 512, k * 128:(k + 1) * 128]
                    dst = blob[g * 128:(g + 1) * 128, k * 512:(k + 1) * 512]
                    last = pack_piece(dst, src, 'T', fold_col=wnc[:, k:k + 1] if fold else None)
                blob_tok[(name, g)] = ('G', last)

        pbB = [p.bitcast(BF16) for p in pb]
        asm = pack[:, 3072:7168]
        a1_slot_T = [0, 0]
        asm_store = [0]
        for g in range(8):
            ev_toks = []
            for sl in range(4):
                r = g * 4 + sl
                slot = r % 2
                stg = pack[:, 1024 + slot * 1024: 1024 + slot * 1024 + 1024]
                wait('g', 'T', a1_slot_T[slot])
                gv = emit('g', lambda eng, d=stg, s=win_h[r * 128:(r + 1) * 128, :]:
                          eng.dma_start(d, s), inc=('G', 16))
                wait('t', 'G', gv)
                wait('t', 'P', pP0)
                for kh in range(2):
                    bk = take_banks(1)[0]
                    for kq in range(4):
                        k = kh * 4 + kq
                        tv = emit('t', lambda eng, o=pbB[bk][:, kq * 128:(kq + 1) * 128],
                                  i_=stg[:, k * 128:(k + 1) * 128]:
                                  eng.transpose(o, i_, identb[:]),
                                  inc=('T', 1) if kq == 3 else None)
                    a1_slot_T[slot] = tv
                    e = 's' if (r + kh) % 2 else 'v'
                    wait(e, 'T', tv)
                    wait(e, 'S', sv_wnc)
                    wait(e, 'G', asm_store[0])
                    for kq in range(4):
                        k = kh * 4 + kq
                        dst = asm[:, k * 512 + sl * 128: k * 512 + sl * 128 + 128]
                        srcp = pbB[bk][:, kq * 128:(kq + 1) * 128]
                        if e == 's':
                            emit('s', lambda eng, d=dst, s=srcp, sc=wnc[:, k:k + 1]:
                                 eng.activation(d, s, AF.Copy, bias=0.0, scale=sc))
                        else:
                            emit('v', lambda eng, d=dst, s=srcp, sc=wnc[:, k:k + 1]:
                                 eng.tensor_scalar(d, s, sc, None, OP.mult))
                    ev = emit(e, lambda eng: eng.drain(), inc=('S' if e == 's' else 'V', 1))
                    release_bank(bk, ('S' if e == 's' else 'V', ev))
                    ev_toks.append(('S' if e == 's' else 'V', ev))
            for tok in ev_toks:
                wait_tok('g', tok)
            gsv = emit('g', lambda eng, d=blobA1[g * 128:(g + 1) * 128, :], s=asm:
                       eng.dma_start(d, s), inc=('G', 16))
            asm_store[0] = gsv
            blob_tok[("in", g)] = ('G', gsv)
            nat_last[0] = gsv

        def APACK2():
            nat_cast(nat_dt, wdt_h, INNER, INNER, "dt")
            pack_attn(blobA2, nat_dt, "dt", 4, 16)

        def APACK3():
            nat_cast(nat_out, wout_h, D, INNER, "out")
            pack_attn(blobA3, nat_out, "out", 2, 16)

        FPACK = []

        def defer_fpack_w1like(blob, src_f32, name):
            for g in range(8):
                def mk(g=g, blob=blob, src=src_f32, name=name):
                    last = None
                    for k in range(8):
                        s = src[k * 128:(k + 1) * 128, g * 512:(g + 1) * 512]
                        d = blob[g * 128:(g + 1) * 128, k * 512:(k + 1) * 512]
                        last = pack_piece(d, s, 'C')
                    blob_tok[(name, g)] = ('G', last)
                FPACK.append(mk)

        def defer_fpack_w2like(blob, src_f32, name):
            for hg in range(2):
                for cg in range(2):
                    gi = hg * 2 + cg
                    def mk(hg=hg, cg=cg, gi=gi, blob=blob, src=src_f32, name=name):
                        last = None
                        for kl in range(16):
                            k = hg * 16 + kl
                            s = src[k * 128:(k + 1) * 128, cg * 512:(cg + 1) * 512]
                            d = blob[gi * 128:(gi + 1) * 128, kl * 512:(kl + 1) * 512]
                            last = pack_piece(d, s, 'C')
                        blob_tok[(name, gi)] = ('G', last)
                    FPACK.append(mk)

        defer_fpack_w1like(blobW1[0], w1_h[0:D, :], "w1_0")
        defer_fpack_w1like(blobW1[1], w1_h[D:2 * D, :], "w1_1")
        defer_fpack_w1like(blobSW1, sw1_h, "sw1")
        defer_fpack_w2like(blobW2[0], w2_h[0:HID, :], "w2_0")
        defer_fpack_w2like(blobW2[1], w2_h[HID:2 * HID, :], "w2_1")
        defer_fpack_w2like(blobSW2, sw2_h, "sw2")

        tok_tpose_done = {}
        tok_x1evac = {}
        tok_x1store = {}
        tok_ln1ops = {}
        tok_wdt_mm = {}
        tok_d1 = {}
        tok_pmul = {}
        tok_win_mm = {}
        tok_wout_mm = {}
        tok_uevac = {}
        tok_statT = {}

        def emit_lnhead(j):
            b, kt = divmod(j, NKT)
            pa = j % 2
            wait_tok('t', xload_tok[j])
            if j >= 2:
                wait_tok('t', tok_x1evac[j - 2])
            for sub in range(4):
                bks = take_banks(2)
                for c in range(Dc):
                    bk = bks[c // 4]
                    q = c % 4
                    tv = emit('t', lambda eng, o=pb[bk][:, q * 128:(q + 1) * 128],
                              i_=stage_x(sub, c * 128, 128): eng.transpose(o, i_, ident[:]),
                              inc=('T', 1) if c % 4 == 3 else None)
                    if c == 3:
                        tv_a = tv
                wait('s', 'T', tv_a)
                if j >= 2:
                    wait_tok('s', tok_x1evac[j - 2])
                for c in range(4):
                    emit('s', lambda eng, d=xr(pa, c, sub * 128, 128),
                         s=pb[bks[0]][:, (c % 4) * 128:(c % 4 + 1) * 128]: eng.copy(d, s))
                wait('s', 'T', tv)
                for c in range(4, 8):
                    emit('s', lambda eng, d=xr(pa, c, sub * 128, 128),
                         s=pb[bks[1]][:, (c % 4) * 128:(c % 4 + 1) * 128]: eng.copy(d, s))
                sv = emit('s', lambda eng: eng.drain(), inc=('S', 1))
                release_bank(bks[0], ('S', sv))
                release_bank(bks[1], ('S', sv))
            tok_tpose_done[j] = ('T', tv)

            if j >= 1:
                wait_tok('s', tok_statT[j - 1])
            if j >= 4:
                wait('s', 'G', nat_tok["out"][-1])
                dump = pack[:, 1024:2048]
            else:
                dump = AR3h[:, 10240:11264]
            for sub in range(4):
                wait('s', 'G', xload_sub[j][sub])
                emit('s', lambda eng, s=stage_x(sub), a=statq[:, 4 + sub:5 + sub], d=dump:
                     eng.activation(d, s, AF.Square, accum_out=a))
                emit('s', lambda eng, s=stage_x(sub), a=statq[:, sub:sub + 1], d=dump:
                     eng.activation(d, s, AF.Copy, bias=0.0, accum_out=a))
            stat_sv = emit('s', lambda eng: eng.drain(), inc=('S', 1))
            if j + 1 < NTA:
                wait_tok('g', tok_tpose_done[j])
                wait('g', 'S', stat_sv)
                emit_xload(j + 1)
            bsx = take_banks(1)[0]
            bsx2 = take_banks(1)[0]
            wait('t', 'S', stat_sv)
            for sub in range(4):
                emit('t', lambda eng, o=pb[bsx][0:1, sub * 128:(sub + 1) * 128],
                     i_=statq[:, sub:sub + 1]: eng.transpose(o, i_, ident[:]))
                tv2 = emit('t', lambda eng, o=pb[bsx2][0:1, sub * 128:(sub + 1) * 128],
                           i_=statq[:, 4 + sub:5 + sub]: eng.transpose(o, i_, ident[:]),
                           inc=('T', 1) if sub == 3 else None)
            tok_statT[j] = ('T', tv2)
            wait('s', 'T', tv2)
            emit('s', lambda eng, s=pb[bsx][0:1, :]: eng.mul(rowA("m"), s, 1.0 / D))
            emit('s', lambda eng, s=pb[bsx2][0:1, :]: eng.mul(rowA("sx2"), s, 1.0 / D))
            sv = emit('s', lambda eng: eng.drain(), inc=('S', 1))
            release_bank(bsx, ('S', sv))
            release_bank(bsx2, ('S', sv))
            wait('v', 'S', sv)
            emit('v', lambda eng: eng.tensor_mul(rowA("sx"), rowA("m"), rowA("m")))
            emit('v', lambda eng: eng.drain())
            emit('v', lambda eng: eng.tensor_sub(rowA("sx2"), rowA("sx2"), rowA("sx")))
            vv = emit('v', lambda eng: eng.drain(), inc=('V', 1))
            wait('s', 'V', vv)
            emit('s', lambda eng: eng.activation(rowA("rstd"), rowA("sx2"), AF.Ln, bias=EPS, scale=1.0))
            emit('s', lambda eng: eng.activation(rowA("sg"), rowA("sx2"), AF.Ln, bias=EPS * EPS,
                                                 scale=1.0 + EPS))
            emit('s', lambda eng: eng.activation(rowA("rstd"), rowA("rstd"), AF.Exp, bias=0.0, scale=-0.5))
            emit('s', lambda eng: eng.activation(rowA("sg"), rowA("sg"), AF.Exp, bias=0.0, scale=-0.5))
            sv = emit('s', lambda eng: eng.drain(), inc=('S', 1))
            wait('v', 'S', sv)
            emit('v', lambda eng: eng.tensor_mul(rowA("p4"), rowA("m"), rowA("sg")))
            emit('v', lambda eng: eng.tensor_mul(rowA("p2"), rowA("m"), rowA("rstd")))
            emit('v', lambda eng: eng.tensor_scalar(rowA("rstd"), rowA("rstd"), 1.0, None, OP.add))
            vv = emit('v', lambda eng: eng.drain(), inc=('V', 1))
            wait('t', 'V', vv)
            bks = take_banks(4)
            srcs = [rowA("rstd"), rowA("p2"), rowA("sg"), rowA("p4")]
            for bi, (bk, src) in enumerate(zip(bks, srcs)):
                tv = emit('t', lambda eng, o=pb[bk][:, 0:512], l=ones_row[0:1, :], r=src:
                          eng.matmul(o, l, r, start=True, stop=True),
                          inc=('T', 1) if bi == 3 else None)
            wait('s', 'T', tv)
            if j >= 1:
                wait_tok('s', tok_ln1ops[j - 1])
            for bi, bk in enumerate(bks):
                emit('s', lambda eng, d=bcB(bi), s=pb[bk][:, 0:512]: eng.copy(d, s))
            sv = emit('s', lambda eng: eng.drain(), inc=('S', 1))
            for bk in bks:
                release_bank(bk, ('S', sv))
            wait('v', 'S', sv)
            if j >= 1:
                wait_tok('v', tok_win[j - 1])
            for c0 in range(0, Dc, 2):
                for c in (c0, c0 + 1):
                    emit('v', lambda eng, a=xr(pa, c), c=c: eng.tensor_mul(gt(c), a, bcB(2)))
                for c in (c0, c0 + 1):
                    emit('v', lambda eng, c=c: eng.tensor_sub(gt(c), gt(c), bcB(3)))
                for c in (c0, c0 + 1):
                    emit('v', lambda eng, a=xr(pa, c): eng.tensor_mul(a, a, bcB(0)))
                for c in (c0, c0 + 1):
                    emit('v', lambda eng, a=xr(pa, c): eng.tensor_sub(a, a, bcB(1)))
            vv = emit('v', lambda eng: eng.drain(), inc=('V', 1))
            tok_ln1ops[j] = ('V', vv)

        def emit_win(i):
            g_ready = tok_ln1ops[i]
            u_done = None
            for g in range(8):
                col0, wv, sid = ring_load(blobA1[g * 128:(g + 1) * 128, :], 4096,
                                          ready_tok=blob_tok.get(("in", g)))
                bks = take_banks(4)
                wait('t', 'W', wv)
                wait_tok('t', g_ready)
                for c4 in range(4):
                    for k in range(8):
                        tv = emit('t', lambda eng, o=pb[bks[c4]][:, 0:512],
                                  l=ring[:, col0 + k * 512 + c4 * 128: col0 + k * 512 + c4 * 128 + 128],
                                  r=gt(k), st=(k == 0), sp=(k == 7):
                                  eng.matmul(o, l, r, start=st, stop=sp),
                                  inc=('T', 1) if (c4 == 3 and k == 7) else None)
                ring_free(sid, ('T', tv))
                wait('s', 'T', tv)
                if i >= 1 and g == 0:
                    wait_tok('s', tok_wdt_mm[i - 1])
                    wait_tok('s', tok_d1[i - 1])
                    wait_tok('s', tok_pmul[i - 1])
                for c4 in range(4):
                    oc = g * 4 + c4
                    dst = ut(oc) if oc < Ic else zt(oc - Ic)
                    emit('s', lambda eng, d=dst, s=pb[bks[c4]][:, 0:512]:
                         eng.activation(d, s, AF.Silu))
                sv = emit('s', lambda eng: eng.drain(), inc=('S', 1))
                for bk in bks:
                    release_bank(bk, ('S', sv))
                if g == 3:
                    u_done = ('S', sv)
            tok_win[i] = ('T', tv)
            tok_uevac[i] = u_done
            return u_done, ('S', sv)

        def emit_body(i, u_done, z_done):
            b, kt = divmod(i, NKT)
            tok0 = b * S + kt * TA
            pa = i % 2
            if i == 0:
                APACK2()

            last_scan_v = None
            for q in range(4):
                colof, wv, sids = ring_load_multi(blobA2[q * 128:(q + 1) * 128, :], 8192,
                                                  ready_tok=blob_tok.get(("dt", q)))
                bks = take_banks(4)
                wait('t', 'W', wv)
                wait_tok('t', u_done)
                for c4 in range(4):
                    for k in range(Ic):
                        tv = emit('t', lambda eng, o=pb[bks[c4]][:, 0:512],
                                  l=ring[:, colof(k) + c4 * 128: colof(k) + c4 * 128 + 128],
                                  r=ut(k), st=(k == 0), sp=(k == Ic - 1):
                                  eng.matmul(o, l, r, start=st, stop=sp),
                                  inc=('T', 1) if (c4 == 3 and k == Ic - 1) else None)
                for sid in sids:
                    ring_free(sid, ('T', tv))
                tok_wdt_mm[i] = ('T', tv)
                wait('s', 'T', tv)
                for c4 in range(4):
                    j = q * 4 + c4
                    emit('s', lambda eng, d=lam_g(c4), s=pb[bks[c4]][:, 0:512], bc=bdtc[:, j:j + 1]:
                         eng.activation(d, s, AF.Sigmoid, bias=bc, scale=1.0))
                sv = emit('s', lambda eng: eng.drain(), inc=('S', 1))
                for bk in bks:
                    release_bank(bk, ('S', sv))
                wait('v', 'S', sv)
                wait_tok('v', z_done)
                if i >= 1 and q == 0:
                    wait_tok('v', tok_x1store[i - 1])
                for c4 in range(4):
                    j = q * 4 + c4
                    emit('v', lambda eng, c4=c4, j=j:
                         eng.scalar_tensor_tensor(d1_g(c4), lam_g(c4), 1.0, ut(j), OP.subtract, OP.mult))
                for c4 in range(4):
                    j = q * 4 + c4
                    if kt == 0:
                        emit('v', lambda eng, c4=c4:
                             eng.tensor_tensor_scan(s_g(c4), lam_g(c4), d1_g(c4), 0.0, OP.mult, OP.subtract))
                    else:
                        emit('v', lambda eng, c4=c4, j=j:
                             eng.tensor_tensor_scan(s_g(c4), lam_g(c4), d1_g(c4), stst[:, j:j + 1],
                                                    OP.mult, OP.subtract))
                for c4 in range(4):
                    j = q * 4 + c4
                    emit('v', lambda eng, c4=c4, j=j:
                         eng.tensor_copy(stst[:, j:j + 1], s_g(c4, 511, 1)))
                for c4 in range(4):
                    j = q * 4 + c4
                    emit('v', lambda eng, c4=c4, j=j:
                         eng.tensor_mul(pt(j), s_g(c4), zt(j)))
                last_scan_v = emit('v', lambda eng: eng.drain(), inc=('V', 1))
            tok_d1[i] = ('V', last_scan_v)
            tok_pmul[i] = ('V', last_scan_v)
            p_done = ('V', last_scan_v)

            if i == 0:
                APACK3()

            st_tok = {}
            for g in range(2):
                colof, wv, sids = ring_load_multi(blobA3[g * 128:(g + 1) * 128, :], 8192,
                                                  ready_tok=blob_tok.get(("out", g)))
                bks = take_banks(4)
                wait('t', 'W', wv)
                wait_tok('t', p_done)
                for c4 in range(4):
                    for k in range(Ic):
                        tv = emit('t', lambda eng, o=pb[bks[c4]][:, 0:512],
                                  l=ring[:, colof(k) + c4 * 128: colof(k) + c4 * 128 + 128],
                                  r=pt(k), st=(k == 0), sp=(k == Ic - 1):
                                  eng.matmul(o, l, r, start=st, stop=sp),
                                  inc=('T', 1) if (c4 == 3 and k == Ic - 1) else None)
                for sid in sids:
                    ring_free(sid, ('T', tv))
                tok_wout_mm[i] = ('T', tv)
                wait('v', 'T', tv)
                for c4 in range(4):
                    c = g * 4 + c4
                    slot = c % 2
                    if c >= 2:
                        wait('v', 'G', st_tok[c - 2])
                    emit('v', lambda eng, slot=slot, s=pb[bks[c4]][:, 0:512], r=xr(pa, c):
                         eng.tensor_add(x1_st(slot), s, r))
                    vv = emit('v', lambda eng: eng.drain(), inc=('V', 1))
                    wait('g', 'V', vv)
                    gv = emit('g', lambda eng, slot=slot, c=c,
                              d=x1blob[c * 128:(c + 1) * 128, tok0:tok0 + TA]:
                              eng.dma_start(d, x1_st(slot)), inc=('G', 16))
                    st_tok[c] = gv
                sv_rel = ('V', vv)
                for bk in bks:
                    release_bank(bk, sv_rel)
            tok_x1evac[i] = ('V', vv)
            tok_x1store[i] = ('G', st_tok[7])

            if DBG and i == 0:
                wait('g', 'V', N['V'])
                wait('g', 'S', N['S'])
                emit('g', lambda eng: eng.dma_start(dbgA, AR2h[:, :]), inc=('G', 16))
                dbg_gv = emit('g', lambda eng: eng.dma_start(dbgB, AR3h[:, :]), inc=('G', 16))
                for e in 'tvs':
                    wait(e, 'G', dbg_gv)

            nslice = (len(FPACK) + NTA - 1 - i) // (NTA - i) if i < NTA - 1 else len(FPACK)
            for _ in range(min(nslice, len(FPACK))):
                FPACK.pop(0)()

        tok_win = {}
        emit_lnhead(0)
        for i in range(NTA):
            u_done, z_done = emit_win(i)
            if i + 1 < NTA:
                emit_lnhead(i + 1)
            emit_body(i, u_done, z_done)

        A_last_x1store = tok_x1store[NTA - 1]
        A_last_x1evac = tok_x1evac[NTA - 1]
        A_ln1 = tok_ln1ops[NTA - 1]
        A_pmul = tok_pmul[NTA - 1]
        A_wout = tok_wout_mm[NTA - 1]
        A_wdt = tok_wdt_mm[NTA - 1]
        A_d1 = tok_d1[NTA - 1]
        A_uevac = ('S', N['S'])
        A_tpose = tok_tpose_done[NTA - 1]

        tok_f_stt = {}
        tok_f_outst = {}
        tok_f_outcp = {}
        tok_f_m0rd = {}
        tok_f_g2rd = {}
        tok_f_hrd = {}
        tok_f_acc = {}

        for i in range(NTF):
            tok0 = i * TF

            if i == 0:
                wait_tok('g', A_last_x1evac)
                wait_tok('g', A_last_x1store)
            else:
                wait_tok('g', tok_f_stt[i - 1])
                wait_tok('g', tok_f_outst[i - 1])
            xg = {}
            for c in range(Dc):
                gv = emit('g', lambda eng, d=x1f(c), s=x1blob[c * 128:(c + 1) * 128, tok0:tok0 + TF]:
                          eng.dma_start(d, s), inc=('G', 16))
                xg[c] = gv
            x1_tok = ('G', gv)

            bsx = take_banks(2)
            bsx2 = take_banks(2)
            bgd = take_banks(2)
            wait('t', 'G', gv)
            if i == 0:
                wait_tok('s', A_last_x1store)
                wait_tok('s', A_pmul)
            else:
                wait_tok('s', tok_f_outcp[i - 1])
            sq_tok = {}
            bslot = lambda sl: AR4b[:, sl * 512:(sl + 1) * 512]
            for half in range(2):
                for c in range(Dc):
                    slot = (half * Dc + c) % 2
                    key = (half, c)
                    if (half * Dc + c) >= 2:
                        pk = (half, c - 2) if c >= 2 else (0, Dc - 2 + c)
                        wait('s', 'T', sq_tok[pk])
                    if i >= 1:
                        wait_tok('s', tok_f_acc[i - 1])
                    emit('s', lambda eng, d=bslot(slot * 2 + 1), s=x1f(c, half * 512, 512):
                         eng.activation(d, s, AF.Square))
                    ssv = emit('s', lambda eng: eng.drain(), inc=('S', 1))
                    tv = emit('t', lambda eng, o=pb[bsx[half]][0:1, :], l=ones_colf[:, 0:1],
                              r=x1f(c, half * 512, 512), st=(c == 0), sp=(c == Dc - 1):
                              eng.matmul(o, l, r, start=st, stop=sp),
                              inc=('T', 1) if c == Dc - 1 else None)
                    wait('t', 'S', ssv)
                    tv2 = emit('t', lambda eng, o=pb[bsx2[half]][0:1, :], l=ones_colb[:, 0:1],
                               r=bslot(slot * 2 + 1), st=(c == 0), sp=(c == Dc - 1):
                               eng.matmul(o, l, r, start=st, stop=sp), inc=('T', 1))
                    sq_tok[key] = tv2
                for c in range(Dc):
                    tg = emit('t', lambda eng, o=pb[bgd[half]][0:1, :], l=gwd[:, c:c + 1],
                              r=x1f(c, half * 512, 512), st=(c == 0), sp=(c == Dc - 1):
                              eng.matmul(o, l, r, start=st, stop=sp),
                              inc=('T', 1) if c == Dc - 1 else None)
            wait('s', 'T', tg)
            for half in range(2):
                emit('s', lambda eng, h=half, s=pb[bsx[half]][0:1, :]:
                     eng.mul(rowF("m2")[:, h * 512:(h + 1) * 512], s, 1.0 / D))
                emit('s', lambda eng, h=half, s=pb[bsx2[half]][0:1, :]:
                     eng.mul(rowF("sx2")[:, h * 512:(h + 1) * 512], s, 1.0 / D))
                emit('s', lambda eng, h=half, s=pb[bgd[half]][0:1, :]:
                     eng.copy(rowF("gd")[:, h * 512:(h + 1) * 512], s))
            sv = emit('s', lambda eng: eng.drain(), inc=('S', 1))
            for bk in bsx + bsx2 + bgd:
                release_bank(bk, ('S', sv))
            wait('v', 'S', sv)
            emit('v', lambda eng: eng.tensor_mul(rowF("sx"), rowF("m2"), rowF("m2")))
            emit('v', lambda eng: eng.drain())
            emit('v', lambda eng: eng.tensor_sub(rowF("sx2"), rowF("sx2"), rowF("sx")))
            vv = emit('v', lambda eng: eng.drain(), inc=('V', 1))
            wait('s', 'V', vv)
            emit('s', lambda eng: eng.activation(rowF("rstd2"), rowF("sx2"), AF.Ln, bias=EPS, scale=1.0))
            emit('s', lambda eng: eng.drain())
            emit('s', lambda eng: eng.activation(rowF("rstd2"), rowF("rstd2"), AF.Exp, bias=0.0,
                                                 scale=-0.5))
            sv = emit('s', lambda eng: eng.drain(), inc=('S', 1))
            wait('v', 'S', sv)
            emit('v', lambda eng: eng.tensor_scalar(rowF("sx"), rowF("m2"), swd[0:1, 0:1], None, OP.mult))
            emit('v', lambda eng: eng.tensor_mul(rowF("m2r2"), rowF("m2"), rowF("rstd2")))
            emit('v', lambda eng: eng.tensor_sub(rowF("gd"), rowF("gd"), rowF("sx")))
            emit('v', lambda eng: eng.drain())
            emit('v', lambda eng: eng.tensor_scalar(rowF("gd"), rowF("gd"), 0.0, None, OP.is_ge))
            vv = emit('v', lambda eng: eng.drain(), inc=('V', 1))
            wait('t', 'V', vv)
            bks = take_banks(6)
            for half in range(2):
                for bi, src in enumerate([rowF("rstd2"), rowF("m2r2"), rowF("gd")]):
                    bk = bks[bi * 2 + half]
                    tv = emit('t', lambda eng, o=pb[bk][:, 0:512], l=ones_row[0:1, :],
                              r=src[:, half * 512:(half + 1) * 512]:
                              eng.matmul(o, l, r, start=True, stop=True),
                              inc=('T', 1) if (half == 1 and bi == 2) else None)
            wait('s', 'T', tv)
            if i >= 1:
                wait_tok('s', tok_f_acc[i - 1])
            for half in range(2):
                for bi, dstf in enumerate([rstd2B, m2r2B, mask0B]):
                    bk = bks[bi * 2 + half]
                    emit('s', lambda eng, d=dstf(half * 512, 512), s=pb[bk][:, 0:512]:
                         eng.copy(d, s))
            sv = emit('s', lambda eng: eng.drain(), inc=('S', 1))
            for bk in bks:
                release_bank(bk, ('S', sv))
            wait('v', 'S', sv)
            if i == 0:
                wait_tok('v', A_wdt)
                wait_tok('v', A_d1)
                wait_tok('v', A_uevac)
                wait_tok('v', A_pmul)
            else:
                wait_tok('v', tok_f_g2rd[i - 1])
                wait_tok('v', tok_f_m0rd[i - 1])
            scrL = [AR4[:, 3584:4608], AR4[:, 4608:5632]]
            for c0 in range(0, Dc, 2):
                for c in (c0, c0 + 1):
                    emit('v', lambda eng, c=c, s=scrL[c % 2]: eng.tensor_mul(s, x1f(c), rstd2B()))
                for c in (c0, c0 + 1):
                    emit('v', lambda eng, c=c, s=scrL[c % 2]: eng.tensor_sub(g2(c), s, m2r2B()))
                for c in (c0, c0 + 1):
                    emit('v', lambda eng, c=c: eng.tensor_mul(m0(c), g2(c), mask0B()))
            vv = emit('v', lambda eng: eng.drain(), inc=('V', 1))
            m_ready = ('V', vv)

            if DBG and i == 0:
                wait('g', 'V', N['V'])
                dbg_gv = emit('g', lambda eng: eng.dma_start(dbgC, AR2h[:, :]), inc=('G', 16))
                for e in 'tvs':
                    wait(e, 'G', dbg_gv)

            def h_pass(blob_name, blob, rhs_fn, hg, pass_idx):
                relu_last = None
                if blob_name == "w1_0":
                    bias_col = lambda j: b1c[:, j:j + 1]
                elif blob_name == "w1_1":
                    bias_col = lambda j: b1c[:, 32 + j:33 + j]
                else:
                    bias_col = lambda j: sb1c[:, j:j + 1]
                for gl in range(4):
                    g = hg * 4 + gl
                    col0, wv, sid = ring_load(blob[g * 128:(g + 1) * 128, :], 4096,
                                              ready_tok=blob_tok.get((blob_name, g)))
                    for sub in range(2):
                        bks = take_banks(4)
                        wait('t', 'W', wv)
                        wait_tok('t', m_ready)
                        if pass_idx == 0 and hg == 0 and gl == 0 and sub == 0:
                            if i == 0:
                                wait_tok('t', A_wout)
                                wait_tok('s', A_d1)
                            else:
                                wait_tok('t', tok_f_hrd[i - 1])
                        for c2 in range(2):
                            for half in range(2):
                                bk = bks[c2 * 2 + half]
                                for k in range(Dc):
                                    tv = emit('t', lambda eng, o=pb[bk][:, 0:512],
                                              l=ring[:, col0 + k * 512 + (sub * 2 + c2) * 128:
                                                     col0 + k * 512 + (sub * 2 + c2) * 128 + 128],
                                              r=rhs_fn(k, half), st=(k == 0), sp=(k == Dc - 1):
                                              eng.matmul(o, l, r, start=st, stop=sp),
                                              inc=('T', 1) if (c2 == 1 and half == 1 and k == Dc - 1) else None)
                        if sub == 1:
                            ring_free(sid, ('T', tv))
                        wait('s', 'T', tv)
                        for c2 in range(2):
                            j = g * 4 + sub * 2 + c2
                            jj = j - hg * 16
                            for half in range(2):
                                bk = bks[c2 * 2 + half]
                                emit('s', lambda eng, d=ht(jj, half * 512, 512), s=pb[bk][:, 0:512],
                                     bc=bias_col(j):
                                     eng.activation(d, s, AF.Relu, bias=bc, scale=1.0))
                        sv = emit('s', lambda eng: eng.drain(), inc=('S', 1))
                        for bk in bks:
                            release_bank(bk, ('S', sv))
                        relu_last = sv
                return ('S', relu_last)

            def w2_pass(blob_name, blob, hg, first):
                h_ready = None
                for cg in range(2):
                    gi = hg * 2 + cg
                    colof, wv, sids = ring_load_multi(blob[gi * 128:(gi + 1) * 128, :], 8192,
                                                      ready_tok=blob_tok.get((blob_name, gi)))
                    for sub in range(2):
                        bks = take_banks(4)
                        wait('t', 'W', wv)
                        for c2 in range(2):
                            for half in range(2):
                                bk = bks[c2 * 2 + half]
                                for kl in range(16):
                                    tv = emit('t', lambda eng, o=pb[bk][:, 0:512],
                                              l=ring[:, colof(kl) + (sub * 2 + c2) * 128:
                                                     colof(kl) + (sub * 2 + c2) * 128 + 128],
                                              r=ht(kl, half * 512, 512), st=(kl == 0), sp=(kl == 15):
                                              eng.matmul(o, l, r, start=st, stop=sp),
                                              inc=('T', 1) if (c2 == 1 and half == 1 and kl == 15) else None)
                        if sub == 1:
                            for sid in sids:
                                ring_free(sid, ('T', tv))
                        wait('v', 'T', tv)
                        for c2 in range(2):
                            c = cg * 4 + sub * 2 + c2
                            for half in range(2):
                                bk = bks[c2 * 2 + half]
                                if first:
                                    vv = emit('v', lambda eng, c=c, half=half, bk=bk:
                                              eng.tensor_copy(out_acc(c, half * 512, 512),
                                                              pb[bk][:, 0:512]),
                                              inc=('V', 1) if (c2 == 1 and half == 1) else None)
                                else:
                                    vv = emit('v', lambda eng, c=c, half=half, bk=bk:
                                              eng.tensor_add(out_acc(c, half * 512, 512),
                                                             out_acc(c, half * 512, 512),
                                                             pb[bk][:, 0:512]),
                                              inc=('V', 1) if (c2 == 1 and half == 1) else None)
                        for bk in bks:
                            release_bank(bk, ('V', vv))
                return ('V', vv)

            first_acc = True
            acc_tok = None
            for (nm1, bl1, rhs1, nm2, bl2) in [("w1_0", blobW1[0], lambda k, h: m0(k, h * 512, 512), "w2_0", blobW2[0]),
                                               ("w1_1", blobW1[1], lambda k, h: m0(k, h * 512, 512), "w2_1", blobW2[1]),
                                               ("sw1", blobSW1, lambda k, h: g2(k, h * 512, 512), "sw2", blobSW2)]:
                if nm1 == "w1_1":
                    wait_tok('v', tok_f_m0rd[i])
                    for c in range(Dc):
                        emit('v', lambda eng, c=c: eng.tensor_sub(m0(c), g2(c), m0(c)))
                    mv = emit('v', lambda eng: eng.drain(), inc=('V', 1))
                    m_ready = ('V', mv)
                for hg in range(2):
                    hr = h_pass(nm1, bl1, rhs1, hg, 0 if first_acc else 1)
                    wait_tok('t', hr)
                    acc_tok = w2_pass(nm2, bl2, hg, first_acc and hg == 0)
                    if hg == 0 and first_acc:
                        first_acc = False
                    tok_f_hrd[i] = ('T', N['T'])
                if nm1 == "w1_0":
                    tok_f_m0rd[i] = ('T', N['T'])
                if nm1 == "w1_1":
                    tok_f_m0rd[i] = ('T', N['T'])
                if nm1 == "sw1":
                    tok_f_g2rd[i] = ('T', N['T'])

            if DBG and i == 0:
                wait_tok('g', acc_tok)
                emit('g', lambda eng: eng.dma_start(dbgD, AR4[:, :]), inc=('G', 16))
                emit('g', lambda eng: eng.dma_start(dbgF, AR1[:, :]), inc=('G', 16))
                dbg_gv = emit('g', lambda eng: eng.dma_start(dbgG, bfc[:, 0:8]), inc=('G', 16))
                for e in 'tvs':
                    wait(e, 'G', dbg_gv)

            wait_tok('v', acc_tok)
            for c in range(Dc):
                emit('v', lambda eng, c=c:
                     eng.scalar_tensor_tensor(out_acc(c), out_acc(c), bfc[:, c:c + 1], x1f(c),
                                              OP.add, OP.add))
            fv = emit('v', lambda eng: eng.drain(), inc=('V', 1))
            tok_f_stt[i] = ('V', fv)

            if DBG and i == 0:
                wait('g', 'V', N['V'])
                dbg_gv = emit('g', lambda eng: eng.dma_start(dbgE, AR4[:, :]), inc=('G', 16))
                for e in 'tvs':
                    wait(e, 'G', dbg_gv)

            out_last_g = None
            for sub in range(8):
                slot = sub % 2
                bks = take_banks(2)
                wait('t', 'V', fv)
                for c in range(Dc):
                    bk = bks[c // 4]
                    qq = c % 4
                    tv = emit('t', lambda eng, o=pb[bk][:, qq * 128:(qq + 1) * 128],
                              i_=out_acc(c, sub * 128, 128): eng.transpose(o, i_, ident[:]),
                              inc=('T', 1) if qq == 3 else None)
                    if c == 3:
                        tv_a = tv
                wait('s', 'T', tv)
                if i >= 1 and sub < 2:
                    wait_tok('s', tok_f_outst[i - 1])
                if sub >= 2:
                    wait('s', 'G', outst_tok[sub - 2])
                elif i == 0 and sub < 2:
                    wait_tok('s', A_last_x1evac)
                if sub == 0:
                    outst_tok = {}
                for c in range(Dc):
                    bk = bks[c // 4]
                    emit('s', lambda eng, d=out_tm(slot)[:, c * 128:(c + 1) * 128],
                         s=pb[bk][:, (c % 4) * 128:(c % 4 + 1) * 128]: eng.copy(d, s))
                sv = emit('s', lambda eng: eng.drain(), inc=('S', 1))
                for bk in bks:
                    release_bank(bk, ('S', sv))
                tok_f_outcp[i] = ('S', sv)
                wait('g', 'S', sv)
                gv = emit('g', lambda eng, d=out_h[tok0 + sub * 128: tok0 + (sub + 1) * 128, :],
                          s=out_tm(slot): eng.dma_start(d, s), inc=('G', 16))
                outst_tok[sub] = gv
                out_last_g = gv
            tok_f_outst[i] = ('G', out_last_g)
            tok_f_acc[i] = ('T', N['T'])

        while FPACK:
            FPACK.pop(0)()

        wait('g', 'G', N['G'])

        block = E(nc.Block())

        @block.tensor
        def _(eng):
            for f in OPS['t']:
                f(eng)

        @block.vector
        def _(eng):
            for f in OPS['v']:
                f(eng)

        @block.scalar
        def _(eng):
            for f in OPS['s']:
                f(eng)

        @block.gpsimd
        def _(eng):
            for f in OPS['g']:
                f(eng)

        @block.sync
        def _(eng):
            for f in OPS['y']:
                f(eng)

    return nc


_NC_CACHE = [None]


def kernel(**inputs):
    xs = np.ascontiguousarray(inputs["L0_attn_x"], dtype=np.float32)
    ins_common = {
        "w_norm": np.ascontiguousarray(inputs["L0_attn_w_norm"], np.float32).reshape(8, 128),
        "W_in": np.ascontiguousarray(inputs["L0_attn_W_in"], np.float32),
        "W_dt": np.ascontiguousarray(inputs["L0_attn_W_dt"], np.float32),
        "b_dt": np.ascontiguousarray(inputs["L0_attn_b_dt"], np.float32).reshape(16, 128),
        "W_out": np.ascontiguousarray(inputs["L0_attn_W_out"], np.float32),
        "gate_w": np.ascontiguousarray(inputs["L0_ffn_gate_w"], np.float32),
        "w1": np.ascontiguousarray(inputs["L0_ffn_w1"], np.float32).reshape(2 * D, HID),
        "b1": np.ascontiguousarray(inputs["L0_ffn_b1"], np.float32).reshape(64, 128),
        "w2": np.ascontiguousarray(inputs["L0_ffn_w2"], np.float32).reshape(2 * HID, D),
        "b2": np.ascontiguousarray(inputs["L0_ffn_b2"], np.float32).reshape(16, 128),
        "sw1": np.ascontiguousarray(inputs["L0_ffn_sw1"], np.float32),
        "sb1": np.ascontiguousarray(inputs["L0_ffn_sb1"], np.float32).reshape(32, 128),
        "sw2": np.ascontiguousarray(inputs["L0_ffn_sw2"], np.float32),
        "sb2": np.ascontiguousarray(inputs["L0_ffn_sb2"], np.float32).reshape(8, 128),
    }
    in_maps = []
    for core in range(NCORES):
        m = dict(ins_common)
        m["x"] = np.ascontiguousarray(xs[core * NB:(core + 1) * NB].reshape(NTOK, D))
        in_maps.append(m)
    if _NC_CACHE[0] is None:
        _NC_CACHE[0] = build_nc()
    nc = _NC_CACHE[0]
    res = run_bass_kernel_spmd(nc, in_maps, core_ids=list(range(NCORES)))
    out = np.concatenate([r["out"].reshape(NB, S, D) for r in res.results], axis=0)
    return out


# version 10
# speedup vs baseline: 1.2400x; 1.0181x over previous
import contextlib
import numpy as np
import concourse.bass as bass
import concourse.mybir as mybir
from concourse.bass_utils import run_bass_kernel_spmd

F32 = mybir.dt.float32
BF16 = mybir.dt.bfloat16
AF = mybir.ActivationFunctionType
OP = mybir.AluOpType

NCORES = 8
D, Dc = 1024, 8
INNER, Ic = 2048, 16
HID, Hc = 4096, 32
NB, S = 4, 2048
NTOK = NB * S
TA, NKT = 512, 4
NTA = NB * NKT
TF, NTF = 1024, 8
EPS = 1e-5

RF = 16384
DBG = False


def build_nc():
    nc = bass.Bass("TRN2", target_bir_lowering=False)

    x_h = nc.dram_tensor("x", [NTOK, D], F32, kind="ExternalInput").ap()
    wn_h = nc.dram_tensor("w_norm", [8, 128], F32, kind="ExternalInput").ap()
    win_h = nc.dram_tensor("W_in", [2 * INNER, D], F32, kind="ExternalInput").ap()
    wdt_h = nc.dram_tensor("W_dt", [INNER, INNER], F32, kind="ExternalInput").ap()
    bdt_h = nc.dram_tensor("b_dt", [16, 128], F32, kind="ExternalInput").ap()
    wout_h = nc.dram_tensor("W_out", [D, INNER], F32, kind="ExternalInput").ap()
    gw_h = nc.dram_tensor("gate_w", [2, D], F32, kind="ExternalInput").ap()
    w1_h = nc.dram_tensor("w1", [2 * D, HID], F32, kind="ExternalInput").ap()
    b1_h = nc.dram_tensor("b1", [64, 128], F32, kind="ExternalInput").ap()
    w2_h = nc.dram_tensor("w2", [2 * HID, D], F32, kind="ExternalInput").ap()
    b2_h = nc.dram_tensor("b2", [16, 128], F32, kind="ExternalInput").ap()
    sw1_h = nc.dram_tensor("sw1", [D, HID], F32, kind="ExternalInput").ap()
    sb1_h = nc.dram_tensor("sb1", [32, 128], F32, kind="ExternalInput").ap()
    sw2_h = nc.dram_tensor("sw2", [HID, D], F32, kind="ExternalInput").ap()
    sb2_h = nc.dram_tensor("sb2", [8, 128], F32, kind="ExternalInput").ap()
    out_h = nc.dram_tensor("out", [NTOK, D], F32, kind="ExternalOutput").ap()

    nat_in = nc.dram_tensor("nat_in", [2 * INNER, D], BF16, kind="Internal").ap()
    nat_dt = nc.dram_tensor("nat_dt", [INNER, INNER], BF16, kind="Internal").ap()
    nat_out = nc.dram_tensor("nat_out", [D, INNER], BF16, kind="Internal").ap()
    blobA1 = nc.dram_tensor("blobA1", [8 * 128, 8 * 512], BF16, kind="Internal").ap()
    blobA2 = nc.dram_tensor("blobA2", [4 * 128, 16 * 512], BF16, kind="Internal").ap()
    blobA3 = nc.dram_tensor("blobA3", [2 * 128, 16 * 512], BF16, kind="Internal").ap()
    blobW1 = [nc.dram_tensor(f"blobW1_{e}", [8 * 128, 8 * 512], BF16, kind="Internal").ap() for e in range(2)]
    blobSW1 = nc.dram_tensor("blobSW1", [8 * 128, 8 * 512], BF16, kind="Internal").ap()
    blobW2 = [nc.dram_tensor(f"blobW2_{e}", [4 * 128, 16 * 512], BF16, kind="Internal").ap() for e in range(2)]
    blobSW2 = nc.dram_tensor("blobSW2", [4 * 128, 16 * 512], BF16, kind="Internal").ap()
    x1blob = nc.dram_tensor("x1blob", [Dc * 128, NTOK], F32,
                            kind="ExternalOutput" if DBG else "Internal").ap()
    if DBG:
        dbgA = nc.dram_tensor("dbgA", [128, 16384], BF16, kind="ExternalOutput").ap()
        dbgB = nc.dram_tensor("dbgB", [128, 16384], BF16, kind="ExternalOutput").ap()
        dbgC = nc.dram_tensor("dbgC", [128, 16384], BF16, kind="ExternalOutput").ap()
        dbgD = nc.dram_tensor("dbgD", [128, 8192], F32, kind="ExternalOutput").ap()
        dbgE = nc.dram_tensor("dbgE", [128, 8192], F32, kind="ExternalOutput").ap()
        dbgF = nc.dram_tensor("dbgF", [128, 8192], F32, kind="ExternalOutput").ap()
        dbgG = nc.dram_tensor("dbgG", [128, 8], F32, kind="ExternalOutput").ap()
        dbgH = nc.dram_tensor("dbgH", [128, 40], F32, kind="ExternalOutput").ap()

    with contextlib.ExitStack() as es:
        E = es.enter_context
        AR1 = E(nc.sbuf_tensor("AR1", [128, 8192], F32))
        AR2h = E(nc.sbuf_tensor("AR2", [128, 16384], BF16))
        AR3h = E(nc.sbuf_tensor("AR3", [128, 16384], BF16))
        AR4 = E(nc.sbuf_tensor("AR4", [128, 8192], F32))
        ring = E(nc.sbuf_tensor("ringt", [128, RF], BF16))
        pack = E(nc.sbuf_tensor("pack", [128, 7168], BF16))
        statq = E(nc.sbuf_tensor("statq", [128, 8], F32))
        ident = E(nc.sbuf_tensor("ident", [128, 128], F32))
        ones_row = E(nc.sbuf_tensor("ones_row", [1, 128], F32))
        ones_colf = E(nc.sbuf_tensor("ones_colf", [128, 1], F32))
        ones_colb = E(nc.sbuf_tensor("ones_colb", [128, 1], BF16))
        identb = E(nc.sbuf_tensor("identb", [128, 128], BF16))
        rows = E(nc.sbuf_tensor("rows", [1, 6144], F32))
        cst = E(nc.sbuf_tensor("cst", [64, 1024], F32))
        wnc = E(nc.sbuf_tensor("wnc", [128, 8], F32))
        bdtc = E(nc.sbuf_tensor("bdtc", [128, 16], F32))
        b1c = E(nc.sbuf_tensor("b1c", [128, 64], F32))
        sb1c = E(nc.sbuf_tensor("sb1c", [128, 32], F32))
        b2c = E(nc.sbuf_tensor("b2c", [128, 16], F32))
        sb2c = E(nc.sbuf_tensor("sb2c", [128, 8], F32))
        bfc = E(nc.sbuf_tensor("bfc", [128, 8], F32))
        gwd = E(nc.sbuf_tensor("gwd", [128, 8], F32))
        swd = E(nc.sbuf_tensor("swd", [1, 1], F32))
        stst = E(nc.sbuf_tensor("stst", [128, 16], F32))
        ceps = E(nc.sbuf_tensor("ceps", [128, 2], F32))
        bfc2 = E(nc.sbuf_tensor("bfc2", [128, 8], F32)) if DBG else None
        pb = [E(nc.psum_tensor(f"pb{i}", [128, 512], F32)) for i in range(8)]

        sem = {k: E(nc.semaphore(f"sem{k}")) for k in "TVSGWUP"}

        AR4b = AR4.bitcast(BF16)

        OPS = {k: [] for k in "tvsgy"}
        N = {k: 0 for k in "TVSGWUP"}
        WAITED = {e: {} for e in "tvsgy"}

        def emit(e, fn, inc=None):
            v = None
            if inc is not None:
                key, amt = inc
                N[key] += amt
                v = N[key]
                h = sem[key]
                def thunk(eng, fn=fn, h=h, amt=amt):
                    fn(eng).then_inc(h, amt)
            else:
                def thunk(eng, fn=fn):
                    fn(eng)
            OPS[e].append(thunk)
            return v

        def wait(e, key, val):
            if val is None or val <= 0:
                return
            if val <= WAITED[e].get(key, 0):
                return
            WAITED[e][key] = val
            h = sem[key]
            OPS[e].append(lambda eng, h=h, val=val: eng.wait_ge(h, val))

        def wait_tok(e, tok):
            if tok is not None:
                wait(e, tok[0], tok[1])

        bank_tok = [None] * 8
        bank_rr = [0]

        def take_banks(n):
            ids = []
            for _ in range(n):
                i = bank_rr[0] % 8
                bank_rr[0] += 1
                wait_tok('t', bank_tok[i])
                bank_tok[i] = None
                ids.append(i)
            return ids

        def release_bank(i, tok):
            bank_tok[i] = tok

        ring_state = {"head": 0, "spans": []}
        span_autoid = [0]
        span_free = {}

        def ring_load(src_ap, ncols, ready_tok=None):
            st = ring_state
            head = st["head"]
            if head % 4096:
                head += 4096 - head % 4096
            if head + ncols > RF:
                head = 0
            keep = []
            for (s0, e0, sid) in st["spans"]:
                if s0 < head + ncols and e0 > head:
                    tok = span_free.get(sid)
                    assert tok is not None, "ring overwrite before consumer registered"
                    wait_tok('y', tok)
                else:
                    keep.append((s0, e0, sid))
            st["spans"] = keep
            sid = span_autoid[0]
            span_autoid[0] += 1
            st["spans"].append((head, head + ncols, sid))
            st["head"] = head + ncols
            if ready_tok is not None:
                wait_tok('y', ready_tok)
            dst = ring[:, head:head + ncols]
            wv = emit('y', lambda eng, dst=dst, src=src_ap: eng.dma_start(dst, src), inc=('W', 16))
            return head, wv, sid

        def ring_free(sid, tok):
            span_free[sid] = tok

        def ring_load_multi(src_ap, total_cols, ready_tok=None):
            chunks = []
            wv = None
            for off in range(0, total_cols, 4096):
                c0, wv, sid = ring_load(src_ap[:, off:off + 4096], 4096, ready_tok=ready_tok)
                chunks.append((c0, sid))
            def colof(k, chunks=tuple(c for c, _ in chunks)):
                return chunks[(k * 512) // 4096] + (k * 512) % 4096
            return colof, wv, [s for _, s in chunks]

        def xr(pa, c, lo=0, n=512):
            return AR1[:, pa * 4096 + c * 512 + lo: pa * 4096 + c * 512 + lo + n]

        def x1f(c, lo=0, n=1024):
            return AR1[:, c * 1024 + lo: c * 1024 + lo + n]

        def out_tm(slot):
            return AR1[:, slot * 1024:(slot + 1) * 1024]

        def ut(j, lo=0, n=512):
            return AR2h[:, j * 512 + lo: j * 512 + lo + n]

        def zt(j, lo=0, n=512):
            return AR2h[:, 8192 + j * 512 + lo: 8192 + j * 512 + lo + n]

        def g2(c, lo=0, n=1024):
            return AR2h[:, c * 1024 + lo: c * 1024 + lo + n]

        def m0(c, lo=0, n=1024):
            return AR2h[:, 8192 + c * 1024 + lo: 8192 + c * 1024 + lo + n]

        def pt(j, lo=0, n=512):
            return AR3h[:, j * 512 + lo: j * 512 + lo + n]

        def gt(c, lo=0, n=512):
            return AR3h[:, 8192 + c * 512 + lo: 8192 + c * 512 + lo + n]

        def lam_g(q, n=512):
            return AR3h[:, 12288 + q * 512: 12288 + q * 512 + n]

        def d1_g(q, n=512):
            return AR3h[:, 14336 + q * 512: 14336 + q * 512 + n]

        def ht(j, lo=0, n=1024):
            return AR3h[:, j * 1024 + lo: j * 1024 + lo + n]

        def s_g(q, lo=0, n=512):
            return AR4[:, q * 512 + lo: q * 512 + lo + n]

        def x2scr(slot, n=512):
            return AR4[:, slot * 512: slot * 512 + n]

        def x1_st(slot):
            return AR4[:, slot * 512:(slot + 1) * 512]

        def bcB(i, n=512):
            return AR4[:, 2048 + i * 512: 2048 + i * 512 + n]

        def stage_x(sub, lo=0, n=1024):
            return AR4[:, 4096 + sub * 1024 + lo: 4096 + sub * 1024 + lo + n]

        def out_acc(c, lo=0, n=1024):
            return AR4[:, c * 1024 + lo: c * 1024 + lo + n]

        def rstd2B(lo=0, n=1024):
            return AR4[:, 1024 + lo: 1024 + lo + n]

        def m2r2B(lo=0, n=1024):
            return AR4[:, 2048 + lo: 2048 + lo + n]

        def mask0B(lo=0, n=1024):
            return AR4b[:, 6144 + lo: 6144 + lo + n]

        ROWA = {"sx": 0, "sx2": 512, "m": 1024, "rstd": 1536, "sg": 2048, "p4": 2560, "p2": 3072}
        ROWF = {"sx": 0, "sx2": 1024, "m2": 2048, "rstd2": 3072, "m2r2": 4096, "gd": 5120}

        def rowA(nm, n=512):
            return rows[:, ROWA[nm]:ROWA[nm] + n]

        def rowF(nm, n=1024):
            return rows[:, ROWF[nm]:ROWF[nm] + n]

        nc.const_aps.register(F32, EPS, ceps[:, 0:1])
        nc.const_aps.register(F32, EPS * EPS, ceps[:, 1:2])
        emit('g', lambda eng: eng.iota(ident[:], [[1, 128]], base=0, channel_multiplier=-1,
                                       allow_small_or_imprecise_dtypes=True))
        emit('g', lambda eng: eng.memset(ceps[:, 0:1], EPS))
        emit('g', lambda eng: eng.memset(ceps[:, 1:2], EPS * EPS))
        emit('g', lambda eng: eng.memset(ones_row[:], 1.0))
        emit('g', lambda eng: eng.tensor_scalar(ident[:], ident[:], 0.0, None, OP.is_equal))
        emit('g', lambda eng: eng.memset(ones_colb[:], 1.0))
        emit('g', lambda eng: eng.tensor_copy(identb[:], ident[:]))
        pP0 = emit('g', lambda eng: eng.memset(ones_colf[:], 1.0), inc=('P', 1))

        xload_tok = {}
        xload_sub = {}

        def emit_xload(i):
            b, kt = divmod(i, NKT)
            tok0 = b * S + kt * TA
            subs = []
            for sub in range(4):
                src = x_h[tok0 + sub * 128: tok0 + (sub + 1) * 128, :]
                v = emit('g', lambda eng, d=stage_x(sub), s=src: eng.dma_start(d, s), inc=('G', 16))
                subs.append(v)
            xload_tok[i] = ('G', v)
            xload_sub[i] = subs

        emit_xload(0)

        cst_prev_T = [0]

        def load_const(src_ap, nrows, ncols, dst_cols_ap):
            wait('g', 'T', cst_prev_T[0])
            gv = emit('g', lambda eng, d=cst[0:nrows, 0:128], s=src_ap: eng.dma_start(d, s), inc=('G', 16))
            wait('t', 'G', gv)
            wait('t', 'P', pP0)
            bk = take_banks(1)[0]
            tv = emit('t', lambda eng, o=pb[bk][0:128, 0:nrows], i_=cst[0:nrows, 0:128],
                      idn=ident[0:nrows, 0:nrows]:
                      eng.transpose(o, i_, idn), inc=('T', 1))
            cst_prev_T[0] = tv
            wait('s', 'T', tv)
            emit('s', lambda eng, d=dst_cols_ap, s=pb[bk][0:128, 0:nrows]: eng.copy(d, s))
            sv = emit('s', lambda eng: eng.drain(), inc=('S', 1))
            release_bank(bk, ('S', sv))
            return sv

        sv_wnc = load_const(wn_h, 8, 128, wnc[:, 0:8])
        load_const(bdt_h, 16, 128, bdtc[:, 0:16])
        load_const(b1_h, 64, 128, b1c[:, 0:64])
        load_const(sb1_h, 32, 128, sb1c[:, 0:32])
        load_const(b2_h, 16, 128, b2c[:, 0:16])
        sv_sb2 = load_const(sb2_h, 8, 128, sb2c[:, 0:8])
        wait('v', 'S', sv_sb2)
        emit('v', lambda eng: eng.tensor_add(bfc[:], b2c[:, 0:8], b2c[:, 8:16]))
        emit('v', lambda eng: eng.drain())
        emit('v', lambda eng: eng.tensor_add(bfc[:], bfc[:], sb2c[:, 0:8]))
        vv_bfc = emit('v', lambda eng: eng.drain(), inc=('V', 1))

        wait('g', 'T', cst_prev_T[0])
        gv = emit('g', lambda eng: eng.dma_start(cst[0:2, 0:1024], gw_h), inc=('G', 16))
        wait('t', 'G', gv)
        for c in range(Dc):
            bk = take_banks(1)[0]
            tv = emit('t', lambda eng, o=pb[bk][0:128, 0:2], i_=cst[0:2, c * 128:(c + 1) * 128]:
                      eng.transpose(o, i_, ident[0:2, 0:2]), inc=('T', 1))
            wait('s', 'T', tv)
            emit('s', lambda eng, d=gwd[:, c:c + 1], s=pb[bk][0:128, 0:1]: eng.copy(d, s))
            sv = emit('s', lambda eng: eng.drain(), inc=('S', 1))
            wait('v', 'S', sv)
            emit('v', lambda eng, d=gwd[:, c:c + 1], b=pb[bk][0:128, 1:2]:
                 eng.tensor_sub(d, d, b))
            vv = emit('v', lambda eng: eng.drain(), inc=('V', 1))
            release_bank(bk, ('V', vv))
        bk = take_banks(1)[0]
        wait('t', 'V', vv)
        for c in range(Dc):
            tv = emit('t', lambda eng, o=pb[bk][0:1, 0:1], l=ones_colf[:, 0:1], r=gwd[:, c:c + 1],
                      st=(c == 0), sp=(c == Dc - 1): eng.matmul(o, l, r, start=st, stop=sp),
                      inc=('T', 1) if c == Dc - 1 else None)
        wait('s', 'T', tv)
        sv = emit('s', lambda eng, s=pb[bk][0:1, 0:1]: eng.copy(swd[:], s), inc=('S', 1))
        release_bank(bk, ('S', sv))

        if DBG:
            emit('v', lambda eng: eng.tensor_add(bfc2[:], b2c[:, 0:8], b2c[:, 8:16]))
            emit('v', lambda eng: eng.tensor_add(bfc2[:], bfc2[:], sb2c[:, 0:8]), inc=('V', 1))
            wait('g', 'V', N['V'])
            wait('g', 'S', N['S'])
            emit('g', lambda eng: eng.dma_start(dbgH[:, 0:8], bfc[:, 0:8]), inc=('G', 16))
            emit('g', lambda eng: eng.dma_start(dbgH[:, 32:40], bfc2[:, 0:8]), inc=('G', 16))
            emit('g', lambda eng: eng.dma_start(dbgH[:, 8:24], b2c[:, 0:16]), inc=('G', 16))
            emit('g', lambda eng: eng.dma_start(dbgH[:, 24:32], sb2c[:, 0:8]), inc=('G', 16))

        nat_tok = {}

        nat_last = [0]

        def nat_cast(dst_blob, src_ap, nrows, rowlen, key):
            wait('g', 'G', nat_last[0])
            toks = []
            for r in range(nrows // 128):
                slot = r % 2
                stg = pack[:, 1024 + slot * 2048: 1024 + slot * 2048 + rowlen]
                if len(toks) >= 2:
                    wait('g', 'G', toks[-2])
                gv = emit('g', lambda eng, d=stg, s=src_ap[r * 128:(r + 1) * 128, :]:
                          eng.dma_start(d, s), inc=('G', 16))
                wait('g', 'G', gv)
                gv2 = emit('g', lambda eng, d=dst_blob[r * 128:(r + 1) * 128, :], s=stg:
                           eng.dma_start(d, s), inc=('G', 16))
                toks.append(gv2)
            nat_tok[key] = toks
            nat_last[0] = toks[-1]

        blob_tok = {}
        pk_state = {"u": 0, "g": 0, "v": 0, "slot": 0, "hist": []}

        def pack_piece(dst_ap, src_ap, mode, fold_col=None):
            slot = pk_state["slot"]
            pk_state["slot"] ^= 1
            stg = pack[:, slot * 512:(slot + 1) * 512]
            hist = pk_state["hist"]
            prev = None
            for (sl, tok) in reversed(hist):
                if sl == slot:
                    prev = tok
                    break
            if mode == 'T':
                if prev is not None:
                    wait('y', 'G', prev)
                uv = emit('y', lambda eng, d=stg, s=src_ap: eng.dma_start(d, s, transpose=True),
                          inc=('W', 16))
                fill_tok = ('W', uv)
            else:
                if prev is not None:
                    wait('g', 'G', prev)
                gv = emit('g', lambda eng, d=stg, s=src_ap: eng.dma_start(d, s), inc=('G', 16))
                fill_tok = ('G', gv)
            if fold_col is not None:
                wait('v', 'S', sv_wnc)
                wait_tok('v', fill_tok)
                emit('v', lambda eng, d=stg, col=fold_col:
                     eng.tensor_scalar(d, d, col, None, OP.mult))
                vv = emit('v', lambda eng: eng.drain(), inc=('V', 1))
                fill_tok = ('V', vv)
            wait_tok('g', fill_tok)
            gv2 = emit('g', lambda eng, d=dst_ap, s=stg: eng.dma_start(d, s), inc=('G', 16))
            hist.append((slot, gv2))
            if len(hist) > 8:
                del hist[:4]
            return gv2

        def pack_attn(blob, nat, name, G, K, fold=False):
            outw = G * 512
            for g in range(G):
                last = None
                for k in range(K):
                    sl0 = (g * 512) // 128
                    for sl in range(sl0, sl0 + 4):
                        wait('y', 'G', nat_tok[name][sl])
                    src = nat[g * 512:(g + 1) * 512, k * 128:(k + 1) * 128]
                    dst = blob[g * 128:(g + 1) * 128, k * 512:(k + 1) * 512]
                    last = pack_piece(dst, src, 'T', fold_col=wnc[:, k:k + 1] if fold else None)
                blob_tok[(name, g)] = ('G', last)

        pbB = [p.bitcast(BF16) for p in pb]
        asm = pack[:, 3072:7168]
        a1_slot_T = [0, 0]
        asm_store = [0]
        for g in range(8):
            ev_toks = []
            for sl in range(4):
                r = g * 4 + sl
                slot = r % 2
                stg = pack[:, 1024 + slot * 1024: 1024 + slot * 1024 + 1024]
                wait('g', 'T', a1_slot_T[slot])
                gv = emit('g', lambda eng, d=stg, s=win_h[r * 128:(r + 1) * 128, :]:
                          eng.dma_start(d, s), inc=('G', 16))
                wait('t', 'G', gv)
                wait('t', 'P', pP0)
                for kh in range(2):
                    bk = take_banks(1)[0]
                    for kq in range(4):
                        k = kh * 4 + kq
                        tv = emit('t', lambda eng, o=pbB[bk][:, kq * 128:(kq + 1) * 128],
                                  i_=stg[:, k * 128:(k + 1) * 128]:
                                  eng.transpose(o, i_, identb[:]),
                                  inc=('T', 1) if kq == 3 else None)
                    a1_slot_T[slot] = tv
                    e = 's' if (r + kh) % 2 else 'v'
                    wait(e, 'T', tv)
                    wait(e, 'S', sv_wnc)
                    wait(e, 'G', asm_store[0])
                    for kq in range(4):
                        k = kh * 4 + kq
                        dst = asm[:, k * 512 + sl * 128: k * 512 + sl * 128 + 128]
                        srcp = pbB[bk][:, kq * 128:(kq + 1) * 128]
                        if e == 's':
                            emit('s', lambda eng, d=dst, s=srcp, sc=wnc[:, k:k + 1]:
                                 eng.activation(d, s, AF.Copy, bias=0.0, scale=sc))
                        else:
                            emit('v', lambda eng, d=dst, s=srcp, sc=wnc[:, k:k + 1]:
                                 eng.tensor_scalar(d, s, sc, None, OP.mult))
                    ev = emit(e, lambda eng: eng.drain(), inc=('S' if e == 's' else 'V', 1))
                    release_bank(bk, ('S' if e == 's' else 'V', ev))
                    ev_toks.append(('S' if e == 's' else 'V', ev))
            for tok in ev_toks:
                wait_tok('g', tok)
            gsv = emit('g', lambda eng, d=blobA1[g * 128:(g + 1) * 128, :], s=asm:
                       eng.dma_start(d, s), inc=('G', 16))
            asm_store[0] = gsv
            blob_tok[("in", g)] = ('G', gsv)
            nat_last[0] = gsv

        def APACK2():
            nat_cast(nat_dt, wdt_h, INNER, INNER, "dt")
            pack_attn(blobA2, nat_dt, "dt", 4, 16)

        def APACK3():
            nat_cast(nat_out, wout_h, D, INNER, "out")
            pack_attn(blobA3, nat_out, "out", 2, 16)

        FPACK = []

        def defer_fpack_w1like(blob, src_f32, name):
            for g in range(8):
                def mk(g=g, blob=blob, src=src_f32, name=name):
                    last = None
                    for k in range(8):
                        s = src[k * 128:(k + 1) * 128, g * 512:(g + 1) * 512]
                        d = blob[g * 128:(g + 1) * 128, k * 512:(k + 1) * 512]
                        last = pack_piece(d, s, 'C')
                    blob_tok[(name, g)] = ('G', last)
                FPACK.append(mk)

        def defer_fpack_w2like(blob, src_f32, name):
            for hg in range(2):
                for cg in range(2):
                    gi = hg * 2 + cg
                    def mk(hg=hg, cg=cg, gi=gi, blob=blob, src=src_f32, name=name):
                        last = None
                        for kl in range(16):
                            k = hg * 16 + kl
                            s = src[k * 128:(k + 1) * 128, cg * 512:(cg + 1) * 512]
                            d = blob[gi * 128:(gi + 1) * 128, kl * 512:(kl + 1) * 512]
                            last = pack_piece(d, s, 'C')
                        blob_tok[(name, gi)] = ('G', last)
                    FPACK.append(mk)

        defer_fpack_w1like(blobW1[0], w1_h[0:D, :], "w1_0")
        defer_fpack_w1like(blobW1[1], w1_h[D:2 * D, :], "w1_1")
        defer_fpack_w1like(blobSW1, sw1_h, "sw1")
        defer_fpack_w2like(blobW2[0], w2_h[0:HID, :], "w2_0")
        defer_fpack_w2like(blobW2[1], w2_h[HID:2 * HID, :], "w2_1")
        defer_fpack_w2like(blobSW2, sw2_h, "sw2")

        tok_tpose_done = {}
        tok_x1evac = {}
        tok_x1store = {}
        tok_ln1ops = {}
        tok_wdt_mm = {}
        tok_d1 = {}
        tok_pmul = {}
        tok_win_mm = {}
        tok_wout_mm = {}
        tok_uevac = {}
        tok_statT = {}

        def emit_lnA(j):
            b, kt = divmod(j, NKT)
            pa = j % 2
            wait_tok('t', xload_tok[j])
            if j >= 2:
                wait_tok('t', tok_x1evac[j - 2])
            for sub in range(4):
                bks = take_banks(2)
                for c in range(Dc):
                    bk = bks[c // 4]
                    q = c % 4
                    tv = emit('t', lambda eng, o=pb[bk][:, q * 128:(q + 1) * 128],
                              i_=stage_x(sub, c * 128, 128): eng.transpose(o, i_, ident[:]),
                              inc=('T', 1) if c % 4 == 3 else None)
                    if c == 3:
                        tv_a = tv
                wait('s', 'T', tv_a)
                if j >= 2:
                    wait_tok('s', tok_x1evac[j - 2])
                for c in range(4):
                    emit('s', lambda eng, d=xr(pa, c, sub * 128, 128),
                         s=pb[bks[0]][:, (c % 4) * 128:(c % 4 + 1) * 128]: eng.copy(d, s))
                wait('s', 'T', tv)
                for c in range(4, 8):
                    emit('s', lambda eng, d=xr(pa, c, sub * 128, 128),
                         s=pb[bks[1]][:, (c % 4) * 128:(c % 4 + 1) * 128]: eng.copy(d, s))
                sv = emit('s', lambda eng: eng.drain(), inc=('S', 1))
                release_bank(bks[0], ('S', sv))
                release_bank(bks[1], ('S', sv))
            tok_tpose_done[j] = ('T', tv)

            if j >= 1:
                wait_tok('s', tok_statT[j - 1])
            if j >= 4:
                wait('s', 'G', nat_tok["out"][-1])
                dump = pack[:, 1024:2048]
            else:
                if j >= 1:
                    wait_tok('s', tok_win[j - 1])
                dump = AR3h[:, 10240:11264]
            for sub in range(4):
                wait('s', 'G', xload_sub[j][sub])
                emit('s', lambda eng, s=stage_x(sub), a=statq[:, 4 + sub:5 + sub], d=dump:
                     eng.activation(d, s, AF.Square, accum_out=a))
                emit('s', lambda eng, s=stage_x(sub), a=statq[:, sub:sub + 1], d=dump:
                     eng.activation(d, s, AF.Copy, bias=0.0, accum_out=a))
            stat_sv = emit('s', lambda eng: eng.drain(), inc=('S', 1))
            if j + 1 < NTA:
                wait_tok('g', tok_tpose_done[j])
                wait('g', 'S', stat_sv)
                emit_xload(j + 1)
            lnstate[j] = {"stat_sv": stat_sv}

        def emit_lnB(j):
            stat_sv = lnstate[j]["stat_sv"]
            bsx = take_banks(1)[0]
            bsx2 = take_banks(1)[0]
            wait('t', 'S', stat_sv)
            for sub in range(4):
                emit('t', lambda eng, o=pb[bsx][0:1, sub * 128:(sub + 1) * 128],
                     i_=statq[:, sub:sub + 1]: eng.transpose(o, i_, ident[:]))
                tv2 = emit('t', lambda eng, o=pb[bsx2][0:1, sub * 128:(sub + 1) * 128],
                           i_=statq[:, 4 + sub:5 + sub]: eng.transpose(o, i_, ident[:]),
                           inc=('T', 1) if sub == 3 else None)
            tok_statT[j] = ('T', tv2)
            wait('s', 'T', tv2)
            emit('s', lambda eng, s=pb[bsx][0:1, :]: eng.mul(rowA("m"), s, 1.0 / D))
            emit('s', lambda eng, s=pb[bsx2][0:1, :]: eng.mul(rowA("sx2"), s, 1.0 / D))
            sv = emit('s', lambda eng: eng.drain(), inc=('S', 1))
            release_bank(bsx, ('S', sv))
            release_bank(bsx2, ('S', sv))
            wait('v', 'S', sv)
            emit('v', lambda eng: eng.tensor_mul(rowA("sx"), rowA("m"), rowA("m")))
            emit('v', lambda eng: eng.drain())
            emit('v', lambda eng: eng.tensor_sub(rowA("sx2"), rowA("sx2"), rowA("sx")))
            vv = emit('v', lambda eng: eng.drain(), inc=('V', 1))
            wait('s', 'V', vv)
            emit('s', lambda eng: eng.activation(rowA("rstd"), rowA("sx2"), AF.Ln, bias=EPS, scale=1.0))
            emit('s', lambda eng: eng.activation(rowA("sg"), rowA("sx2"), AF.Ln, bias=EPS * EPS,
                                                 scale=1.0 + EPS))
            emit('s', lambda eng: eng.activation(rowA("rstd"), rowA("rstd"), AF.Exp, bias=0.0, scale=-0.5))
            emit('s', lambda eng: eng.activation(rowA("sg"), rowA("sg"), AF.Exp, bias=0.0, scale=-0.5))
            sv = emit('s', lambda eng: eng.drain(), inc=('S', 1))
            wait('v', 'S', sv)
            emit('v', lambda eng: eng.tensor_mul(rowA("p4"), rowA("m"), rowA("sg")))
            emit('v', lambda eng: eng.tensor_mul(rowA("p2"), rowA("m"), rowA("rstd")))
            emit('v', lambda eng: eng.tensor_scalar(rowA("rstd"), rowA("rstd"), 1.0, None, OP.add))
            vv = emit('v', lambda eng: eng.drain(), inc=('V', 1))
            lnstate[j]["rows_vv"] = vv

        def emit_lnC(j):
            pa = j % 2
            vv = lnstate[j]["rows_vv"]
            wait('t', 'V', vv)
            bks = take_banks(4)
            srcs = [rowA("rstd"), rowA("p2"), rowA("sg"), rowA("p4")]
            for bi, (bk, src) in enumerate(zip(bks, srcs)):
                tv = emit('t', lambda eng, o=pb[bk][:, 0:512], l=ones_row[0:1, :], r=src:
                          eng.matmul(o, l, r, start=True, stop=True),
                          inc=('T', 1) if bi == 3 else None)
            wait('s', 'T', tv)
            if j >= 1:
                wait_tok('s', tok_ln1ops[j - 1])
            for bi, bk in enumerate(bks):
                emit('s', lambda eng, d=bcB(bi), s=pb[bk][:, 0:512]: eng.copy(d, s))
            sv = emit('s', lambda eng: eng.drain(), inc=('S', 1))
            for bk in bks:
                release_bank(bk, ('S', sv))
            wait('v', 'S', sv)
            if j >= 1:
                wait_tok('v', tok_win[j - 1])
            for c0 in range(0, Dc, 2):
                for c in (c0, c0 + 1):
                    emit('v', lambda eng, a=xr(pa, c), c=c: eng.tensor_mul(gt(c), a, bcB(2)))
                for c in (c0, c0 + 1):
                    emit('v', lambda eng, c=c: eng.tensor_sub(gt(c), gt(c), bcB(3)))
                for c in (c0, c0 + 1):
                    emit('v', lambda eng, a=xr(pa, c): eng.tensor_mul(a, a, bcB(0)))
                for c in (c0, c0 + 1):
                    emit('v', lambda eng, a=xr(pa, c): eng.tensor_sub(a, a, bcB(1)))
            vv = emit('v', lambda eng: eng.drain(), inc=('V', 1))
            tok_ln1ops[j] = ('V', vv)

        def emit_win(i):
            g_ready = tok_ln1ops[i]
            u_done = None
            for g in range(8):
                col0, wv, sid = ring_load(blobA1[g * 128:(g + 1) * 128, :], 4096,
                                          ready_tok=blob_tok.get(("in", g)))
                bks = take_banks(4)
                wait('t', 'W', wv)
                wait_tok('t', g_ready)
                for c4 in range(4):
                    for k in range(8):
                        tv = emit('t', lambda eng, o=pb[bks[c4]][:, 0:512],
                                  l=ring[:, col0 + k * 512 + c4 * 128: col0 + k * 512 + c4 * 128 + 128],
                                  r=gt(k), st=(k == 0), sp=(k == 7):
                                  eng.matmul(o, l, r, start=st, stop=sp),
                                  inc=('T', 1) if (c4 == 3 and k == 7) else None)
                ring_free(sid, ('T', tv))
                wait('s', 'T', tv)
                if i >= 1 and g == 0:
                    wait_tok('s', tok_wdt_mm[i - 1])
                    wait_tok('s', tok_d1[i - 1])
                    wait_tok('s', tok_pmul[i - 1])
                for c4 in range(4):
                    oc = g * 4 + c4
                    dst = ut(oc) if oc < Ic else zt(oc - Ic)
                    emit('s', lambda eng, d=dst, s=pb[bks[c4]][:, 0:512]:
                         eng.activation(d, s, AF.Silu))
                sv = emit('s', lambda eng: eng.drain(), inc=('S', 1))
                for bk in bks:
                    release_bank(bk, ('S', sv))
                if g == 3:
                    u_done = ('S', sv)
            tok_win[i] = ('T', tv)
            tok_uevac[i] = u_done
            return u_done, ('S', sv)

        def emit_wdt(i, u_done, z_done, qs):
            b, kt = divmod(i, NKT)
            tok0 = b * S + kt * TA
            pa = i % 2
            if i == 0 and qs[0] == 0:
                APACK2()

            last_scan_v = None
            for q in qs:
                colof, wv, sids = ring_load_multi(blobA2[q * 128:(q + 1) * 128, :], 8192,
                                                  ready_tok=blob_tok.get(("dt", q)))
                bks = take_banks(4)
                wait('t', 'W', wv)
                wait_tok('t', u_done)
                for c4 in range(4):
                    for k in range(Ic):
                        tv = emit('t', lambda eng, o=pb[bks[c4]][:, 0:512],
                                  l=ring[:, colof(k) + c4 * 128: colof(k) + c4 * 128 + 128],
                                  r=ut(k), st=(k == 0), sp=(k == Ic - 1):
                                  eng.matmul(o, l, r, start=st, stop=sp),
                                  inc=('T', 1) if (c4 == 3 and k == Ic - 1) else None)
                for sid in sids:
                    ring_free(sid, ('T', tv))
                tok_wdt_mm[i] = ('T', tv)
                wait('s', 'T', tv)
                for c4 in range(4):
                    j = q * 4 + c4
                    emit('s', lambda eng, d=lam_g(c4), s=pb[bks[c4]][:, 0:512], bc=bdtc[:, j:j + 1]:
                         eng.activation(d, s, AF.Sigmoid, bias=bc, scale=1.0))
                sv = emit('s', lambda eng: eng.drain(), inc=('S', 1))
                for bk in bks:
                    release_bank(bk, ('S', sv))
                wait('v', 'S', sv)
                wait_tok('v', z_done)
                if i >= 1 and q == 0:
                    wait_tok('v', tok_x1store[i - 1])
                for c4 in range(4):
                    j = q * 4 + c4
                    emit('v', lambda eng, c4=c4, j=j:
                         eng.scalar_tensor_tensor(d1_g(c4), lam_g(c4), 1.0, ut(j), OP.subtract, OP.mult))
                for c4 in range(4):
                    j = q * 4 + c4
                    if kt == 0:
                        emit('v', lambda eng, c4=c4:
                             eng.tensor_tensor_scan(s_g(c4), lam_g(c4), d1_g(c4), 0.0, OP.mult, OP.subtract))
                    else:
                        emit('v', lambda eng, c4=c4, j=j:
                             eng.tensor_tensor_scan(s_g(c4), lam_g(c4), d1_g(c4), stst[:, j:j + 1],
                                                    OP.mult, OP.subtract))
                for c4 in range(4):
                    j = q * 4 + c4
                    emit('v', lambda eng, c4=c4, j=j:
                         eng.tensor_copy(stst[:, j:j + 1], s_g(c4, 511, 1)))
                for c4 in range(4):
                    j = q * 4 + c4
                    emit('v', lambda eng, c4=c4, j=j:
                         eng.tensor_mul(pt(j), s_g(c4), zt(j)))
                last_scan_v = emit('v', lambda eng: eng.drain(), inc=('V', 1))
            tok_d1[i] = ('V', last_scan_v)
            tok_pmul[i] = ('V', last_scan_v)
            return ('V', last_scan_v)

        def emit_wout(i, p_done):
            b, kt = divmod(i, NKT)
            tok0 = b * S + kt * TA
            pa = i % 2
            if i == 0:
                APACK3()

            st_tok = {}
            for g in range(2):
                colof, wv, sids = ring_load_multi(blobA3[g * 128:(g + 1) * 128, :], 8192,
                                                  ready_tok=blob_tok.get(("out", g)))
                bks = take_banks(4)
                wait('t', 'W', wv)
                wait_tok('t', p_done)
                for c4 in range(4):
                    for k in range(Ic):
                        tv = emit('t', lambda eng, o=pb[bks[c4]][:, 0:512],
                                  l=ring[:, colof(k) + c4 * 128: colof(k) + c4 * 128 + 128],
                                  r=pt(k), st=(k == 0), sp=(k == Ic - 1):
                                  eng.matmul(o, l, r, start=st, stop=sp),
                                  inc=('T', 1) if (c4 == 3 and k == Ic - 1) else None)
                for sid in sids:
                    ring_free(sid, ('T', tv))
                tok_wout_mm[i] = ('T', tv)
                wait('v', 'T', tv)
                for c4 in range(4):
                    c = g * 4 + c4
                    slot = c % 2
                    if c >= 2:
                        wait('v', 'G', st_tok[c - 2])
                    emit('v', lambda eng, slot=slot, s=pb[bks[c4]][:, 0:512], r=xr(pa, c):
                         eng.tensor_add(x1_st(slot), s, r))
                    vv = emit('v', lambda eng: eng.drain(), inc=('V', 1))
                    wait('g', 'V', vv)
                    gv = emit('g', lambda eng, slot=slot, c=c,
                              d=x1blob[c * 128:(c + 1) * 128, tok0:tok0 + TA]:
                              eng.dma_start(d, x1_st(slot)), inc=('G', 16))
                    st_tok[c] = gv
                sv_rel = ('V', vv)
                for bk in bks:
                    release_bank(bk, sv_rel)
            tok_x1evac[i] = ('V', vv)
            tok_x1store[i] = ('G', st_tok[7])

            if DBG and i == 0:
                wait('g', 'V', N['V'])
                wait('g', 'S', N['S'])
                emit('g', lambda eng: eng.dma_start(dbgA, AR2h[:, :]), inc=('G', 16))
                dbg_gv = emit('g', lambda eng: eng.dma_start(dbgB, AR3h[:, :]), inc=('G', 16))
                for e in 'tvs':
                    wait(e, 'G', dbg_gv)

            nslice = (len(FPACK) + NTA - 1 - i) // (NTA - i) if i < NTA - 1 else len(FPACK)
            for _ in range(min(nslice, len(FPACK))):
                FPACK.pop(0)()

        tok_win = {}
        lnstate = {}
        emit_lnA(0)
        emit_lnB(0)
        emit_lnC(0)
        for i in range(NTA):
            u_done, z_done = emit_win(i)
            nxt = i + 1
            if nxt < NTA and nxt < 4:
                emit_lnA(nxt)
                emit_lnB(nxt)
                emit_lnC(nxt)
                emit_wdt(i, u_done, z_done, (0, 1))
                p_done = emit_wdt(i, u_done, z_done, (2, 3))
            elif nxt < NTA:
                emit_lnA(nxt)
                emit_wdt(i, u_done, z_done, (0, 1))
                emit_lnB(nxt)
                p_done = emit_wdt(i, u_done, z_done, (2, 3))
                emit_lnC(nxt)
            else:
                emit_wdt(i, u_done, z_done, (0, 1))
                p_done = emit_wdt(i, u_done, z_done, (2, 3))
            emit_wout(i, p_done)

        A_last_x1store = tok_x1store[NTA - 1]
        A_last_x1evac = tok_x1evac[NTA - 1]
        A_ln1 = tok_ln1ops[NTA - 1]
        A_pmul = tok_pmul[NTA - 1]
        A_wout = tok_wout_mm[NTA - 1]
        A_wdt = tok_wdt_mm[NTA - 1]
        A_d1 = tok_d1[NTA - 1]
        A_uevac = ('S', N['S'])
        A_tpose = tok_tpose_done[NTA - 1]

        tok_f_stt = {}
        tok_f_outst = {}
        tok_f_outcp = {}
        tok_f_m0rd = {}
        tok_f_g2rd = {}
        tok_f_hrd = {}
        tok_f_acc = {}

        for i in range(NTF):
            tok0 = i * TF

            if i == 0:
                wait_tok('g', A_last_x1evac)
                wait_tok('g', A_last_x1store)
            else:
                wait_tok('g', tok_f_stt[i - 1])
                wait_tok('g', tok_f_outst[i - 1])
            xg = {}
            for c in range(Dc):
                gv = emit('g', lambda eng, d=x1f(c), s=x1blob[c * 128:(c + 1) * 128, tok0:tok0 + TF]:
                          eng.dma_start(d, s), inc=('G', 16))
                xg[c] = gv
            x1_tok = ('G', gv)

            bsx = take_banks(2)
            bsx2 = take_banks(2)
            bgd = take_banks(2)
            wait('t', 'G', gv)
            if i == 0:
                wait_tok('s', A_last_x1store)
                wait_tok('s', A_pmul)
            else:
                wait_tok('s', tok_f_outcp[i - 1])
            sq_tok = {}
            bslot = lambda sl: AR4b[:, sl * 512:(sl + 1) * 512]
            for half in range(2):
                for c in range(Dc):
                    slot = (half * Dc + c) % 2
                    key = (half, c)
                    if (half * Dc + c) >= 2:
                        pk = (half, c - 2) if c >= 2 else (0, Dc - 2 + c)
                        wait('s', 'T', sq_tok[pk])
                    if i >= 1:
                        wait_tok('s', tok_f_acc[i - 1])
                    emit('s', lambda eng, d=bslot(slot * 2 + 1), s=x1f(c, half * 512, 512):
                         eng.activation(d, s, AF.Square))
                    ssv = emit('s', lambda eng: eng.drain(), inc=('S', 1))
                    tv = emit('t', lambda eng, o=pb[bsx[half]][0:1, :], l=ones_colf[:, 0:1],
                              r=x1f(c, half * 512, 512), st=(c == 0), sp=(c == Dc - 1):
                              eng.matmul(o, l, r, start=st, stop=sp),
                              inc=('T', 1) if c == Dc - 1 else None)
                    wait('t', 'S', ssv)
                    tv2 = emit('t', lambda eng, o=pb[bsx2[half]][0:1, :], l=ones_colb[:, 0:1],
                               r=bslot(slot * 2 + 1), st=(c == 0), sp=(c == Dc - 1):
                               eng.matmul(o, l, r, start=st, stop=sp), inc=('T', 1))
                    sq_tok[key] = tv2
                for c in range(Dc):
                    tg = emit('t', lambda eng, o=pb[bgd[half]][0:1, :], l=gwd[:, c:c + 1],
                              r=x1f(c, half * 512, 512), st=(c == 0), sp=(c == Dc - 1):
                              eng.matmul(o, l, r, start=st, stop=sp),
                              inc=('T', 1) if c == Dc - 1 else None)
            wait('s', 'T', tg)
            for half in range(2):
                emit('s', lambda eng, h=half, s=pb[bsx[half]][0:1, :]:
                     eng.mul(rowF("m2")[:, h * 512:(h + 1) * 512], s, 1.0 / D))
                emit('s', lambda eng, h=half, s=pb[bsx2[half]][0:1, :]:
                     eng.mul(rowF("sx2")[:, h * 512:(h + 1) * 512], s, 1.0 / D))
                emit('s', lambda eng, h=half, s=pb[bgd[half]][0:1, :]:
                     eng.copy(rowF("gd")[:, h * 512:(h + 1) * 512], s))
            sv = emit('s', lambda eng: eng.drain(), inc=('S', 1))
            for bk in bsx + bsx2 + bgd:
                release_bank(bk, ('S', sv))
            wait('v', 'S', sv)
            emit('v', lambda eng: eng.tensor_mul(rowF("sx"), rowF("m2"), rowF("m2")))
            emit('v', lambda eng: eng.drain())
            emit('v', lambda eng: eng.tensor_sub(rowF("sx2"), rowF("sx2"), rowF("sx")))
            vv = emit('v', lambda eng: eng.drain(), inc=('V', 1))
            wait('s', 'V', vv)
            emit('s', lambda eng: eng.activation(rowF("rstd2"), rowF("sx2"), AF.Ln, bias=EPS, scale=1.0))
            emit('s', lambda eng: eng.drain())
            emit('s', lambda eng: eng.activation(rowF("rstd2"), rowF("rstd2"), AF.Exp, bias=0.0,
                                                 scale=-0.5))
            sv = emit('s', lambda eng: eng.drain(), inc=('S', 1))
            wait('v', 'S', sv)
            emit('v', lambda eng: eng.tensor_scalar(rowF("sx"), rowF("m2"), swd[0:1, 0:1], None, OP.mult))
            emit('v', lambda eng: eng.tensor_mul(rowF("m2r2"), rowF("m2"), rowF("rstd2")))
            emit('v', lambda eng: eng.tensor_sub(rowF("gd"), rowF("gd"), rowF("sx")))
            emit('v', lambda eng: eng.drain())
            emit('v', lambda eng: eng.tensor_scalar(rowF("gd"), rowF("gd"), 0.0, None, OP.is_ge))
            vv = emit('v', lambda eng: eng.drain(), inc=('V', 1))
            wait('t', 'V', vv)
            bks = take_banks(6)
            for half in range(2):
                for bi, src in enumerate([rowF("rstd2"), rowF("m2r2"), rowF("gd")]):
                    bk = bks[bi * 2 + half]
                    tv = emit('t', lambda eng, o=pb[bk][:, 0:512], l=ones_row[0:1, :],
                              r=src[:, half * 512:(half + 1) * 512]:
                              eng.matmul(o, l, r, start=True, stop=True),
                              inc=('T', 1) if (half == 1 and bi == 2) else None)
            wait('s', 'T', tv)
            if i >= 1:
                wait_tok('s', tok_f_acc[i - 1])
            for half in range(2):
                for bi, dstf in enumerate([rstd2B, m2r2B, mask0B]):
                    bk = bks[bi * 2 + half]
                    emit('s', lambda eng, d=dstf(half * 512, 512), s=pb[bk][:, 0:512]:
                         eng.copy(d, s))
            sv = emit('s', lambda eng: eng.drain(), inc=('S', 1))
            for bk in bks:
                release_bank(bk, ('S', sv))
            wait('v', 'S', sv)
            if i == 0:
                wait_tok('v', A_wdt)
                wait_tok('v', A_d1)
                wait_tok('v', A_uevac)
                wait_tok('v', A_pmul)
            else:
                wait_tok('v', tok_f_g2rd[i - 1])
                wait_tok('v', tok_f_m0rd[i - 1])
            scrL = [AR4[:, 3584:4608], AR4[:, 4608:5632]]
            for c0 in range(0, Dc, 2):
                for c in (c0, c0 + 1):
                    emit('v', lambda eng, c=c, s=scrL[c % 2]: eng.tensor_mul(s, x1f(c), rstd2B()))
                for c in (c0, c0 + 1):
                    emit('v', lambda eng, c=c, s=scrL[c % 2]: eng.tensor_sub(g2(c), s, m2r2B()))
                for c in (c0, c0 + 1):
                    emit('v', lambda eng, c=c: eng.tensor_mul(m0(c), g2(c), mask0B()))
            vv = emit('v', lambda eng: eng.drain(), inc=('V', 1))
            m_ready = ('V', vv)

            if DBG and i == 0:
                wait('g', 'V', N['V'])
                dbg_gv = emit('g', lambda eng: eng.dma_start(dbgC, AR2h[:, :]), inc=('G', 16))
                for e in 'tvs':
                    wait(e, 'G', dbg_gv)

            def h_pass(blob_name, blob, rhs_fn, hg, pass_idx):
                relu_last = None
                if blob_name == "w1_0":
                    bias_col = lambda j: b1c[:, j:j + 1]
                elif blob_name == "w1_1":
                    bias_col = lambda j: b1c[:, 32 + j:33 + j]
                else:
                    bias_col = lambda j: sb1c[:, j:j + 1]
                for gl in range(4):
                    g = hg * 4 + gl
                    col0, wv, sid = ring_load(blob[g * 128:(g + 1) * 128, :], 4096,
                                              ready_tok=blob_tok.get((blob_name, g)))
                    for sub in range(2):
                        bks = take_banks(4)
                        wait('t', 'W', wv)
                        wait_tok('t', m_ready)
                        if pass_idx == 0 and hg == 0 and gl == 0 and sub == 0:
                            if i == 0:
                                wait_tok('t', A_wout)
                                wait_tok('s', A_d1)
                            else:
                                wait_tok('t', tok_f_hrd[i - 1])
                        for c2 in range(2):
                            for half in range(2):
                                bk = bks[c2 * 2 + half]
                                for k in range(Dc):
                                    tv = emit('t', lambda eng, o=pb[bk][:, 0:512],
                                              l=ring[:, col0 + k * 512 + (sub * 2 + c2) * 128:
                                                     col0 + k * 512 + (sub * 2 + c2) * 128 + 128],
                                              r=rhs_fn(k, half), st=(k == 0), sp=(k == Dc - 1):
                                              eng.matmul(o, l, r, start=st, stop=sp),
                                              inc=('T', 1) if (c2 == 1 and half == 1 and k == Dc - 1) else None)
                        if sub == 1:
                            ring_free(sid, ('T', tv))
                        wait('s', 'T', tv)
                        for c2 in range(2):
                            j = g * 4 + sub * 2 + c2
                            jj = j - hg * 16
                            for half in range(2):
                                bk = bks[c2 * 2 + half]
                                emit('s', lambda eng, d=ht(jj, half * 512, 512), s=pb[bk][:, 0:512],
                                     bc=bias_col(j):
                                     eng.activation(d, s, AF.Relu, bias=bc, scale=1.0))
                        sv = emit('s', lambda eng: eng.drain(), inc=('S', 1))
                        for bk in bks:
                            release_bank(bk, ('S', sv))
                        relu_last = sv
                return ('S', relu_last)

            def w2_pass(blob_name, blob, hg, first):
                h_ready = None
                for cg in range(2):
                    gi = hg * 2 + cg
                    colof, wv, sids = ring_load_multi(blob[gi * 128:(gi + 1) * 128, :], 8192,
                                                      ready_tok=blob_tok.get((blob_name, gi)))
                    for sub in range(2):
                        bks = take_banks(4)
                        wait('t', 'W', wv)
                        for c2 in range(2):
                            for half in range(2):
                                bk = bks[c2 * 2 + half]
                                for kl in range(16):
                                    tv = emit('t', lambda eng, o=pb[bk][:, 0:512],
                                              l=ring[:, colof(kl) + (sub * 2 + c2) * 128:
                                                     colof(kl) + (sub * 2 + c2) * 128 + 128],
                                              r=ht(kl, half * 512, 512), st=(kl == 0), sp=(kl == 15):
                                              eng.matmul(o, l, r, start=st, stop=sp),
                                              inc=('T', 1) if (c2 == 1 and half == 1 and kl == 15) else None)
                        if sub == 1:
                            for sid in sids:
                                ring_free(sid, ('T', tv))
                        wait('v', 'T', tv)
                        for c2 in range(2):
                            c = cg * 4 + sub * 2 + c2
                            for half in range(2):
                                bk = bks[c2 * 2 + half]
                                if first:
                                    vv = emit('v', lambda eng, c=c, half=half, bk=bk:
                                              eng.tensor_copy(out_acc(c, half * 512, 512),
                                                              pb[bk][:, 0:512]),
                                              inc=('V', 1) if (c2 == 1 and half == 1) else None)
                                else:
                                    vv = emit('v', lambda eng, c=c, half=half, bk=bk:
                                              eng.tensor_add(out_acc(c, half * 512, 512),
                                                             out_acc(c, half * 512, 512),
                                                             pb[bk][:, 0:512]),
                                              inc=('V', 1) if (c2 == 1 and half == 1) else None)
                        for bk in bks:
                            release_bank(bk, ('V', vv))
                return ('V', vv)

            first_acc = True
            acc_tok = None
            for (nm1, bl1, rhs1, nm2, bl2) in [("w1_0", blobW1[0], lambda k, h: m0(k, h * 512, 512), "w2_0", blobW2[0]),
                                               ("w1_1", blobW1[1], lambda k, h: m0(k, h * 512, 512), "w2_1", blobW2[1]),
                                               ("sw1", blobSW1, lambda k, h: g2(k, h * 512, 512), "sw2", blobSW2)]:
                if nm1 == "w1_1":
                    wait_tok('v', tok_f_m0rd[i])
                    for c in range(Dc):
                        emit('v', lambda eng, c=c: eng.tensor_sub(m0(c), g2(c), m0(c)))
                    mv = emit('v', lambda eng: eng.drain(), inc=('V', 1))
                    m_ready = ('V', mv)
                for hg in range(2):
                    hr = h_pass(nm1, bl1, rhs1, hg, 0 if first_acc else 1)
                    wait_tok('t', hr)
                    acc_tok = w2_pass(nm2, bl2, hg, first_acc and hg == 0)
                    if hg == 0 and first_acc:
                        first_acc = False
                    tok_f_hrd[i] = ('T', N['T'])
                if nm1 == "w1_0":
                    tok_f_m0rd[i] = ('T', N['T'])
                if nm1 == "w1_1":
                    tok_f_m0rd[i] = ('T', N['T'])
                if nm1 == "sw1":
                    tok_f_g2rd[i] = ('T', N['T'])

            if DBG and i == 0:
                wait_tok('g', acc_tok)
                emit('g', lambda eng: eng.dma_start(dbgD, AR4[:, :]), inc=('G', 16))
                emit('g', lambda eng: eng.dma_start(dbgF, AR1[:, :]), inc=('G', 16))
                dbg_gv = emit('g', lambda eng: eng.dma_start(dbgG, bfc[:, 0:8]), inc=('G', 16))
                for e in 'tvs':
                    wait(e, 'G', dbg_gv)

            wait_tok('v', acc_tok)
            for c in range(Dc):
                emit('v', lambda eng, c=c:
                     eng.scalar_tensor_tensor(out_acc(c), out_acc(c), bfc[:, c:c + 1], x1f(c),
                                              OP.add, OP.add))
            fv = emit('v', lambda eng: eng.drain(), inc=('V', 1))
            tok_f_stt[i] = ('V', fv)

            if DBG and i == 0:
                wait('g', 'V', N['V'])
                dbg_gv = emit('g', lambda eng: eng.dma_start(dbgE, AR4[:, :]), inc=('G', 16))
                for e in 'tvs':
                    wait(e, 'G', dbg_gv)

            out_last_g = None
            for sub in range(8):
                slot = sub % 2
                bks = take_banks(2)
                wait('t', 'V', fv)
                for c in range(Dc):
                    bk = bks[c // 4]
                    qq = c % 4
                    tv = emit('t', lambda eng, o=pb[bk][:, qq * 128:(qq + 1) * 128],
                              i_=out_acc(c, sub * 128, 128): eng.transpose(o, i_, ident[:]),
                              inc=('T', 1) if qq == 3 else None)
                    if c == 3:
                        tv_a = tv
                wait('s', 'T', tv)
                if i >= 1 and sub < 2:
                    wait_tok('s', tok_f_outst[i - 1])
                if sub >= 2:
                    wait('s', 'G', outst_tok[sub - 2])
                elif i == 0 and sub < 2:
                    wait_tok('s', A_last_x1evac)
                if sub == 0:
                    outst_tok = {}
                for c in range(Dc):
                    bk = bks[c // 4]
                    emit('s', lambda eng, d=out_tm(slot)[:, c * 128:(c + 1) * 128],
                         s=pb[bk][:, (c % 4) * 128:(c % 4 + 1) * 128]: eng.copy(d, s))
                sv = emit('s', lambda eng: eng.drain(), inc=('S', 1))
                for bk in bks:
                    release_bank(bk, ('S', sv))
                tok_f_outcp[i] = ('S', sv)
                wait('g', 'S', sv)
                gv = emit('g', lambda eng, d=out_h[tok0 + sub * 128: tok0 + (sub + 1) * 128, :],
                          s=out_tm(slot): eng.dma_start(d, s), inc=('G', 16))
                outst_tok[sub] = gv
                out_last_g = gv
            tok_f_outst[i] = ('G', out_last_g)
            tok_f_acc[i] = ('T', N['T'])

        while FPACK:
            FPACK.pop(0)()

        wait('g', 'G', N['G'])

        block = E(nc.Block())

        @block.tensor
        def _(eng):
            for f in OPS['t']:
                f(eng)

        @block.vector
        def _(eng):
            for f in OPS['v']:
                f(eng)

        @block.scalar
        def _(eng):
            for f in OPS['s']:
                f(eng)

        @block.gpsimd
        def _(eng):
            for f in OPS['g']:
                f(eng)

        @block.sync
        def _(eng):
            for f in OPS['y']:
                f(eng)

    return nc


_NC_CACHE = [None]


def kernel(**inputs):
    xs = np.ascontiguousarray(inputs["L0_attn_x"], dtype=np.float32)
    ins_common = {
        "w_norm": np.ascontiguousarray(inputs["L0_attn_w_norm"], np.float32).reshape(8, 128),
        "W_in": np.ascontiguousarray(inputs["L0_attn_W_in"], np.float32),
        "W_dt": np.ascontiguousarray(inputs["L0_attn_W_dt"], np.float32),
        "b_dt": np.ascontiguousarray(inputs["L0_attn_b_dt"], np.float32).reshape(16, 128),
        "W_out": np.ascontiguousarray(inputs["L0_attn_W_out"], np.float32),
        "gate_w": np.ascontiguousarray(inputs["L0_ffn_gate_w"], np.float32),
        "w1": np.ascontiguousarray(inputs["L0_ffn_w1"], np.float32).reshape(2 * D, HID),
        "b1": np.ascontiguousarray(inputs["L0_ffn_b1"], np.float32).reshape(64, 128),
        "w2": np.ascontiguousarray(inputs["L0_ffn_w2"], np.float32).reshape(2 * HID, D),
        "b2": np.ascontiguousarray(inputs["L0_ffn_b2"], np.float32).reshape(16, 128),
        "sw1": np.ascontiguousarray(inputs["L0_ffn_sw1"], np.float32),
        "sb1": np.ascontiguousarray(inputs["L0_ffn_sb1"], np.float32).reshape(32, 128),
        "sw2": np.ascontiguousarray(inputs["L0_ffn_sw2"], np.float32),
        "sb2": np.ascontiguousarray(inputs["L0_ffn_sb2"], np.float32).reshape(8, 128),
    }
    in_maps = []
    for core in range(NCORES):
        m = dict(ins_common)
        m["x"] = np.ascontiguousarray(xs[core * NB:(core + 1) * NB].reshape(NTOK, D))
        in_maps.append(m)
    if _NC_CACHE[0] is None:
        _NC_CACHE[0] = build_nc()
    nc = _NC_CACHE[0]
    res = run_bass_kernel_spmd(nc, in_maps, core_ids=list(range(NCORES)))
    out = np.concatenate([r["out"].reshape(NB, S, D) for r in res.results], axis=0)
    return out
